# version 01
# baseline (speedup 1.0000x reference)
import sys
sys.path.insert(0, "/opt/trn_rl_repo")
import contextlib
import numpy as np
import ml_dtypes
import concourse.bass as bass
import concourse.mybir as mybir
from concourse.bass import IndirectOffsetOnAxis
from concourse.bass_utils import run_bass_kernel_spmd

F32 = mybir.dt.float32
BF16 = mybir.dt.bfloat16
I32 = mybir.dt.int32
AF = mybir.ActivationFunctionType
ALU = mybir.AluOpType
AX = mybir.AxisListType

T = 2048
DM = 1024
NG = DM // 128
NTT = T // 512
NCORES = 8
KCAP = 352
EROWS = 8 * KCAP

ENGINES = ("sync", "tensor", "vector", "scalar", "gpsimd")
QSEM = {"sync": "Ds", "gpsimd": "Dg", "scalar": "Da"}
MSEM = {"tensor": "T", "vector": "V", "scalar": "S", "gpsimd": "P", "sync": "Y"}


class KB:
    def __init__(self, name):
        self.name = name
        self.nc = bass.Bass("TRN2", target_bir_lowering=False)
        self.es = contextlib.ExitStack()
        self.streams = {e: [] for e in ENGINES}
        self.semh = {}
        self.cnt = {}
        self.needed = {}
        self.dma_uses = {}

    def sem(self, name):
        if name not in self.semh:
            self.semh[name] = self.es.enter_context(self.nc.semaphore(name))
            self.cnt[name] = 0
        return name

    def sbuf(self, name, shape, dt):
        return self.es.enter_context(self.nc.sbuf_tensor(name, shape, dt))

    def psum(self, name, shape, dt=F32):
        return self.es.enter_context(self.nc.psum_tensor(name, shape, dt))

    def din(self, name, shape, dt=F32):
        return self.nc.dram_tensor(name, shape, dt, kind="ExternalInput").ap()

    def dout(self, name, shape, dt=F32):
        return self.nc.dram_tensor(name, shape, dt, kind="ExternalOutput").ap()

    def dtmp(self, name, shape, dt):
        return self.nc.dram_tensor(name, shape, dt).ap()

    def do(self, eng, fn, *, waits=()):
        seq = len(self.streams[eng]) + 1
        self._register_waits(waits)
        self.streams[eng].append({"fn": fn, "waits": tuple(w for w in waits if w)})
        return ("@" + eng, seq)

    def dma(self, q, out, in_, *, chan, waits=(), **kw):
        u = self.dma_uses[chan] = self.dma_uses.get(chan, 0) + 1
        sname = "C_" + chan
        self.sem(sname)
        tgt = (sname, 16 * u)

        def fn(e, o=out, i=in_, kw=kw, sem=self.semh[sname]):
            e.dma_start(out=o, in_=i, **kw).then_inc(sem, 16)
        self.do(q, fn, waits=waits)
        return tgt

    def idma(self, out, out_idx, in_, in_idx, *, chan, bounds=None, waits=()):
        u = self.dma_uses[chan] = self.dma_uses.get(chan, 0) + 1
        sname = "C_" + chan
        self.sem(sname)
        tgt = (sname, 16 * u)

        def fn(e, o=out, oi=out_idx, i=in_, ii=in_idx, b=bounds,
               sem=self.semh[sname]):
            oo = IndirectOffsetOnAxis(oi, 0) if oi is not None else None
            io = IndirectOffsetOnAxis(ii, 0) if ii is not None else None
            e.indirect_dma_start(o, oo, i, io, bounds_check=b,
                                 oob_is_err=False).then_inc(sem, 16)
        self.do("gpsimd", fn, waits=waits)
        return tgt

    def _register_waits(self, waits):
        for w in waits:
            if w is None:
                continue
            s, v = w
            if s.startswith("@"):
                self.needed.setdefault(s[1:], set()).add(v)
            else:
                self.sem(s)

    def wait(self, eng, waits):
        self._register_waits(waits)
        self.streams[eng].append({"fn": None, "waits": tuple(w for w in waits if w)})

    def val(self, eng):
        eng = {v: k for k, v in MSEM.items()}.get(eng, eng)
        n = len(self.streams[eng])
        if n == 0:
            return None
        return ("@" + eng, n)

    def mm(self, ps, lhsT, rhs, start, stop, *, waits=()):
        return self.do("tensor",
                       lambda e, o=ps, l=lhsT, r=rhs, s1=start, s2=stop:
                       e.matmul(o, l, r, start=s1, stop=s2), waits=waits)

    def tt(self, eng, out, a, b, op, *, waits=()):
        return self.do(eng, lambda e, o=out, x=a, y=b, p=op:
                       e.tensor_tensor(o, x, y, p), waits=waits)

    def ts(self, eng, out, a, s1, s2=None, op0=ALU.mult, op1=None, *, waits=()):
        if s2 is None:
            return self.do(eng, lambda e, o=out, x=a, v=s1, p=op0:
                           e.tensor_scalar(o, x, v, None, p), waits=waits)
        return self.do(eng, lambda e, o=out, x=a, v1=s1, v2=s2, p0=op0, p1=op1:
                       e.tensor_scalar(o, x, v1, v2, p0, p1), waits=waits)

    def stt(self, eng, out, in0, scalar, in1, op0, op1, *, waits=()):
        return self.do(eng, lambda e, o=out, a=in0, s=scalar, b=in1, p0=op0, p1=op1:
                       e.scalar_tensor_tensor(o, a, s, b, p0, p1), waits=waits)

    def act(self, out, in_, func, *, bias=None, scale=None, accum=None, waits=()):
        def fn(e, o=out, i=in_, f=func, b=bias, s=scale, ac=accum):
            kw = {}
            if b is not None:
                kw["bias"] = b
            if s is not None:
                kw["scale"] = s
            if ac is not None:
                kw["accum_out"] = ac
            return e.activation(o, i, f, **kw)
        return self.do("scalar", fn, waits=waits)

    def copy(self, eng, out, in_, *, waits=()):
        if eng == "scalar":
            return self.do(eng, lambda e, o=out, i=in_: e.copy(o, i), waits=waits)
        return self.do(eng, lambda e, o=out, i=in_: e.tensor_copy(o, i), waits=waits)

    def red(self, out, in_, op, *, absv=None, waits=()):
        return self.do("vector", lambda e, o=out, i=in_, p=op, a=absv:
                       e.tensor_reduce(o, i, AX.X, p, apply_absolute_value=a),
                       waits=waits)

    def memset(self, eng, ap, v=0.0, *, waits=()):
        return self.do(eng, lambda e, a=ap, vv=v: e.memset(a, vv), waits=waits)

    def barrier(self):
        marks = [self.do(e, lambda en: en.engine_nop()) for e in ENGINES]
        for e in ENGINES:
            self.wait(e, marks)

    def build(self):
        values = {}
        for eng in ENGINES:
            need = sorted(self.needed.get(eng, ()))
            values[eng] = {s: i + 1 for i, s in enumerate(need)}
            if need:
                self.sem(MSEM[eng])

        def resolve(w):
            s, v = w
            if s.startswith("@"):
                e2 = s[1:]
                return (MSEM[e2], values[e2][v])
            return (s, v)

        def body(e, eng):
            vals = values[eng]
            msem = self.semh.get(MSEM[eng])
            for idx, entry in enumerate(self.streams[eng], 1):
                merged = {}
                for w in entry["waits"]:
                    sn, v = resolve(w)
                    if v > merged.get(sn, -1):
                        merged[sn] = v
                for sn, v in merged.items():
                    e.wait_ge(self.semh[sn], v)
                if entry["fn"] is not None:
                    entry["fn"](e)
                if idx in vals:
                    e.drain(fusable=True).then_inc(msem, 1)

        with self.nc.Block() as block:
            for eng in ENGINES:
                getattr(block, eng)(lambda e, eng=eng: body(e, eng))
        self.nc.finalize()
        return self.nc



def np_bf16(x):
    return np.asarray(x).astype(ml_dtypes.bfloat16)


def colmaj(b, p=128):
    b = np.asarray(b, np.float32)
    return np.ascontiguousarray(b.reshape(-1, p).T)


def tr(a):
    return np.ascontiguousarray(np.asarray(a, np.float32).T)



class Common:

    def __init__(self, kb, *, with_stats=False, alloc_xn=True, ln_bufs=None):
        self.kb = kb
        self.x_in = kb.din("x_in", [DM, T], F32)
        self.x_out = kb.dout("x_out", [DM, T], F32)
        self.onecol_in = kb.din("onecol", [128, 1], F32)
        self.onerow_in = kb.din("onerow", [1, 128], F32)
        self.x = kb.sbuf("x_sb", [128, NG * T], F32)
        if alloc_xn:
            self.xn = kb.sbuf("xn_sb", [128, NG * T], BF16)
        if ln_bufs is None:
            self.sq = kb.sbuf("sq_sb", [128, NG * 512], F32)[:, :]
            self.ln_tmp = kb.sbuf("ln_tmp", [128, 512], F32)[:, :]
            self.rows = kb.sbuf("rows_sb", [1, 3 * 512], F32)[:, :]
        else:
            self.sq, self.ln_tmp, self.rows = ln_bufs
        self.onecol = kb.sbuf("onecol_sb", [128, 1], F32)[:, :]
        self.onerow = kb.sbuf("onerow_sb", [1, 128], F32)[:, :]
        self.ps_s1 = kb.psum("ps_s1", [128, 512])
        self.ps_s2 = kb.psum("ps_s2", [128, 512])
        self.ps_mb = kb.psum("ps_mb", [128, 512])
        self.ps_rb = kb.psum("ps_rb", [128, 512])
        self.with_stats = with_stats
        if with_stats:
            self.stat_d = kb.dtmp("stat_d", [1, 2 * T], F32)

    def load_x(self):
        kb = self.kb
        self.x_loaded = []
        for g in range(NG):
            self.x_loaded.append(
                kb.dma("sync", self.x[:, g * T:(g + 1) * T],
                       self.x_in[g * 128:(g + 1) * 128, :], chan=f"x{g}"))
        self.ones_loaded = [
            kb.dma("scalar", self.onecol[:], self.onecol_in, chan="one_c"),
            kb.dma("scalar", self.onerow[:], self.onerow_in, chan="one_r"),
        ]

    def store_x(self, waits):
        kb = self.kb
        outs = []
        for g in range(NG):
            outs.append(kb.dma("sync", self.x_out[g * 128:(g + 1) * 128, :],
                               self.x[:, g * T:(g + 1) * T], chan="xo",
                               waits=waits if g == 0 else ()))
        kb.wait("sync", outs)

    def emit_ln_tile(self, t, xn_of, extra=()):
        kb = self.kb
        mrow = self.rows[0:1, 0:512]
        rrow = self.rows[0:1, 512:1024]
        trow = self.rows[0:1, 1024:1536]
        prev = getattr(self, "_ln_prev", None)
        mm_last = None
        for g in range(NG):
            xg = self.x[:, g * T + t * 512: g * T + (t + 1) * 512]
            sqg = self.sq[:, g * 512:(g + 1) * 512]
            w = [self.x_loaded[g]] if t == 0 else []
            if g == 0:
                w += [prev] + list(extra)
            sd = kb.act(sqg, xg, AF.Square, waits=w)
            kb.mm(self.ps_s1[0:1, :], self.onecol, xg, g == 0, g == NG - 1,
                  waits=([self.x_loaded[g]] if t == 0 else []) +
                        ([self.ones_loaded[0]] if (t == 0 and g == 0) else []) +
                        ([prev] + list(extra) if g == 0 else []))
            mm_last = kb.mm(self.ps_s2[0:1, :], self.onecol, sqg, g == 0, g == NG - 1,
                            waits=[sd])
        msqrow = self.ln_tmp[0:1, 0:512]
        stat_prev = getattr(self, "stat_w", None) if self.with_stats else None
        m1 = kb.ts("vector", mrow, self.ps_s1[0:1, :], 1.0 / DM,
                   waits=[mm_last] + (stat_prev or []))
        t1 = kb.ts("vector", trow, self.ps_s2[0:1, :], 1.0 / DM)
        sqm = kb.act(msqrow, mrow, AF.Square, waits=[m1])
        vv = kb.tt("vector", trow, trow, msqrow, ALU.subtract, waits=[sqm, t1])
        vv = kb.ts("vector", trow, trow, 1e-5, None, ALU.add, waits=[vv])
        l1 = kb.act(trow, trow, AF.Ln, waits=[vv])
        l2 = kb.act(rrow, trow, AF.Exp, scale=-0.5, waits=[l1])
        bm = kb.mm(self.ps_mb[:], self.onerow, mrow, True, True,
                   waits=[m1, kb.val("V"), self.ones_loaded[1]])
        br = kb.mm(self.ps_rb[:], self.onerow, rrow, True, True, waits=[l2])
        if self.with_stats:
            self.stat_w = [
                kb.dma("gpsimd", self.stat_d[0:1, t * 512:(t + 1) * 512], mrow,
                       chan="stat", waits=[m1]),
                kb.dma("gpsimd", self.stat_d[0:1, T + t * 512:T + (t + 1) * 512],
                       rrow, chan="stat", waits=[l2]),
            ]
        last = None
        for g in range(NG):
            xg = self.x[:, g * T + t * 512: g * T + (t + 1) * 512]
            s = kb.tt("vector", self.ln_tmp[:], xg, self.ps_mb[:], ALU.subtract,
                      waits=[bm, br] + ([last] if last else []))
            last = kb.tt("vector", xn_of(g), self.ln_tmp[:], self.ps_rb[:], ALU.mult,
                         waits=[s])
        self._ln_prev = last
        return last

    def emit_ln(self):
        for t in range(NTT):
            self.emit_ln_tile(
                t, lambda g, t=t: self.xn[:, g * T + t * 512: g * T + (t + 1) * 512])
        self.ln_done = self.kb.val("V")



def build_ffn0(debug=False):
    kb = KB("ffn0")
    cm = Common(kb, with_stats=debug)
    w1t_in = kb.din("w1t", [DM, 4096], F32)
    w2t_in = kb.din("w2t", [4096, DM], F32)
    b1_in = kb.din("b1c", [128, 32], F32)
    b2_in = kb.din("b2c", [128, 8], F32)
    w1q_d = kb.dtmp("w1q_d", [DM, 4096], BF16)
    w2q_d = kb.dtmp("w2q_d", [4096, DM], BF16)

    wq = kb.sbuf("wq_sb", [128, 4096], F32)
    sgn = kb.sbuf("sgn_sb", [128, 4096], BF16)
    msk = kb.sbuf("msk_sb", [128, 4096], BF16)
    acc = kb.sbuf("acc_sb", [128, 4], F32)
    rcol = kb.sbuf("rcol_sb", [128, 2], F32)
    h_sb = kb.sbuf("h_sb", [128, 32 * 512], BF16)
    qbf = h_sb
    slab1 = kb.sbuf("slab1", [128, 2 * 8 * 128], BF16)
    nslab2 = 1 if debug else 2
    slab2 = kb.sbuf("slab2", [128, nslab2 * 32 * 128], BF16)
    b1_sb = kb.sbuf("b1_sb", [128, 32], F32)
    b2h_sb = kb.sbuf("b2h_sb", [128, 8], F32)
    ps_a = kb.psum("ps_a", [128, 512])
    ps_b = kb.psum("ps_b", [128, 512])

    cm.load_x()
    bl1 = kb.dma("scalar", b1_sb[:], b1_in, chan="b1")
    bl2 = kb.dma("scalar", b2h_sb[:], b2_in, chan="b2")
    b2h = kb.ts("vector", b2h_sb[:], b2h_sb[:], 0.5, waits=[bl2])

    cm.emit_ln()

    z1 = kb.memset("vector", acc[:, 0:1], 0.0)
    z2 = kb.memset("vector", acc[:, 1:2], 0.0)

    def qpass1(w_in, nch, width, acc_col, zm, prev=None):
        for ch in range(nch):
            buf = wq[:, 0:width]
            ld = kb.dma("sync", buf, w_in[ch * 128:(ch + 1) * 128, :],
                        chan="wq", waits=[prev] if prev else [])
            col = rcol[:, 0:1]
            r = kb.red(col, buf, ALU.add, absv=True, waits=[ld, zm])
            prev = kb.tt("vector", acc_col, acc_col, col, ALU.add, waits=[r])
        return prev

    a1 = qpass1(w1t_in, 8, 4096, acc[:, 0:1], z1)
    a2 = qpass1(w2t_in, 32, 1024, acc[:, 1:2], z2, prev=a1)

    t1 = kb.mm(cm.ps_s1[0:1, 0:1], acc[:, 0:1], cm.onecol, True, True,
               waits=[a1, cm.ones_loaded[0]])
    t2 = kb.mm(cm.ps_s2[0:1, 0:1], acc[:, 1:2], cm.onecol, True, True, waits=[a2])
    r1 = kb.ts("vector", cm.rows[0:1, 0:1], cm.ps_s1[0:1, 0:1],
               0.5 / (4096 * DM), 0.5e-5, ALU.mult, ALU.add, waits=[t1])
    r2 = kb.ts("vector", cm.rows[0:1, 1:2], cm.ps_s2[0:1, 0:1],
               0.5 / (4096 * DM), 0.5e-5, ALU.mult, ALU.add, waits=[t2])
    bt1 = kb.mm(cm.ps_s1[:, 0:1], cm.onerow, cm.rows[0:1, 0:1], True, True,
                waits=[r1, cm.ones_loaded[1]])
    bt2 = kb.mm(cm.ps_s2[:, 0:1], cm.onerow, cm.rows[0:1, 1:2], True, True, waits=[r2])
    c1 = kb.copy("vector", acc[:, 2:3], cm.ps_s1[:, 0:1], waits=[bt1])
    c2 = kb.copy("vector", acc[:, 3:4], cm.ps_s2[:, 0:1], waits=[bt2])

    def qpass2(w_in, wq_d, nch, width, thr_col, thr_done, tag, prior=None):
        st = None
        sts = {0: prior[0] if prior else None, 1: prior[1] if prior else None}
        for ch in range(nch):
            buf = wq[:, 0:width]
            qb = qbf[:, (ch % 2) * 4096:(ch % 2) * 4096 + width]
            w = [kb.val("V"), kb.val("S")]
            ld = kb.dma("sync", buf, w_in[ch * 128:(ch + 1) * 128, :],
                        chan="wq", waits=w)
            sg = kb.act(sgn[:, 0:width], buf, AF.Sign, waits=[ld])
            ab = kb.act(buf, buf, AF.Abs, waits=[sg])
            mk = kb.ts("vector", msk[:, 0:width], buf, thr_col, None,
                       ALU.is_ge, waits=[ab, thr_done])
            qq = kb.tt("vector", qb, sgn[:, 0:width], msk[:, 0:width], ALU.mult,
                       waits=[sg, mk, sts[ch % 2]])
            st = kb.dma("sync", wq_d[ch * 128:(ch + 1) * 128, :], qb,
                        chan=f"{tag}{ch % 2}", waits=[qq])
            sts[ch % 2] = st
        return [sts[0], sts[1]]

    q1done = qpass2(w1t_in, w1q_d, 8, 4096, acc[:, 2:3], c1, "q1s")
    q2done = qpass2(w2t_in, w2q_d, 32, 1024, acc[:, 3:4], c2, "q2s", prior=q1done)

    sv1 = slab1[:, :].rearrange("p (b kg j) -> p b kg j", b=2, kg=8)
    sv2 = slab2[:, :].rearrange("p (b kg j) -> p b kg j", b=nslab2, kg=32)
    w1qv = w1q_d.rearrange("(kg p) f -> p kg f", p=128)
    w2qv = w2q_d.rearrange("(kg p) f -> p kg f", p=128)
    bank_free = {0: None, 1: None}
    slab_last_use = {}
    for t in range(NTT):
        h_ready = []
        for m in range(32):
            sb = m % 2
            ld = kb.dma("sync", sv1[:, sb], w1qv[:, :, m * 128:(m + 1) * 128],
                        chan=f"sl1{sb}",
                        waits=(q1done if t == 0 else []) +
                              [slab_last_use.get(("1", sb))])
            ps = (ps_a, ps_b)[m % 2]
            mmw = [ld, cm.ln_done if m == 0 else None, bank_free[m % 2]]
            mml = None
            for kg in range(NG):
                mml = kb.mm(ps[:], sv1[:, sb, kg],
                            cm.xn[:, kg * T + t * 512: kg * T + (t + 1) * 512],
                            kg == 0, kg == NG - 1, waits=mmw if kg == 0 else ())
            slab_last_use[("1", sb)] = mml
            ev = kb.act(h_sb[:, m * 512:(m + 1) * 512], ps[:], AF.Relu,
                        bias=b1_sb[:, m:m + 1],
                        waits=[mml] + ([bl1] + q1done + q2done if t == 0 else []))
            bank_free[m % 2] = ev
            h_ready.append(ev)
        for m in range(8):
            sb = m % nslab2
            ld = kb.dma("sync", sv2[:, sb], w2qv[:, :, m * 128:(m + 1) * 128],
                        chan=f"sl2{sb}",
                        waits=(q2done if t == 0 else []) +
                              [slab_last_use.get(("2", sb))])
            ps = (ps_a, ps_b)[m % 2]
            mml = None
            for kg in range(32):
                mml = kb.mm(ps[:], sv2[:, sb, kg], h_sb[:, kg * 512:(kg + 1) * 512],
                            kg == 0, kg == 31,
                            waits=([ld, bank_free[m % 2]] + h_ready) if kg == 0 else ())
            slab_last_use[("2", sb)] = mml
            tv = kb.ts("vector", cm.ln_tmp[:], ps[:], 0.5, b2h_sb[:, m:m + 1],
                       ALU.mult, ALU.add, waits=[mml, b2h, kb.val("V")])
            xg = cm.x[:, m * T + t * 512: m * T + (t + 1) * 512]
            upd = kb.tt("vector", xg, xg, cm.ln_tmp[:], ALU.add, waits=[tv])
            bank_free[m % 2] = upd
    cm.store_x([kb.val("V")])
    if debug:
        dbg_acc = kb.dout("dbg_acc", [128, 4], F32)
        kb.dma("gpsimd", dbg_acc, acc[:], chan="dbg",
               waits=[kb.val("V"), kb.val("S"), kb.val("T")])
        dbg_stat = kb.dout("dbg_stat", [1, 2 * T], F32)
        kb.dma("gpsimd", dbg_stat, cm.stat_d, chan="dbg",
               waits=[kb.val("V"), kb.val("S"), kb.val("T"), cm.stat_w[0], cm.stat_w[1]])
        dbg_xn = kb.dout("dbg_xn", [DM, T], BF16)
        dbg_w1q = kb.dout("dbg_w1q", [DM, 4096], BF16)
        dbg_w2q = kb.dout("dbg_w2q", [4096, DM], BF16)
        w = [kb.val("V"), kb.val("S"), kb.val("T")]
        outs = []
        for g in range(NG):
            outs.append(kb.dma("gpsimd", dbg_xn[g * 128:(g + 1) * 128, :],
                               cm.xn[:, g * T:(g + 1) * T], chan="dbg",
                               waits=w if g == 0 else ()))
        outs.append(kb.dma("gpsimd", dbg_w1q, w1q_d, chan="dbg"))
        outs.append(kb.dma("gpsimd", dbg_w2q, w2q_d, chan="dbg"))
        kb.wait("gpsimd", outs)
    return kb, cm



DI = 2048
NGI = DI // 128
NST = 16
TC = 512


def build_mamba(debug=False):
    kb = KB("mamba")

    arena = kb.sbuf("arena", [128, 39 * 1024], mybir.dt.uint8)
    def carve(off, n, dt, rows=128):
        nb = n * mybir.dt.size(dt)
        return arena[0:rows, off:off + nb].bitcast(dt)
    sq_ap = carve(0, NG * 512, F32)
    zsig_ap = carve(16384, NGI * TC, BF16)
    lntmp_ap = carve(16384 + 8192, 512, F32)
    rows_ap = carve(16384 + 8192 + 2048, 1536, F32, rows=1)
    H = carve(0, TC * NST, BF16)
    convA = carve(32768, 512, F32)
    convB = carve(32768 + 2048, 512, F32)
    y_sb = carve(32768 + 4096, 512, F32)

    cm = Common(kb, alloc_xn=False, ln_bufs=(sq_ap, lntmp_ap, rows_ap))
    xn = kb.sbuf("xn_c", [128, NG * 512], BF16)

    wino_in = kb.din("wino", [DM, 2 * DI], F32)
    bino_in = kb.din("bino", [128, 32], F32)
    wini_in = kb.din("wini", [DI, 2 * DI], F32)
    convw_in = kb.din("convw", [128, 16 * 4], F32)
    convb_in = kb.din("convb", [128, 16], F32)
    wxp_in = kb.din("wxp", [DI, 160], F32)
    wdt_in = kb.din("wdt", [128, DI], F32)
    bdt_in = kb.din("bdt", [128, 16], F32)
    alog_in = kb.din("alog", [128, 16 * 16], F32)
    dskip_in = kb.din("dskip", [128, 16], F32)
    woi_in = kb.din("woi", [DI, DI], F32)
    woo_in = kb.din("woo", [DI, DM], F32)
    boo_in = kb.din("boo", [128, 8], F32)
    rowsel_in = kb.din("rowsel", [32, 32 * 128], BF16)

    bino = kb.sbuf("bino_sb", [128, 32], F32)
    convw = kb.sbuf("convw_sb", [128, 64], F32)
    convb = kb.sbuf("convb_sb", [128, 16], F32)
    bdt = kb.sbuf("bdt_sb", [128, 16], F32)
    dskip = kb.sbuf("dskip_sb", [128, 16], F32)
    boo = kb.sbuf("boo_sb", [128, 8], F32)
    A_sb = kb.sbuf("A_sb", [128, 256], F32)
    wxp = kb.sbuf("wxp_sb", [128, 16 * 160], BF16)
    wdt = kb.sbuf("wdt_sb", [128, DI], BF16)
    rowsel = kb.sbuf("rowsel_sb", [32, 32 * 128], BF16)
    hcar = kb.sbuf("hcar_sb", [128, 256], F32)

    xproj = kb.sbuf("xproj_sb", [128, NGI * TC], BF16)
    zsig = zsig_ap
    XMW = 516
    xm = kb.sbuf("xm_sb", [128, NGI * XMW], BF16)
    zmsil = kb.sbuf("zmsil_sb", [128, NGI * TC], BF16)
    xc = kb.sbuf("xc_sb", [128, NGI * TC], BF16)
    dtb = kb.sbuf("dtb_sb", [128, TC], BF16)
    BC = kb.sbuf("BC_sb", [32, TC], BF16)
    tail = kb.sbuf("tail_sb", [128, 16 * 3], BF16)
    delta = xm[:, 0:1024].bitcast(F32)
    w_sb = xm[:, 1024:2048].bitcast(F32)
    a_sb = xm[:, 2048:3072].bitcast(F32)
    b_sb = xm[:, 3072:4096].bitcast(F32)
    sptA = xm[:, 4096:5120].bitcast(F32)
    sptB = xm[:, 5120:6144].bitcast(F32)
    sptC = xm[:, 6144:7168].bitcast(F32)

    slab = kb.sbuf("slab_sb", [128, 2 * 16 * 128], BF16)
    sv = slab[:, :].rearrange("p (b kg j) -> p b kg j", b=2, kg=16)

    ps_a = kb.psum("ps_a", [128, 512])
    ps_b = kb.psum("ps_b", [128, 512])
    ps_bc = kb.psum("ps_bc", [32, 512])
    ps_d = kb.psum("ps_d", [128, 512])

    cm.load_x()
    consts = []
    for name, dst, src in [("bino", bino, bino_in), ("convw", convw, convw_in),
                           ("convb", convb, convb_in), ("bdt", bdt, bdt_in),
                           ("dskip", dskip, dskip_in), ("boo", boo, boo_in),
                           ("rowsel", rowsel, rowsel_in)]:
        consts.append(kb.dma("scalar", dst[:], src, chan="c_" + name))
    wxpl = kb.dma("gpsimd", wxp[:].rearrange("p (kg f) -> p kg f", f=160),
                  wxp_in.rearrange("(kg p) f -> p kg f", p=128), chan="c_wxp")
    wdtl = kb.dma("gpsimd", wdt[:], wdt_in, chan="c_wdt")
    alogl = kb.dma("scalar", A_sb[:], alog_in, chan="c_alog")
    ae = kb.act(A_sb[:], A_sb[:], AF.Exp, waits=[alogl])
    A_done = kb.ts("vector", A_sb[:], A_sb[:], -1.0, waits=[ae])
    hz = kb.memset("vector", hcar[:], 0.0)

    wino_v = wino_in.rearrange("(kg p) f -> p kg f", p=128)
    wini_v = wini_in.rearrange("(kg p) f -> p kg f", p=128)
    woi_v = woi_in.rearrange("(kg p) f -> p kg f", p=128)
    woo_v = woo_in.rearrange("(kg p) f -> p kg f", p=128)

    slab_guard = {0: None, 1: None}
    bank_guard = {0: None, 1: None}
    dbc_guard = None
    dd_guard = None

    state = {"conv_tail": None, "stageB_last": None, "scanH_free": None,
             "stageA_last": None, "lnx": None}

    def stream_mm(st, wview, n_k, m_range, rhs_of, evac, first_waits=(), tag=""):
        toks = []
        for mi, m in enumerate(m_range):
            sb = mi % 2
            ld = kb.dma("gpsimd", sv[:, sb, 0:n_k, :],
                        wview[:, 0:n_k, m * 128:(m + 1) * 128],
                        chan=f"sl{sb}", waits=[slab_guard[sb]])
            ps = (ps_a, ps_b)[mi % 2]
            mml = None
            for kg in range(n_k):
                mml = kb.mm(ps[:], sv[:, sb, kg], rhs_of(kg), kg == 0, kg == n_k - 1,
                            waits=([ld, bank_guard[mi % 2]] + list(first_waits))
                            if kg == 0 else ())
            slab_guard[sb] = mml
            tok = evac(m, ps, mml)
            bank_guard[mi % 2] = tok
            toks.append(tok)
        return toks

    for c in range(NTT):
        cs = slice(c * TC, (c + 1) * TC)
        ln_tok = cm.emit_ln_tile(
            c, lambda g: xn[:, g * 512:(g + 1) * 512],
            extra=[state["stageA_last"], state["scanH_free"],
                   state.get("stageF_last")])
        state["lnx"] = ln_tok

        def evacA(m, ps, mmtok):
            if m < 16:
                return kb.ts("vector", xproj[:, m * TC:(m + 1) * TC], ps[:],
                             bino[:, m:m + 1], None, ALU.add,
                             waits=[mmtok, consts[0], state.get("xproj_free")])
            return kb.act(zsig[:, (m - 16) * TC:(m - 15) * TC], ps[:], AF.Sigmoid,
                          bias=bino[:, m:m + 1],
                          waits=[mmtok, consts[0], state["stageF_last"]]
                          if "stageF_last" in state else [mmtok, consts[0]])
        atoks = stream_mm(state, wino_v, NG, range(32),
                          lambda kg: xn[:, kg * 512:(kg + 1) * 512],
                          evacA, first_waits=[ln_tok], tag="A")
        state["stageA_last"] = atoks[-1]

        tail_toks = []
        for g in range(NGI):
            xg = xm[:, g * XMW:g * XMW + XMW]
            if c == 0:
                tail_toks.append(kb.memset("vector", xg[:, 0:3], 0.0))
            else:
                tail_toks.append(kb.copy("vector", xg[:, 0:3],
                                         tail[:, g * 3:(g + 1) * 3],
                                         waits=[state["tailsv"][g],
                                                state.get("scan_w_done")]))
        def evacB(m, ps, mmtok):
            if m < 16:
                return kb.copy("vector", xm[:, m * XMW + 3:m * XMW + 515], ps[:],
                               waits=[mmtok, tail_toks[m], state.get("scan_w_done")])
            return kb.act(zmsil[:, (m - 16) * TC:(m - 15) * TC], ps[:], AF.Silu,
                          waits=[mmtok, state.get("zmsil_free")])
        btoks = stream_mm(state, wini_v, NGI, range(32),
                          lambda kg: xproj[:, kg * TC:(kg + 1) * TC],
                          evacB, first_waits=atoks, tag="B")
        state["stageB_last"] = btoks[-1]
        tailsv = []
        for g in range(NGI):
            tailsv.append(kb.copy("vector", tail[:, g * 3:(g + 1) * 3],
                                  xm[:, g * XMW + 512:g * XMW + 515],
                                  waits=[btoks[g]] +
                                        ([state["tailsv"][g]] if c else [])))
        state["tailsv"] = tailsv

        convdone = []
        for g in range(NGI):
            xg = xm[:, g * XMW:g * XMW + XMW]
            wof = lambda j: convw[:, g * 4 + j:g * 4 + j + 1]
            t1 = kb.ts("vector", convA[:, :], xg[:, 0:512], wof(0), None, ALU.mult,
                       waits=[btoks[g], tail_toks[g], consts[1],
                              state.get("convA_free")])
            t2 = kb.stt("vector", convB[:, :], xg[:, 1:513], wof(1), convA[:, :],
                        ALU.mult, ALU.add, waits=[t1, state.get("convB_free")])
            t3 = kb.stt("vector", convA[:, :], xg[:, 2:514], wof(2), convB[:, :],
                        ALU.mult, ALU.add, waits=[t2])
            t4 = kb.stt("vector", convB[:, :], xg[:, 3:515], wof(3), convA[:, :],
                        ALU.mult, ALU.add, waits=[t3])
            state["convA_free"] = t4
            xcg = kb.act(xc[:, g * TC:(g + 1) * TC], convB[:, :], AF.Silu,
                         bias=convb[:, g:g + 1],
                         waits=[t4, consts[2], state.get("yb_free", (None,) * 99)[g]
                                if isinstance(state.get("yb_free"), list) else None])
            state["convB_free"] = xcg
            convdone.append(xcg)
        state["convdone"] = convdone

        wxp_v = wxp[:].rearrange("p (kg f) -> p kg f", f=160)
        mmd = mmbc = None
        for kg in range(NGI):
            mmd = kb.mm(ps_a[:], wxp_v[:, kg, 0:128], xc[:, kg * TC:(kg + 1) * TC],
                        kg == 0, kg == NGI - 1,
                        waits=[wxpl, bank_guard[0]] + convdone if kg == 0 else ())
            mmbc = kb.mm(ps_bc[:], wxp_v[:, kg, 128:160], xc[:, kg * TC:(kg + 1) * TC],
                         kg == 0, kg == NGI - 1,
                         waits=[dbc_guard] if kg == 0 else ())
        dtb_t = kb.copy("vector", dtb[:], ps_a[:], waits=[mmd, state.get("dtb_free")])
        bank_guard[0] = dtb_t
        bc_t = kb.act(BC[:], ps_bc[:], AF.Copy, waits=[mmbc, state.get("bc_free")])

        Hv = H.rearrange("p (t n) -> p t n", n=NST)
        yb_free = [None] * NGI
        carries = state.get("carries", [None] * NGI)
        for g in range(NGI):
            mmdl = kb.mm(ps_d[:], wdt[:, g * 128:(g + 1) * 128], dtb[:],
                         True, True, waits=[wdtl, dtb_t, dd_guard])
            bcol = bdt[:, g:g + 1]
            s1 = kb.act(sptA[:, :], ps_d[:], AF.Abs, bias=bcol,
                        waits=[mmdl, consts[3], state.get("sptA_free")])
            s2 = kb.act(sptB[:, :], sptA[:, :], AF.Exp, scale=-1.0,
                        waits=[s1, state.get("sptB_free")])
            s3 = kb.act(sptA[:, :], sptB[:, :], AF.Ln, bias=1.0, waits=[s2])
            s4 = kb.act(sptC[:, :], ps_d[:], AF.Relu, bias=bcol,
                        waits=[s1, state.get("sptC_free")])
            dd_guard = s4
            dl = kb.tt("vector", delta[:, :], sptA[:, :], sptC[:, :], ALU.add,
                       waits=[s3, s4, state.get("delta_free")])
            state["sptA_free"] = dl
            state["sptC_free"] = dl
            wg = kb.tt("vector", w_sb[:, :], delta[:, :], xc[:, g * TC:(g + 1) * TC],
                       ALU.mult, waits=[dl, convdone[g], state.get("w_free")])
            scan_toks = []
            prev_scan = state.get("last_scan")
            prev_b = state.get("last_b")
            prev_cb = state.get("last_cb")
            for n in range(NST):
                an = kb.act(a_sb[:, :], delta[:, :], AF.Exp,
                            scale=A_sb[:, g * 16 + n:g * 16 + n + 1],
                            waits=[dl, A_done, prev_scan])
                mmb = kb.mm(ps_a[:], rowsel[:, n * 128:(n + 1) * 128], BC[:],
                            True, True, waits=[bc_t, consts[6], bank_guard[0],
                                               prev_b])
                bn = kb.tt("vector", b_sb[:, :], ps_a[:], w_sb[:, :], ALU.mult,
                           waits=[mmb, wg] + ([prev_scan] if prev_scan else []))
                bank_guard[0] = bn
                prev_b = bn
                sc = kb.do("vector",
                           lambda e, o=Hv[:, :, n], a=a_sb[:, :], b=b_sb[:, :],
                           ini=hcar[:, g * 16 + n:g * 16 + n + 1]:
                           e.tensor_tensor_scan(o, a, b, ini, ALU.mult, ALU.add),
                           waits=[an, bn, hz, state.get("H_free"), carries[g]])
                prev_scan = sc
                scan_toks.append(sc)
            carry = kb.copy("vector", hcar[:, g * 16:(g + 1) * 16],
                            H[:, (TC - 1) * NST:TC * NST], waits=[scan_toks[-1]])
            carries[g] = carry
            prod_toks = []
            prev_prod = None
            for n in range(NST):
                mmc = kb.mm(ps_b[:], rowsel[:, (16 + n) * 128:(17 + n) * 128], BC[:],
                            True, True, waits=[bc_t, bank_guard[1], prev_cb])
                pr = kb.tt("vector", Hv[:, :, n], Hv[:, :, n], ps_b[:], ALU.mult,
                           waits=[mmc, carry, scan_toks[n]])
                bank_guard[1] = pr
                prev_cb = pr
                prod_toks.append(pr)
            yt = kb.red(y_sb[:, :], Hv, ALU.add,
                        waits=prod_toks + [state.get("y_free")])
            state["H_free"] = yt
            td = kb.stt("vector", sptA[:, :], xc[:, g * TC:(g + 1) * TC],
                        dskip[:, g:g + 1], y_sb[:, :], ALU.mult, ALU.add,
                        waits=[yt, consts[4], dl])
            state["sptA_free"] = td
            ybt = kb.tt("vector", xc[:, g * TC:(g + 1) * TC], sptA[:, :],
                        zmsil[:, g * TC:(g + 1) * TC], ALU.mult,
                        waits=[td, btoks[16 + g]])
            yb_free[g] = ybt
            state["sptA_free"] = ybt
            state["y_free"] = td
            state["delta_free"] = an
            state["w_free"] = bn
            state["sptB_free"] = s3
            state["last_scan"] = prev_scan
            state["last_b"] = prev_b
            state["last_cb"] = prev_cb
        state["carries"] = carries
        state["dtb_free"] = mmdl
        state["bc_free"] = prev_cb
        dbc_guard = bc_t
        state["yb_free"] = yb_free
        state["zmsil_free"] = yb_free[-1]
        state["scan_w_done"] = yb_free[-1]

        def evacF(m, ps, mmtok):
            return kb.tt("vector", xproj[:, m * TC:(m + 1) * TC], ps[:],
                         zsig[:, m * TC:(m + 1) * TC], ALU.mult,
                         waits=[mmtok, atoks[16 + m]])
        ftoks = stream_mm(state, woi_v, NGI, range(16),
                          lambda kg: xc[:, kg * TC:(kg + 1) * TC],
                          evacF, first_waits=yb_free, tag="F")
        state["stageF_last"] = ftoks[-1]

        def evacG(m, ps, mmtok):
            xg = cm.x[:, m * T + c * TC: m * T + (c + 1) * TC]
            return kb.stt("vector", xg, ps[:], boo[:, m:m + 1], xg,
                          ALU.add, ALU.add, waits=[mmtok, consts[5]])
        gtoks = stream_mm(state, woo_v, NGI, range(8),
                          lambda kg: xproj[:, kg * TC:(kg + 1) * TC],
                          evacG, first_waits=[ftoks[-1]], tag="G")
        state["scanH_free"] = state["H_free"]
        state["xproj_free"] = gtoks[-1]

    cm.store_x([kb.val("V")])
    return kb, cm


def mamba_inputs(inp):
    p = "L0_attn_"
    sel = np.zeros((32, 32 * 128), np.float32)
    for n in range(32):
        sel[n, n * 128:(n + 1) * 128] = 1.0
    return {
        "wino": np.asarray(inp[p + "W_in_o"], np.float32),
        "bino": colmaj(inp[p + "b_in_o"]),
        "wini": np.asarray(inp[p + "W_in_i"], np.float32),
        "convw": np.ascontiguousarray(
            np.asarray(inp[p + "conv_w"], np.float32).reshape(16, 128, 4)
            .transpose(1, 0, 2).reshape(128, 64)),
        "convb": colmaj(inp[p + "conv_b"]),
        "wxp": np.asarray(inp[p + "W_xp"], np.float32),
        "wdt": np.asarray(inp[p + "W_dt"], np.float32),
        "bdt": colmaj(inp[p + "b_dt"]),
        "alog": np.ascontiguousarray(
            np.asarray(inp[p + "A_log"], np.float32).reshape(16, 128, 16)
            .transpose(1, 0, 2).reshape(128, 256)),
        "dskip": colmaj(inp[p + "D_skip"]),
        "woi": np.asarray(inp[p + "W_out_i"], np.float32),
        "woo": np.asarray(inp[p + "W_out_o"], np.float32),
        "boo": colmaj(inp[p + "b_out_o"]),
        "rowsel": np_bf16(sel),
        "onecol": np.ones((128, 1), np.float32),
        "onerow": np.ones((1, 128), np.float32),
    }


def run_mamba(x_cores, inp):
    kb, nc = get_prog("mamba")
    base = mamba_inputs(inp)
    in_maps = [dict(base, x_in=x) for x in x_cores]
    res = run_bass_kernel_spmd(nc, in_maps, core_ids=list(range(NCORES)))
    return [r["x_out"] for r in res.results]



def build_attn(debug=False):
    kb = KB("attn")
    cm = Common(kb)

    wq_in = kb.din("wqT", [DM, DM], F32)
    wk_in = kb.din("wkT", [DM, DM], F32)
    wv_in = kb.din("wvT", [DM, DM], F32)
    wo_in = kb.din("woT", [DM, DM], F32)
    bq_in = kb.din("bq", [128, 8], F32)
    bk_in = kb.din("bk", [128, 8], F32)
    bv_in = kb.din("bvrow", [1, DM], F32)
    bo_in = kb.din("bo", [128, 8], F32)
    mask_in = kb.din("cmask", [128, 128], F32)
    ident_in = kb.din("ident", [128, 128], BF16)
    onerowb_in = kb.din("onerowb", [1, 128], BF16)

    qT = kb.sbuf("qT_sb", [128, 2 * T], BF16)
    kT = kb.sbuf("kT_sb", [128, 2 * T], BF16)
    vtm = kb.sbuf("vtm_sb", [128, 16 * 256], BF16)
    oT = kb.sbuf("oT_sb", [128, NG * T], BF16)
    srow = kb.sbuf("srow_sb", [128, T], F32)
    P = kb.sbuf("P_sb", [128, T], BF16)
    Pt = kb.sbuf("Pt_sb", [128, T], BF16)
    mask = kb.sbuf("mask_sb", [128, 128], F32)
    ident = kb.sbuf("ident_sb", [128, 128], BF16)
    bq = kb.sbuf("bq_sb", [128, 8], F32)
    bk = kb.sbuf("bk_sb", [128, 8], F32)
    bo = kb.sbuf("bo_sb", [128, 8], F32)
    bvr = kb.sbuf("bvr_sb", [1, DM], BF16)
    mxc = kb.sbuf("mx_sb", [128, 4], F32)
    slabq = kb.sbuf("slabq_sb", [128, 2 * 8 * 128], BF16)
    slabv = kb.sbuf("slabv_sb", [128, 8 * 256], BF16)
    svq = slabq[:, :].rearrange("p (b kg j) -> p b kg j", b=2, kg=8)
    svv = slabv[:, :].rearrange("p (kg j) -> p kg j", kg=8)

    ps_a = kb.psum("ps_a", [128, 512])
    ps_b = kb.psum("ps_b", [128, 512])
    ps_tr = kb.psum("ps_tr", [128, 1024], BF16)
    ps_o = kb.psum("ps_o", [128, 512])

    cm.load_x()
    consts = {}
    for name, dst, src in [("bq", bq, bq_in), ("bk", bk, bk_in), ("bo", bo, bo_in),
                           ("mask", mask, mask_in), ("ident", ident, ident_in)]:
        consts[name] = kb.dma("scalar", dst[:], src, chan="c_" + name)
    consts["bvr"] = kb.dma("gpsimd", bvr[:], bv_in, chan="c_bvr")
    onerowb = kb.sbuf("onerowb_sb", [1, 128], BF16)
    consts["orb"] = kb.dma("scalar", onerowb[:], onerowb_in, chan="c_orb")
    bqs = kb.ts("vector", bq[:], bq[:], 0.125, waits=[consts["bq"]])

    cm.emit_ln()

    wq_v = wq_in.rearrange("(kg p) f -> p kg f", p=128)
    wk_v = wk_in.rearrange("(kg p) f -> p kg f", p=128)
    wv_v = wv_in.rearrange("(kg p) f -> p kg f", p=128)
    wo_v = wo_in.rearrange("(kg p) f -> p kg f", p=128)

    state = {}
    slab_guard = {0: None, 1: None}
    bank_guard = {0: None, 1: None}
    tr_guard = None
    o_guard = None

    def qk_build(w_v, dst, bias_sb, scale, hg, chan, extra=()):
        toks = []
        for mq in range(2):
            m = hg * 2 + mq
            sb = mq % 2
            bias_col = bias_sb[:, m:m + 1]
            ld = kb.dma("gpsimd", svq[:, sb], w_v[:, :, m * 128:(m + 1) * 128],
                        chan=f"{chan}{sb}", waits=[slab_guard[sb]])
            for t in range(NTT):
                ps = (ps_a, ps_b)[t % 2]
                mml = None
                for kg in range(NG):
                    mml = kb.mm(ps[:], svq[:, sb, kg],
                                cm.xn[:, kg * T + t * 512: kg * T + (t + 1) * 512],
                                kg == 0, kg == NG - 1,
                                waits=[ld, cm.ln_done, bank_guard[t % 2]] + list(extra)
                                if kg == 0 else ())
                if scale is None:
                    ev = kb.ts("vector", dst[:, mq * T + t * 512: mq * T + (t + 1) * 512],
                               ps[:], bias_col, None, ALU.add, waits=[mml])
                else:
                    ev = kb.ts("vector", dst[:, mq * T + t * 512: mq * T + (t + 1) * 512],
                               ps[:], scale, bias_col, ALU.mult, ALU.add, waits=[mml])
                bank_guard[t % 2] = ev
                toks.append(ev)
            slab_guard[sb] = mml
        return toks

    for hg in range(4):
        qtok = qk_build(wq_v, qT, bq, 0.125, hg, "slq", extra=[bqs])
        ktok = qk_build(wk_v, kT, bk, None, hg, "slk", extra=[consts["bk"]])
        vl = kb.dma("gpsimd", svv[:, :, :], wv_v[:, :, hg * 256:(hg + 1) * 256],
                    chan="slv", waits=[state.get("v_slab_free")])
        vtoks = []
        for tt_ in range(16):
            mml = None
            for kg in range(NG):
                mml = kb.mm(ps_a[:, 0:256], cm.xn[:, kg * T + tt_ * 128: kg * T + (tt_ + 1) * 128],
                            svv[:, kg], kg == 0, False,
                            waits=[vl, cm.ln_done, bank_guard[0]] if kg == 0 else ())
            mml = kb.mm(ps_a[:, 0:256], onerowb[:], bvr[0:1, hg * 256:(hg + 1) * 256],
                        False, True, waits=[consts["bvr"], consts["orb"]])
            ev = kb.copy("vector", vtm[:, tt_ * 256:(tt_ + 1) * 256], ps_a[:, 0:256],
                         waits=[mml])
            bank_guard[0] = ev
            vtoks.append(ev)
        state["v_slab_free"] = mml

        for hl in range(4):
            h = hg * 4 + hl
            qbase = (hl % 2) * 64
            qhalf = (hl // 2) * T
            for qc in range(16):
                L = (qc + 1) * 128
                nkt = (L + 511) // 512
                ev_toks = []
                for kt in range(nkt):
                    wdt_ = min(512, L - kt * 512)
                    ps = (ps_a, ps_b)[kt % 2]
                    mml = kb.mm(ps[:, 0:wdt_],
                                qT[qbase:qbase + 64, qhalf + qc * 128: qhalf + (qc + 1) * 128],
                                kT[qbase:qbase + 64, qhalf + kt * 512: qhalf + kt * 512 + wdt_],
                                True, True,
                                waits=[qtok[-1], ktok[-1], bank_guard[kt % 2]])
                    ev = kb.copy("vector", srow[:, kt * 512:kt * 512 + wdt_],
                                 ps[:, 0:wdt_], waits=[mml, state.get("srow_free")])
                    bank_guard[kt % 2] = ev
                    ev_toks.append(ev)
                mk_ = kb.tt("vector", srow[:, L - 128:L], srow[:, L - 128:L], mask[:],
                            ALU.add, waits=[consts["mask"], ev_toks[-1]])
                mx = kb.red(mxc[:, 0:1], srow[:, 0:L], ALU.max,
                            waits=[mk_] + ev_toks)
                ng = kb.ts("vector", mxc[:, 1:2], mxc[:, 0:1], -1.0, waits=[mx])
                pe_ = kb.act(P[:, 0:L], srow[:, 0:L], AF.Exp, bias=mxc[:, 1:2],
                             accum=mxc[:, 2:3], waits=[ng, state.get("P_free")])
                rc_ = kb.do("vector", lambda e, o=mxc[:, 3:4], i=mxc[:, 2:3]:
                            e.reciprocal(o, i), waits=[pe_])
                pn = kb.ts("vector", P[:, 0:L], P[:, 0:L], mxc[:, 3:4], None,
                           ALU.mult, waits=[rc_])
                pt_toks = []
                for ktile in range(qc + 1):
                    tr = kb.do("tensor", lambda e, o=ps_tr[:, 0:128],
                               i=P[:, ktile * 128:(ktile + 1) * 128], idn=ident[:]:
                               e.transpose(o, i, idn),
                               waits=[pn, consts["ident"], tr_guard])
                    tev = kb.copy("scalar", Pt[:, ktile * 128:(ktile + 1) * 128],
                                  ps_tr[:, 0:128], waits=[tr, state.get("Pt_free")])
                    tr_guard = tev
                    pt_toks.append(tev)
                mml = None
                for ktile in range(qc + 1):
                    mml = kb.mm(ps_o[0:64, 0:128],
                                vtm[:, ktile * 256 + hl * 64: ktile * 256 + hl * 64 + 64],
                                Pt[:, ktile * 128:(ktile + 1) * 128],
                                ktile == 0, ktile == qc,
                                waits=[pt_toks[ktile], vtoks[ktile],
                                       o_guard if ktile == 0 else None])
                g_o = h // 2
                off_o = (h % 2) * 64
                oev = kb.copy("vector",
                              oT[off_o:off_o + 64, g_o * T + qc * 128: g_o * T + (qc + 1) * 128],
                              ps_o[0:64, 0:128], waits=[mml])
                o_guard = oev
                state["srow_free"] = pe_
                state["P_free"] = tr
                state["Pt_free"] = mml
    o_done = kb.val("V")

    for m in range(8):
        sb = m % 2
        ld = kb.dma("gpsimd", svq[:, sb], wo_v[:, :, m * 128:(m + 1) * 128],
                    chan=f"slo{sb}", waits=[slab_guard[sb]])
        for t in range(NTT):
            ps = (ps_a, ps_b)[t % 2]
            mml = None
            for kg in range(NG):
                mml = kb.mm(ps[:], svq[:, sb, kg],
                            oT[:, kg * T + t * 512: kg * T + (t + 1) * 512],
                            kg == 0, kg == NG - 1,
                            waits=[ld, o_done, bank_guard[t % 2]] if kg == 0 else ())
            xg = cm.x[:, m * T + t * 512: m * T + (t + 1) * 512]
            ev = kb.stt("vector", xg, ps[:], bo[:, m:m + 1], xg, ALU.add, ALU.add,
                        waits=[mml, consts["bo"]])
            bank_guard[t % 2] = ev
        slab_guard[sb] = mml
    cm.store_x([kb.val("V")])
    return kb, cm


def attn_inputs(inp):
    mask = np.zeros((128, 128), np.float32)
    mask[np.triu_indices(128, 1)] = -1e30
    return {
        "onerowb": np_bf16(np.ones((1, 128))),
        "wqT": tr(inp["L1_attn_Wq"]), "wkT": tr(inp["L1_attn_Wk"]),
        "wvT": tr(inp["L1_attn_Wv"]), "woT": tr(inp["L1_attn_Wo"]),
        "bq": colmaj(inp["L1_attn_bq"]), "bk": colmaj(inp["L1_attn_bk"]),
        "bvrow": np.asarray(inp["L1_attn_bv"], np.float32).reshape(1, DM),
        "bo": colmaj(inp["L1_attn_bo"]),
        "cmask": mask, "ident": np_bf16(np.eye(128)),
        "onecol": np.ones((128, 1), np.float32),
        "onerow": np.ones((1, 128), np.float32),
    }


def run_attn(x_cores, inp):
    kb, nc = get_prog("attn")
    base = attn_inputs(inp)
    in_maps = [dict(base, x_in=x) for x in x_cores]
    res = run_bass_kernel_spmd(nc, in_maps, core_ids=list(range(NCORES)))
    return [r["x_out"] for r in res.results]



def build_moe(sim1=False, debug=False):
    kb = KB("moe")
    arena = kb.sbuf("arena", [128, 44 * 1024], mybir.dt.uint8)
    def carve(off, n, dt, rows=128):
        nb = n * mybir.dt.size(dt)
        return arena[0:rows, off:off + nb].bitcast(dt)
    sq_ap = carve(0, NG * 512, F32)
    Gfm = carve(0, NG * EROWS, BF16)
    lntmp = kb.sbuf("lntmp_sb", [128, 512], F32)[:, :]
    rows_ap = kb.sbuf("rows_sb", [1, 1536], F32)[:, :]
    cm = Common(kb, with_stats=True, ln_bufs=(sq_ap, lntmp, rows_ap))

    rw_in = kb.din("routerw", [DM, 8], F32)
    w1_in = kb.din("moew1", [8, DM, 4096], F32)
    b1_in = kb.din("moeb1", [128, 8 * 32], F32)
    w2_in = kb.din("moew2", [8, 4096, DM], F32)
    b2_in = kb.din("moeb2", [128, 8 * 8], F32)
    presel_in = kb.din("presel", [8, 1], F32)
    cumt_in = kb.din("cumt", [128, 128], F32)
    e8_in = kb.din("e8", [128, 8], F32)
    ident_in = kb.din("identb", [128, 128], BF16)
    identf_in = kb.din("identf", [128, 128], F32)

    G_d = kb.dtmp("G_d", [EROWS, DM], BF16)
    Go_d = kb.dtmp("Go_d", [EROWS, DM], BF16)
    cc_in = kb.dtmp("cc_in", [1, 8], F32)
    cc_out = kb.dtmp("cc_out", [8, 8], F32)

    h_sb = kb.sbuf("h_sb", [128, 32 * KCAP], BF16)
    xntm = kb.sbuf("xntm_sb", [128, DM], BF16)
    moetm = kb.sbuf("moetm_sb", [128, DM], BF16)
    gtm = kb.sbuf("gtm_sb", [128, DM], BF16)
    slab1 = kb.sbuf("slab1_sb", [128, 2 * 8 * 128], BF16)
    slab2 = kb.sbuf("slab2_sb", [128, 2 * 32 * 128], BF16)
    sv1 = slab1[:, :].rearrange("p (b kg j) -> p b kg j", b=2, kg=8)
    sv2 = slab2[:, :].rearrange("p (b kg j) -> p b kg j", b=2, kg=32)
    rw_sb = kb.sbuf("rw_sb", [128, 8 * 8], F32)
    sw_sb = kb.sbuf("sw_sb", [128, 8], F32)
    stat16 = kb.sbuf("stat16_sb", [16, 2 * 128], F32)
    mtm = kb.sbuf("mtm_sb", [128, 16], F32)
    rtm = kb.sbuf("rtm_sb", [128, 16], F32)
    logit = kb.sbuf("logit_sb", [128, 8], F32)
    emask = kb.sbuf("emask_sb", [128, 16 * 8], F32)
    cum = kb.sbuf("cum_sb", [128, 16 * 8], F32)
    gate = kb.sbuf("gate_sb", [128, 16], F32)
    gk = kb.sbuf("gk_sb", [128, 16], F32)
    growi = kb.sbuf("growi_sb", [128, 16], I32)
    runrow = kb.sbuf("runrow_sb", [1, 16], F32)
    ag_sb = kb.sbuf("ag_sb", [8, 8], F32)
    offs128 = kb.sbuf("offs_sb", [128, 8], F32)
    scr = kb.sbuf("scr_sb", [128, 8 * 4], F32)
    cumt = kb.sbuf("cumt_sb", [128, 128], F32)
    e8 = kb.sbuf("e8_sb", [128, 8], F32)
    identb = kb.sbuf("identb_sb", [128, 128], BF16)
    identf = kb.sbuf("identf_sb", [128, 128], F32)
    presel = kb.sbuf("presel_sb", [8, 1], F32)
    b1_sb = kb.sbuf("b1_sb", [128, 8 * 32], F32)
    b2_sb = kb.sbuf("b2_sb", [128, 8 * 8], F32)

    ps_a = kb.psum("ps_a", [128, 512])
    ps_b = kb.psum("ps_b", [128, 512])
    ps_c = kb.psum("ps_c", [128, 512])
    ps_tr = kb.psum("ps_tr", [128, 1024], BF16)

    cm.load_x()
    consts = {}
    rwload = (rw_sb[:].rearrange("p (kg e) -> p kg e", e=8),
              rw_in.rearrange("(kg p) e -> p kg e", p=128))
    for name, dst, src in [("rw", *rwload),
                           ("cumt", cumt, cumt_in), ("e8", e8, e8_in),
                           ("identb", identb, ident_in), ("identf", identf, identf_in),
                           ("presel", presel, presel_in), ("b1", b1_sb, b1_in),
                           ("b2", b2_sb, b2_in)]:
        consts[name] = kb.dma("scalar", dst if name == "rw" else dst[:], src,
                              chan="c_" + name)
    zz = kb.memset("vector", ag_sb[:], 0.0)
    rwv = rw_sb[:].rearrange("p (kg e) -> p kg e", e=8)

    gz = kb.memset("vector", gtm[:], 0.0)
    gzt = []
    for st in range(EROWS // 128):
        gzt.append(kb.dma("sync", G_d[st * 128:(st + 1) * 128, :], gtm[:],
                          chan="gz", waits=[gz]))

    cm.emit_ln()

    swm = None
    for kg in range(NG):
        swm = kb.mm(ps_c[0:1, 0:8], cm.onecol, rwv[:, kg], kg == 0, kg == NG - 1,
                    waits=[consts["rw"], cm.ones_loaded[0]] if kg == 0 else ())
    swr = kb.copy("vector", runrow[0:1, 8:16], ps_c[0:1, 0:8], waits=[swm])
    swb = kb.mm(ps_c[:, 0:8], cm.onerow, runrow[0:1, 8:16], True, True,
                waits=[swr, cm.ones_loaded[1]])
    swe = kb.copy("vector", sw_sb[:], ps_c[:, 0:8], waits=[swb])

    stl = kb.dma("sync", stat16[:].rearrange("t (h p) -> t h p", h=2),
                 cm.stat_d.rearrange("o (h tt p) -> (o tt) h p", p=128, h=2),
                 chan="statl", waits=cm.stat_w)
    t_m = kb.do("tensor", lambda e, o=ps_a[:, 0:16], i=stat16[0:16, 0:128],
                idn=identf[0:16, 0:16]: e.transpose(o, i, idn),
                waits=[stl, consts["identf"]])
    ev_m = kb.copy("vector", mtm[:], ps_a[:, 0:16], waits=[t_m])
    t_r = kb.do("tensor", lambda e, o=ps_a[:, 16:32], i=stat16[0:16, 128:256],
                idn=identf[0:16, 0:16]: e.transpose(o, i, idn),
                waits=[stl, ev_m])
    ev_r = kb.copy("vector", rtm[:], ps_a[:, 16:32], waits=[t_r])

    zrun = kb.memset("vector", runrow[0:1, 0:8], 0.0)
    emv = emask[:].rearrange("p (tt e) -> p tt e", e=8)
    cuv = cum[:].rearrange("p (tt e) -> p tt e", e=8)
    rtoks = []
    run_tok = zrun
    prev_ps = None
    for tt_ in range(16):
        mml = None
        for kg in range(NG):
            mml = kb.mm(ps_b[:, 0:8], cm.x[:, kg * T + tt_ * 128:kg * T + (tt_ + 1) * 128],
                        rwv[:, kg], kg == 0, kg == NG - 1,
                        waits=[consts["rw"], cm.ln_done, prev_ps] if kg == 0 else ())
        s1 = kb.ts("vector", scr[:, 0:8], sw_sb[:], mtm[:, tt_:tt_ + 1], None,
                   ALU.mult, waits=[swe, ev_m, state_none(rtoks)])
        s2 = kb.tt("vector", scr[:, 0:8], ps_b[:, 0:8], scr[:, 0:8], ALU.subtract,
                   waits=[mml, s1])
        prev_ps = s2
        lg = kb.ts("vector", logit[:], scr[:, 0:8], rtm[:, tt_:tt_ + 1], None,
                   ALU.mult, waits=[s2, ev_r, rtoks[-1][0] if rtoks else None])
        mx = kb.red(scr[:, 8:9], logit[:], ALU.max, waits=[lg])
        ng = kb.ts("vector", scr[:, 9:10], scr[:, 8:9], -1.0, waits=[mx])
        pe_ = kb.act(scr[:, 16:24], logit[:], AF.Exp, bias=scr[:, 9:10],
                     accum=scr[:, 10:11], waits=[ng])
        rc = kb.do("vector", lambda e, o=gate[:, tt_:tt_ + 1], i=scr[:, 10:11]:
                   e.reciprocal(o, i), waits=[pe_])
        em = kb.ts("vector", emv[:, tt_], logit[:], scr[:, 8:9], None, ALU.is_ge,
                   waits=[mx, lg])
        c1 = kb.mm(ps_c[:, 0:8], cumt[:], emv[:, tt_], True, False,
                   waits=[em, consts["cumt"], prev_cum_ev(rtoks), run_tok, swe])
        c2 = kb.mm(ps_c[:, 0:8], cm.onerow, runrow[0:1, 0:8], False, True,
                   waits=[run_tok])
        cv = kb.copy("vector", cuv[:, tt_], ps_c[:, 0:8], waits=[c2])
        c3 = kb.mm(ps_c[0:1, 16:24], cm.onecol, emv[:, tt_], True, True,
                   waits=[em, cv])
        tr_ = kb.copy("vector", runrow[0:1, 8:16], ps_c[0:1, 16:24], waits=[c3, swe])
        run_tok = kb.tt("vector", runrow[0:1, 0:8], runrow[0:1, 0:8],
                        runrow[0:1, 8:16], ALU.add, waits=[tr_])
        rtoks.append((em, cv))
    ccw = kb.dma("gpsimd", cc_in, runrow[0:1, 0:8], chan="ccw", waits=[run_tok])
    groups = [[0]] if sim1 else [list(range(NCORES))]
    ccsem = kb.sem("C_cc")
    kb.cnt["C_cc"] = kb.cnt.get("C_cc", 0)

    cc_out_ap = cc_out[0:1, :] if sim1 else cc_out

    def ccfn(e, g=groups, sem=kb.semh["C_cc"], oap=cc_out_ap):
        e.collective_compute("AllGather", ALU.bypass, replica_groups=g,
                             ins=[cc_in], outs=[oap]).then_inc(sem, 1)
    kb.do("gpsimd", ccfn, waits=[ccw])
    cctok = ("C_cc", 1)
    agl = kb.dma("gpsimd", ag_sb[0:1 if sim1 else 8, :],
                 cc_out[0:1 if sim1 else 8, :], chan="agl", waits=[cctok, zz])
    of1 = kb.mm(ps_c[0:1, 0:8], presel[:], ag_sb[:], True, True,
                waits=[agl, consts["presel"], run_tok])
    of2 = kb.copy("vector", runrow[0:1, 8:16], ps_c[0:1, 0:8], waits=[of1, run_tok])
    of3 = kb.mm(ps_c[:, 0:8], cm.onerow, runrow[0:1, 8:16], True, True, waits=[of2])
    of4 = kb.copy("vector", offs128[:], ps_c[:, 0:8], waits=[of3])

    gtoks = []
    for tt_ in range(16):
        g1 = kb.tt("vector", scr[:, 0:8], cuv[:, tt_], offs128[:], ALU.add,
                   waits=[of4, rtoks[tt_][1]] + ([gtoks[-1]] if gtoks else []))
        g2 = kb.tt("vector", scr[:, 0:8], scr[:, 0:8], emv[:, tt_], ALU.mult,
                   waits=[g1])
        g3 = kb.red(scr[:, 8:9], scr[:, 0:8], ALU.add, waits=[g2])
        g4 = kb.ts("vector", scr[:, 9:10], scr[:, 8:9], float(T * NCORES // 8), None,
                   ALU.is_le, waits=[g3])
        g5 = kb.tt("vector", scr[:, 0:8], cuv[:, tt_], emv[:, tt_], ALU.mult,
                   waits=[g2])
        g6 = kb.red(scr[:, 10:11], scr[:, 0:8], ALU.add, waits=[g5])
        g7 = kb.ts("vector", scr[:, 11:12], scr[:, 10:11], -1.0, None, ALU.add,
                   waits=[g6])
        g8 = kb.ts("vector", scr[:, 12:13], scr[:, 11:12], float(KCAP - 1), None,
                   ALU.is_le, waits=[g7])
        g9 = kb.tt("vector", scr[:, 0:8], emv[:, tt_], e8[:], ALU.mult,
                   waits=[g5, consts["e8"]])
        g10 = kb.red(scr[:, 13:14], scr[:, 0:8], ALU.add, waits=[g9])
        g11 = kb.stt("vector", scr[:, 14:15], scr[:, 13:14], float(KCAP),
                     scr[:, 11:12], ALU.mult, ALU.add, waits=[g10, g7])
        g12 = kb.ts("vector", scr[:, 15:16], scr[:, 12:13], -1e6, 1e6,
                    ALU.mult, ALU.add, waits=[g8])
        g13 = kb.tt("vector", scr[:, 14:15], scr[:, 14:15], scr[:, 15:16],
                    ALU.add, waits=[g11, g12])
        g14 = kb.copy("vector", growi[:, tt_:tt_ + 1], scr[:, 14:15], waits=[g13])
        g15 = kb.tt("vector", gk[:, tt_:tt_ + 1], gate[:, tt_:tt_ + 1],
                    scr[:, 9:10], ALU.mult, waits=[g4])
        g16 = kb.ts("vector", gk[:, tt_:tt_ + 1], gk[:, tt_:tt_ + 1], 0.5,
                    waits=[g15])
        gtoks.append(g14)
        gtoks.append(g16)

    tr_guard = None
    sc_toks = []
    prev_sc = None
    for tt_ in range(16):
        evs = []
        for kg in range(NG):
            trx = kb.do("tensor", lambda e, o=ps_tr[:, 0:128],
                        i=cm.xn[:, kg * T + tt_ * 128:kg * T + (tt_ + 1) * 128],
                        idn=identb[:]: e.transpose(o, i, idn),
                        waits=[cm.ln_done, consts["identb"], tr_guard])
            ev = kb.copy("scalar", xntm[:, kg * 128:(kg + 1) * 128], ps_tr[:, 0:128],
                         waits=[trx, prev_sc])
            tr_guard = ev
            evs.append(ev)
        sc = kb.idma(G_d, growi[:, tt_:tt_ + 1], xntm[:], None, chan="scat",
                     bounds=EROWS - 1,
                     waits=evs + [gtoks[2 * tt_]] + gzt)
        prev_sc = sc
        sc_toks.append(sc)

    gl_toks = []
    gfm_evs = []
    for st in range(EROWS // 128):
        gl = kb.dma("sync", gtm[:], G_d[st * 128:(st + 1) * 128, :], chan="gl",
                    waits=sc_toks + [gz] + ([gfm_evs[-1]] if gfm_evs else []))
        for kg in range(NG):
            trx = kb.do("tensor", lambda e, o=ps_tr[:, 0:128],
                        i=gtm[:, kg * 128:(kg + 1) * 128], idn=identb[:]:
                        e.transpose(o, i, idn), waits=[gl, tr_guard])
            ev = kb.copy("scalar", Gfm[:, kg * EROWS + st * 128:kg * EROWS + (st + 1) * 128],
                         ps_tr[:, 0:128], waits=[trx])
            tr_guard = ev
            gfm_evs.append(ev)
    gfm_done = gfm_evs[-1]

    w1v = [w1_in[e].rearrange("(kg p) f -> p kg f", p=128) for e in range(8)]
    w2v = [w2_in[e].rearrange("(kg p) f -> p kg f", p=128) for e in range(8)]
    slab_guard = {("1", 0): None, ("1", 1): None, ("2", 0): None, ("2", 1): None}
    bank_guard = {0: None, 1: None}
    hv = h_sb[:].rearrange("p (kg j) -> p kg j", j=KCAP)
    out_evs = []
    for e_ in range(8):
        esl = slice(e_ * KCAP, (e_ + 1) * KCAP)
        h_toks = []
        for m in range(32):
            sb = m % 2
            ld = kb.dma("gpsimd", sv1[:, sb], w1v[e_][:, :, m * 128:(m + 1) * 128],
                        chan=f"m1{sb}", waits=[slab_guard[("1", sb)]])
            ps = (ps_a, ps_b)[m % 2]
            mml = None
            for kg in range(NG):
                mml = kb.mm(ps[:, 0:KCAP], sv1[:, sb, kg],
                            Gfm[:, kg * EROWS + e_ * KCAP:kg * EROWS + (e_ + 1) * KCAP],
                            kg == 0, kg == NG - 1,
                            waits=[ld, gfm_done, bank_guard[m % 2]] +
                                  (out_evs if m == 0 and kg == 0 else [])
                            if kg == 0 else ())
            slab_guard[("1", sb)] = mml
            ev = kb.act(hv[:, m], ps[:, 0:KCAP], AF.Relu,
                        bias=b1_sb[:, e_ * 32 + m:e_ * 32 + m + 1],
                        waits=[mml, consts["b1"], state_h_free(out_evs, h_toks)])
            bank_guard[m % 2] = ev
            h_toks.append(ev)
        out_evs = []
        for m in range(8):
            sb = m % 2
            ld = kb.dma("gpsimd", sv2[:, sb], w2v[e_][:, :, m * 128:(m + 1) * 128],
                        chan=f"m2{sb}", waits=[slab_guard[("2", sb)]])
            ps = (ps_a, ps_b)[m % 2]
            mml = None
            for kg in range(32):
                mml = kb.mm(ps[:, 0:KCAP], sv2[:, sb, kg], hv[:, kg],
                            kg == 0, kg == 31,
                            waits=([ld, bank_guard[m % 2]] + h_toks)
                            if kg == 0 else ())
            slab_guard[("2", sb)] = mml
            ev = kb.ts("vector", Gfm[:, m * EROWS + e_ * KCAP:m * EROWS + (e_ + 1) * KCAP],
                       ps[:, 0:KCAP], b2_sb[:, e_ * 8 + m:e_ * 8 + m + 1], None,
                       ALU.add, waits=[mml, consts["b2"], h_toks[-1]])
            bank_guard[m % 2] = ev
            out_evs.append(ev)

    go_st = []
    for st in range(EROWS // 128):
        evs = []
        for kg in range(NG):
            trx = kb.do("tensor", lambda e, o=ps_tr[:, 0:128],
                        i=Gfm[:, kg * EROWS + st * 128:kg * EROWS + (st + 1) * 128],
                        idn=identb[:]: e.transpose(o, i, idn),
                        waits=[tr_guard] + out_evs)
            ev = kb.copy("scalar", gtm[:, kg * 128:(kg + 1) * 128], ps_tr[:, 0:128],
                         waits=[trx, go_st[-1] if go_st else None])
            tr_guard = ev
            evs.append(ev)
        go_st.append(kb.dma("sync", Go_d[st * 128:(st + 1) * 128, :], gtm[:],
                            chan="gost", waits=evs))

    prev_add = None
    for tt_ in range(16):
        mz = kb.memset("vector", moetm[:], 0.0, waits=[prev_add])
        ga = kb.idma(moetm[:], None, Go_d, growi[:, tt_:tt_ + 1], chan="gab",
                     bounds=EROWS - 1, waits=[mz] + go_st + [gtoks[2 * tt_]])
        gm = kb.ts("vector", moetm[:], moetm[:], gk[:, tt_:tt_ + 1], None,
                   ALU.mult, waits=[ga, gtoks[2 * tt_ + 1]])
        for kg in range(NG):
            trx = kb.do("tensor", lambda e, o=ps_tr[:, 0:128],
                        i=moetm[:, kg * 128:(kg + 1) * 128], idn=identb[:]:
                        e.transpose(o, i, idn), waits=[gm, tr_guard])
            xg = cm.x[:, kg * T + tt_ * 128:kg * T + (tt_ + 1) * 128]
            prev_add = kb.tt("vector", xg, xg, ps_tr[:, 0:128], ALU.add,
                             waits=[trx])
            tr_guard = prev_add
    cm.store_x([kb.val("V")])
    return kb, cm


def state_none(rtoks):
    return rtoks[-1][0] if rtoks else None


def prev_cum_ev(rtoks):
    return rtoks[-1][1] if rtoks else None


def state_h_free(out_evs, h_toks):
    return h_toks[-1] if h_toks else (out_evs[-1] if out_evs else None)


def moe_inputs(inp, core_id):
    presel = np.zeros((8, 1), np.float32)
    presel[:core_id] = 1.0
    b1 = np.concatenate([colmaj(inp["L1_ffn_b1"][e]) for e in range(8)], axis=1)
    b2 = np.concatenate([colmaj(inp["L1_ffn_b2"][e]) for e in range(8)], axis=1)
    return {
        "routerw": np.asarray(inp["L1_ffn_router_w"], np.float32),
        "moew1": np.asarray(inp["L1_ffn_w1"], np.float32),
        "moeb1": b1,
        "moew2": np.asarray(inp["L1_ffn_w2"], np.float32),
        "moeb2": b2,
        "presel": presel,
        "cumt": np.triu(np.ones((128, 128), np.float32)),
        "e8": np.broadcast_to(np.arange(8, dtype=np.float32), (128, 8)).copy(),
        "identb": np_bf16(np.eye(128)),
        "identf": np.eye(128, dtype=np.float32),
        "onecol": np.ones((128, 1), np.float32),
        "onerow": np.ones((1, 128), np.float32),
    }


def run_moe(x_cores, inp):
    kb, nc = get_prog("moe")
    in_maps = [dict(moe_inputs(inp, c), x_in=x_cores[c]) for c in range(NCORES)]
    res = run_bass_kernel_spmd(nc, in_maps, core_ids=list(range(NCORES)))
    return [r["x_out"] for r in res.results]



_cache = {}


def get_prog(name, **kw):
    key = (name, tuple(sorted(kw.items())))
    if key not in _cache:
        kb, cm = {"ffn0": build_ffn0, "mamba": build_mamba, "attn": build_attn, "moe": build_moe}[name.split(":")[0]](**kw)
        nc = kb.build()
        _cache[key] = (kb, nc)
    return _cache[key]


def ffn0_inputs(inp):
    return {
        "w1t": tr(inp["L0_ffn_W1"]),
        "w2t": tr(inp["L0_ffn_W2"]),
        "b1c": colmaj(inp["L0_ffn_b1"]),
        "b2c": colmaj(inp["L0_ffn_b2"]),
        "onecol": np.ones((128, 1), np.float32),
        "onerow": np.ones((1, 128), np.float32),
    }


def run_ffn0(x_cores, inp):
    kb, nc = get_prog("ffn0")
    base = ffn0_inputs(inp)
    in_maps = [dict(base, x_in=x) for x in x_cores]
    res = run_bass_kernel_spmd(nc, in_maps, core_ids=list(range(NCORES)))
    return [r["x_out"] for r in res.results]






def kernel(**inputs):
    x = np.asarray(inputs["L0_attn_x"], np.float32)
    xs = [np.ascontiguousarray(x[c].T) for c in range(NCORES)]
    xs = run_ffn0(xs, inputs)
    xs = run_mamba(xs, inputs)
    xs = run_ffn0(xs, inputs)
    xs = run_moe(xs, inputs)
    xs = run_attn(xs, inputs)
    xs = run_moe(xs, inputs)
    return np.ascontiguousarray(np.stack([np.asarray(o).T for o in xs])).astype(np.float32)


# version 05
# speedup vs baseline: 1.2028x; 1.2028x over previous
import sys
sys.path.insert(0, "/opt/trn_rl_repo")
import contextlib
import numpy as np
import ml_dtypes
import concourse.bass as bass
import concourse.mybir as mybir
from concourse.bass import IndirectOffsetOnAxis
from concourse.bass_utils import run_bass_kernel_spmd

F32 = mybir.dt.float32
BF16 = mybir.dt.bfloat16
I32 = mybir.dt.int32
AF = mybir.ActivationFunctionType
ALU = mybir.AluOpType
AX = mybir.AxisListType

T = 2048
DM = 1024
NG = DM // 128
NTT = T // 512
NCORES = 8
KCAP = 352
EROWS = 8 * KCAP

ENGINES = ("sync", "tensor", "vector", "scalar", "gpsimd")
QSEM = {"sync": "Ds", "gpsimd": "Dg", "scalar": "Da"}
MSEM = {"tensor": "T", "vector": "V", "scalar": "S", "gpsimd": "P", "sync": "Y"}


class KB:
    def __init__(self, name):
        self.name = name
        self.nc = bass.Bass("TRN2", target_bir_lowering=False)
        self.es = contextlib.ExitStack()
        self.streams = {e: [] for e in ENGINES}
        self.semh = {}
        self.cnt = {}
        self.needed = {}
        self._sticky = {}
        self._stickyc = {}
        self.dma_uses = {}

    def sem(self, name):
        if name not in self.semh:
            self.semh[name] = self.es.enter_context(self.nc.semaphore(name))
            self.cnt[name] = 0
        return name

    def sbuf(self, name, shape, dt):
        return self.es.enter_context(self.nc.sbuf_tensor(name, shape, dt))

    def psum(self, name, shape, dt=F32):
        return self.es.enter_context(self.nc.psum_tensor(name, shape, dt))

    def din(self, name, shape, dt=F32):
        return self.nc.dram_tensor(name, shape, dt, kind="ExternalInput").ap()

    def dout(self, name, shape, dt=F32):
        return self.nc.dram_tensor(name, shape, dt, kind="ExternalOutput").ap()

    def dtmp(self, name, shape, dt):
        return self.nc.dram_tensor(name, shape, dt).ap()

    notrust = False

    def do(self, eng, fn, *, waits=()):
        seq = len(self.streams[eng]) + 1
        self._register_waits(waits)
        self.streams[eng].append({"fn": fn, "waits": tuple(w for w in waits if w),
                                  "nt": self.notrust})
        return ("@" + eng, seq)

    def dma(self, q, out, in_, *, chan, waits=(), **kw):
        u = self.dma_uses[chan] = self.dma_uses.get(chan, 0) + 1
        sname = "C_" + chan
        self.sem(sname)
        tgt = (sname, 16 * u)

        def fn(e, o=out, i=in_, kw=kw, sem=self.semh[sname]):
            e.dma_start(out=o, in_=i, **kw).then_inc(sem, 16)
        self.do(q, fn, waits=waits)
        return tgt

    def _breg(self, v):
        if not hasattr(self, "_bregs"):
            self._bregs = {}
        if v not in self._bregs:
            slot = {}
            self._bregs[v] = slot

            def prep(e, v=v, slot=slot):
                slot["reg"] = e.to_reg(v)
            self.streams["gpsimd"].append({"fn": None, "raw": prep, "waits": ()})
        return self._bregs[v]

    def idma(self, out, out_idx, in_, in_idx, *, chan, bounds=None, waits=()):
        u = self.dma_uses[chan] = self.dma_uses.get(chan, 0) + 1
        sname = "C_" + chan
        self.sem(sname)
        tgt = (sname, 16 * u)
        slot = self._breg(bounds) if bounds is not None else None

        def fn(e, o=out, oi=out_idx, i=in_, ii=in_idx, s=slot,
               sem=self.semh[sname]):
            oo = IndirectOffsetOnAxis(oi, 0) if oi is not None else None
            io = IndirectOffsetOnAxis(ii, 0) if ii is not None else None
            e.indirect_dma_start(o, oo, i, io,
                                 bounds_check=None if s is None else s["reg"],
                                 oob_is_err=False).then_inc(sem, 16)
        self.do("gpsimd", fn, waits=waits)
        return tgt

    def _register_waits(self, waits):
        for w in waits:
            if w is None:
                continue
            s, v = w
            if s.startswith("@"):
                self.needed.setdefault(s[1:], set()).add(v)
            else:
                self.sem(s)

    def wait(self, eng, waits):
        self._register_waits(waits)
        self.streams[eng].append({"fn": None, "waits": tuple(w for w in waits if w)})

    def val(self, eng):
        eng = {v: k for k, v in MSEM.items()}.get(eng, eng)
        n = len(self.streams[eng])
        if n == 0:
            return None
        return ("@" + eng, n)

    @staticmethod
    def _in_psum(*aps):
        for a in aps:
            try:
                if "PSUM" in str(a.space).upper() or "Psum" in str(a.space):
                    return True
            except Exception:
                pass
        return False

    def psticky(self, tok, *aps):
        return self.sticky(tok) if self._in_psum(*aps) else tok

    def mm(self, ps, lhsT, rhs, start, stop, *, waits=()):
        return self.do("tensor",
                       lambda e, o=ps, l=lhsT, r=rhs, s1=start, s2=stop:
                       e.matmul(o, l, r, start=s1, stop=s2), waits=waits)

    def tt(self, eng, out, a, b, op, *, waits=()):
        return self.psticky(self.do(eng, lambda e, o=out, x=a, y=b, p=op:
                            e.tensor_tensor(o, x, y, p), waits=waits), a, b)

    def ts(self, eng, out, a, s1, s2=None, op0=ALU.mult, op1=None, *, waits=()):
        if s2 is None:
            return self.psticky(self.do(eng, lambda e, o=out, x=a, v=s1, p=op0:
                                e.tensor_scalar(o, x, v, None, p), waits=waits), a)
        return self.psticky(self.do(
            eng, lambda e, o=out, x=a, v1=s1, v2=s2, p0=op0, p1=op1:
            e.tensor_scalar(o, x, v1, v2, p0, p1), waits=waits), a)

    def stt(self, eng, out, in0, scalar, in1, op0, op1, *, waits=()):
        return self.psticky(self.do(
            eng, lambda e, o=out, a=in0, s=scalar, b=in1, p0=op0, p1=op1:
            e.scalar_tensor_tensor(o, a, s, b, p0, p1), waits=waits), in0, in1)

    def act(self, out, in_, func, *, bias=None, scale=None, accum=None, waits=()):
        def fn(e, o=out, i=in_, f=func, b=bias, s=scale, ac=accum):
            kw = {}
            if b is not None:
                kw["bias"] = b
            if s is not None:
                kw["scale"] = s
            if ac is not None:
                kw["accum_out"] = ac
            return e.activation(o, i, f, **kw)
        return self.do("scalar", fn, waits=waits)

    def copy(self, eng, out, in_, *, waits=()):
        if eng == "scalar":
            return self.do(eng, lambda e, o=out, i=in_: e.copy(o, i), waits=waits)
        tok = self.do(eng, lambda e, o=out, i=in_: e.tensor_copy(o, i), waits=waits)
        try:
            cast = out.dtype != in_.dtype
        except Exception:
            cast = True
        return self.sticky(tok) if (cast or self._in_psum(in_)) else tok

    def sticky(self, tok):
        if tok is not None and tok[0].startswith("@"):
            eng = tok[0][1:]
            self._sticky.setdefault(eng, set()).add(tok[1])
            self._stickyc.setdefault(eng, set()).add(tok[1])
        return tok

    def red(self, out, in_, op, *, absv=None, waits=()):
        return self.sticky(self.do(
            "vector", lambda e, o=out, i=in_, p=op, a=absv:
            e.tensor_reduce(o, i, AX.X, p, apply_absolute_value=a), waits=waits))

    def memset(self, eng, ap, v=0.0, *, waits=()):
        return self.do(eng, lambda e, a=ap, vv=v: e.memset(a, vv), waits=waits)

    def barrier(self):
        marks = [self.do(e, lambda en: en.nop()) for e in ENGINES]
        for e in ENGINES:
            self.wait(e, marks)

    TRUST = ("vector",)

    def build(self):
        needed = {}
        for eng in ENGINES:
            stick = self._sticky.get(eng, set())
            stickc = self._stickyc.get(eng, set())
            for idx, entry in enumerate(self.streams[eng], 1):
                consumer_sticky = idx in stickc
                rw = []
                for w in entry["waits"]:
                    s, v = w
                    if s.startswith("@"):
                        e2 = s[1:]
                        if (e2 == eng and eng in self.TRUST and v not in stick
                                and not consumer_sticky and not entry.get("nt")):
                            continue
                        needed.setdefault(e2, set()).add(v)
                    rw.append(w)
                entry["_rw"] = tuple(rw)
        self.needed = needed
        values = {}
        for eng in ENGINES:
            need = sorted(self.needed.get(eng, ()))
            values[eng] = {s: i + 1 for i, s in enumerate(need)}
            if need:
                self.sem(MSEM[eng])

        def resolve(w):
            s, v = w
            if s.startswith("@"):
                e2 = s[1:]
                return (MSEM[e2], values[e2][v])
            return (s, v)

        def body(e, eng):
            vals = values[eng]
            msem = self.semh.get(MSEM[eng])
            for idx, entry in enumerate(self.streams[eng], 1):
                merged = {}
                for w in entry["_rw"]:
                    sn, v = resolve(w)
                    if v > merged.get(sn, -1):
                        merged[sn] = v
                for sn, v in merged.items():
                    e.wait_ge(self.semh[sn], v)
                if entry.get("raw") is not None:
                    entry["raw"](e)
                elif entry["fn"] is not None:
                    entry["fn"](e)
                if idx in vals:
                    e.drain(fusable=True).then_inc(msem, 1)

        with self.nc.Block() as block:
            for eng in ENGINES:
                getattr(block, eng)(lambda e, eng=eng: body(e, eng))
        self.nc.finalize()
        return self.nc



def np_bf16(x):
    return np.asarray(x).astype(ml_dtypes.bfloat16)


def colmaj(b, p=128):
    b = np.asarray(b, np.float32)
    return np.ascontiguousarray(b.reshape(-1, p).T)


def tr(a):
    return np.ascontiguousarray(np.asarray(a, np.float32).T)



ARENA_NB = 143 * 1024


class Arena:

    def __init__(self, handle, nb=ARENA_NB):
        self.h = handle
        self.nb = nb
        self.off = 0

    def take(self, n, dt, rows=128):
        sz = n * mybir.dt.size(dt)
        off = (self.off + 63) & ~63
        self.off = off + sz
        assert self.off <= self.nb, f"arena overflow: {self.off} > {self.nb}"
        return self.h[0:rows, off:off + sz].bitcast(dt)

    def take_raw(self, nbytes):
        off = (self.off + 63) & ~63
        self.off = off + nbytes
        assert self.off <= self.nb, f"arena overflow: {self.off} > {self.nb}"
        return (self.h, off)

    def reset(self):
        self.off = 0


class Common:

    def __init__(self, kb):
        self.kb = kb
        self.x_in = kb.din("x_in", [DM, T], F32)
        self.x_out = kb.dout("x_out", [DM, T], F32)
        self.onecol_in = kb.din("onecol", [128, 1], F32)
        self.onerow_in = kb.din("onerow", [1, 128], F32)
        self.x = kb.sbuf("x_sb", [128, NG * T], F32)
        self.onecol = kb.sbuf("onecol_sb", [128, 1], F32)[:, :]
        self.onerow = kb.sbuf("onerow_sb", [1, 128], F32)[:, :]
        self.stat_d = kb.dtmp("stat_d", [1, 2 * T], F32)
        self.with_stats = False

    def set_ln(self, ar, *, xn=True, with_stats=False):
        self.sq = ar.take(NG * 512, F32)
        self.ln_tmp = ar.take(512, F32)
        self.rows = ar.take(3 * 512, F32, rows=1)
        if xn:
            self.xn = ar.take(NG * T, BF16)
        self.with_stats = with_stats
        self._ln_prev = None
        self.stat_w = None

    def set_ln_custom(self, sq, ln_tmp, rows, with_stats=False):
        self.sq, self.ln_tmp, self.rows = sq, ln_tmp, rows
        self.with_stats = with_stats
        self._ln_prev = None
        self.stat_w = None

    def load_x(self):
        kb = self.kb
        self.x_loaded = []
        for g in range(NG):
            self.x_loaded.append(
                kb.dma("sync", self.x[:, g * T:(g + 1) * T],
                       self.x_in[g * 128:(g + 1) * 128, :], chan=f"x{g}"))
        self.ones_loaded = [
            kb.dma("scalar", self.onecol[:], self.onecol_in, chan="one_c"),
            kb.dma("scalar", self.onerow[:], self.onerow_in, chan="one_r"),
        ]

    def store_x(self, waits):
        kb = self.kb
        outs = []
        for g in range(NG):
            outs.append(kb.dma("sync", self.x_out[g * 128:(g + 1) * 128, :],
                               self.x[:, g * T:(g + 1) * T], chan="xo",
                               waits=waits if g == 0 else ()))
        kb.wait("sync", outs)

    def emit_ln_tile(self, t, xn_of, extra=()):
        kb = self.kb
        mrow = self.rows[0:1, 0:512]
        rrow = self.rows[0:1, 512:1024]
        trow = self.rows[0:1, 1024:1536]
        prev = getattr(self, "_ln_prev", None)
        mm_last = None
        for g in range(NG):
            xg = self.x[:, g * T + t * 512: g * T + (t + 1) * 512]
            sqg = self.sq[:, g * 512:(g + 1) * 512]
            w = [self.x_loaded[g]] if t == 0 else []
            if g == 0:
                w += [prev] + list(extra)
            sd = kb.act(sqg, xg, AF.Square, waits=w)
            kb.mm(self.ps_s1[0:1, :], self.onecol, xg, g == 0, g == NG - 1,
                  waits=([self.x_loaded[g]] if t == 0 else []) +
                        ([self.ones_loaded[0]] if (t == 0 and g == 0) else []) +
                        ([prev] + list(extra) if g == 0 else []))
            mm_last = kb.mm(self.ps_s2[0:1, :], self.onecol, sqg, g == 0, g == NG - 1,
                            waits=[sd])
        msqrow = self.ln_tmp[0:1, 0:512]
        stat_prev = getattr(self, "stat_w", None) if self.with_stats else None
        m1 = kb.ts("vector", mrow, self.ps_s1[0:1, :], 1.0 / DM,
                   waits=[mm_last] + (stat_prev or []))
        t1 = kb.ts("vector", trow, self.ps_s2[0:1, :], 1.0 / DM)
        sqm = kb.act(msqrow, mrow, AF.Square, waits=[m1])
        vv = kb.tt("vector", trow, trow, msqrow, ALU.subtract, waits=[sqm, t1])
        vv = kb.ts("vector", trow, trow, 1e-5, None, ALU.add, waits=[vv])
        l1 = kb.act(trow, trow, AF.Ln, waits=[vv])
        l2 = kb.act(rrow, trow, AF.Exp, scale=-0.5, waits=[l1])
        bm = kb.mm(self.ps_mb[:], self.onerow, mrow, True, True,
                   waits=[m1, kb.val("V"), self.ones_loaded[1]])
        br = kb.mm(self.ps_rb[:], self.onerow, rrow, True, True, waits=[l2])
        if self.with_stats:
            self.stat_w = [
                kb.dma("gpsimd", self.stat_d[0:1, t * 512:(t + 1) * 512], mrow,
                       chan="stat", waits=[m1]),
                kb.dma("gpsimd", self.stat_d[0:1, T + t * 512:T + (t + 1) * 512],
                       rrow, chan="stat", waits=[l2]),
            ]
        last = None
        for g in range(NG):
            xg = self.x[:, g * T + t * 512: g * T + (t + 1) * 512]
            s = kb.tt("vector", self.ln_tmp[:], xg, self.ps_mb[:], ALU.subtract,
                      waits=[bm, br] + ([last] if last else []))
            last = kb.tt("vector", xn_of(g), self.ln_tmp[:], self.ps_rb[:], ALU.mult,
                         waits=[s])
        self._ln_prev = last
        return last

    def emit_ln(self):
        for t in range(NTT):
            self.emit_ln_tile(
                t, lambda g, t=t: self.xn[:, g * T + t * 512: g * T + (t + 1) * 512])
        self.ln_done = self.kb.val("V")



def make_ctx(name):
    kb = KB(name)
    cm = Common(kb)
    ar = Arena(kb.sbuf("big_arena", [128, ARENA_NB], mybir.dt.uint8))
    ps = {
        "s1": kb.psum("ps_s1", [128, 512]),
        "s2": kb.psum("ps_s2", [128, 512]),
        "mb": kb.psum("ps_mb", [128, 512]),
        "rb": kb.psum("ps_rb", [128, 512]),
        "a": kb.psum("ps_a", [128, 512]),
        "b": kb.psum("ps_b", [128, 512]),
        "f1": kb.psum("ps_f1", [128, 512]),
        "f2": kb.psum("ps_f2", [128, 512]),
    }
    cm.ps_s1, cm.ps_s2 = ps["s1"], ps["s2"]
    cm.ps_mb, cm.ps_rb = ps["mb"], ps["rb"]
    return kb, cm, ar, ps


def emit_ffn0(kb, cm, ar, ps, quant=True):
    if not hasattr(cm, "ffn0_io"):
        cm.ffn0_io = (kb.din("w1t", [DM, 4096], F32), kb.din("w2t", [4096, DM], F32),
                      kb.din("b1c", [128, 32], F32), kb.din("b2c", [128, 8], F32),
                      kb.dtmp("w1q_d", [DM, 4096], BF16),
                      kb.dtmp("w2q_d", [4096, DM], BF16))
    w1t_in, w2t_in, b1_in, b2_in, w1q_d, w2q_d = cm.ffn0_io

    cm.set_ln(ar)
    wq = ar.take(4096, F32)
    sgn = ar.take(4096, BF16)
    msk = ar.take(4096, BF16)
    acc = ar.take(4, F32)
    rcol = ar.take(2, F32)
    h_sb = ar.take(32 * 512, BF16)
    qbf = h_sb
    slab1 = ar.take(2 * 8 * 128, BF16)
    slab2 = ar.take(2 * 32 * 128, BF16)
    b1_sb = ar.take(32, F32)
    b2h_sb = ar.take(8, F32)
    ps_a = ps["a"]
    ps_b = ps["b"]

    bl1 = kb.dma("scalar", b1_sb[:], b1_in, chan="b1")
    bl2 = kb.dma("scalar", b2h_sb[:], b2_in, chan="b2")
    b2h = kb.ts("vector", b2h_sb[:], b2h_sb[:], 0.5, waits=[bl2])

    cm.emit_ln()

    if not quant:
        q1done, q2done = cm.ffn0_qdone
    else:
        q1done, q2done = _emit_ffn0_quant(
            kb, cm, w1t_in, w2t_in, w1q_d, w2q_d, wq, sgn, msk, acc, rcol, qbf)
        cm.ffn0_qdone = (q1done, q2done)
    _emit_ffn0_mm(kb, cm, ps_a, ps_b, slab1, slab2, w1q_d, w2q_d, h_sb,
                  b1_sb, b2h_sb, bl1, b2h, q1done, q2done)


def _emit_ffn0_quant(kb, cm, w1t_in, w2t_in, w1q_d, w2q_d, wq, sgn, msk, acc,
                     rcol, qbf):
    z1 = kb.memset("vector", acc[:, 0:1], 0.0)
    z2 = kb.memset("vector", acc[:, 1:2], 0.0)

    def qpass1(w_in, nch, width, acc_col, zm, prev=None):
        for ch in range(nch):
            buf = wq[:, 0:width]
            ld = kb.dma("sync", buf, w_in[ch * 128:(ch + 1) * 128, :],
                        chan="wq", waits=[prev] if prev else [])
            col = rcol[:, 0:1]
            r = kb.red(col, buf, ALU.add, absv=True, waits=[ld, zm])
            prev = kb.tt("vector", acc_col, acc_col, col, ALU.add, waits=[r])
        return prev

    a1 = qpass1(w1t_in, 8, 4096, acc[:, 0:1], z1)
    a2 = qpass1(w2t_in, 32, 1024, acc[:, 1:2], z2, prev=a1)

    t1 = kb.mm(cm.ps_s1[0:1, 0:1], acc[:, 0:1], cm.onecol, True, True,
               waits=[a1, cm.ones_loaded[0]])
    t2 = kb.mm(cm.ps_s2[0:1, 0:1], acc[:, 1:2], cm.onecol, True, True, waits=[a2])
    r1 = kb.ts("vector", cm.rows[0:1, 0:1], cm.ps_s1[0:1, 0:1],
               0.5 / (4096 * DM), 0.5e-5, ALU.mult, ALU.add, waits=[t1])
    r2 = kb.ts("vector", cm.rows[0:1, 1:2], cm.ps_s2[0:1, 0:1],
               0.5 / (4096 * DM), 0.5e-5, ALU.mult, ALU.add, waits=[t2])
    bt1 = kb.mm(cm.ps_s1[:, 0:1], cm.onerow, cm.rows[0:1, 0:1], True, True,
                waits=[r1, cm.ones_loaded[1]])
    bt2 = kb.mm(cm.ps_s2[:, 0:1], cm.onerow, cm.rows[0:1, 1:2], True, True, waits=[r2])
    c1 = kb.copy("vector", acc[:, 2:3], cm.ps_s1[:, 0:1], waits=[bt1])
    c2 = kb.copy("vector", acc[:, 3:4], cm.ps_s2[:, 0:1], waits=[bt2])

    def qpass2(w_in, wq_d, nch, width, thr_col, thr_done, tag, prior=None):
        st = None
        sts = {0: prior[0] if prior else None, 1: prior[1] if prior else None}
        for ch in range(nch):
            buf = wq[:, 0:width]
            qb = qbf[:, (ch % 2) * 4096:(ch % 2) * 4096 + width]
            w = [kb.val("V"), kb.val("S")]
            ld = kb.dma("sync", buf, w_in[ch * 128:(ch + 1) * 128, :],
                        chan="wq", waits=w)
            sg = kb.act(sgn[:, 0:width], buf, AF.Sign, waits=[ld])
            ab = kb.act(buf, buf, AF.Abs, waits=[sg])
            mk = kb.ts("vector", msk[:, 0:width], buf, thr_col, None,
                       ALU.is_ge, waits=[ab, thr_done])
            qq = kb.tt("vector", qb, sgn[:, 0:width], msk[:, 0:width], ALU.mult,
                       waits=[sg, mk, sts[ch % 2]])
            st = kb.dma("sync", wq_d[ch * 128:(ch + 1) * 128, :], qb,
                        chan=f"{tag}{ch % 2}", waits=[qq])
            sts[ch % 2] = st
        return [sts[0], sts[1]]

    q1done = qpass2(w1t_in, w1q_d, 8, 4096, acc[:, 2:3], c1, "q1s")
    q2done = qpass2(w2t_in, w2q_d, 32, 1024, acc[:, 3:4], c2, "q2s", prior=q1done)
    return q1done, q2done


def _emit_ffn0_mm(kb, cm, ps_a, ps_b, slab1, slab2, w1q_d, w2q_d, h_sb,
                  b1_sb, b2h_sb, bl1, b2h, q1done, q2done):
    sv1 = slab1[:, :].rearrange("p (b kg j) -> p b kg j", b=2, kg=8)
    sv2 = slab2[:, :].rearrange("p (b kg j) -> p b kg j", b=2, kg=32)
    w1qv = w1q_d.rearrange("(kg p) f -> p kg f", p=128)
    w2qv = w2q_d.rearrange("(kg p) f -> p kg f", p=128)
    bank_free = {0: None, 1: None}
    slab_last_use = {}
    for t in range(NTT):
        h_ready = []
        for m in range(32):
            sb = m % 2
            ld = kb.dma("sync", sv1[:, sb], w1qv[:, :, m * 128:(m + 1) * 128],
                        chan=f"sl1{sb}",
                        waits=(q1done if t == 0 else []) +
                              [slab_last_use.get(("1", sb))])
            ps = (ps_a, ps_b)[m % 2]
            mmw = [ld, cm.ln_done if m == 0 else None, bank_free[m % 2]]
            mml = None
            for kg in range(NG):
                mml = kb.mm(ps[:], sv1[:, sb, kg],
                            cm.xn[:, kg * T + t * 512: kg * T + (t + 1) * 512],
                            kg == 0, kg == NG - 1, waits=mmw if kg == 0 else ())
            slab_last_use[("1", sb)] = mml
            ev = kb.act(h_sb[:, m * 512:(m + 1) * 512], ps[:], AF.Relu,
                        bias=b1_sb[:, m:m + 1],
                        waits=[mml] + ([bl1] + q1done + q2done if t == 0 else []))
            bank_free[m % 2] = ev
            h_ready.append(ev)
        for m in range(8):
            sb = m % 2
            ld = kb.dma("sync", sv2[:, sb], w2qv[:, :, m * 128:(m + 1) * 128],
                        chan=f"sl2{sb}",
                        waits=(q2done if t == 0 else []) +
                              [slab_last_use.get(("2", sb))])
            ps = (ps_a, ps_b)[m % 2]
            mml = None
            for kg in range(32):
                mml = kb.mm(ps[:], sv2[:, sb, kg], h_sb[:, kg * 512:(kg + 1) * 512],
                            kg == 0, kg == 31,
                            waits=([ld, bank_free[m % 2]] + h_ready) if kg == 0 else ())
            slab_last_use[("2", sb)] = mml
            tv = kb.ts("vector", cm.ln_tmp[:], ps[:], 0.5, b2h_sb[:, m:m + 1],
                       ALU.mult, ALU.add, waits=[mml, b2h, kb.val("V")])
            xg = cm.x[:, m * T + t * 512: m * T + (t + 1) * 512]
            upd = kb.tt("vector", xg, xg, cm.ln_tmp[:], ALU.add, waits=[tv])
            bank_free[m % 2] = upd



def build_ffn0(debug=False):
    kb, cm, ar, ps = make_ctx("ffn0")
    cm.load_x()
    emit_ffn0(kb, cm, ar, ps)
    cm.store_x([kb.val("V")])
    return kb, cm


DI = 2048
NGI = DI // 128
NST = 16
TC = 512


def emit_mamba(kb, cm, ar, ps):
    ah, abase = ar.take_raw(39 * 1024)
    def carve(off, n, dt, rows=128):
        nb = n * mybir.dt.size(dt)
        return ah[0:rows, abase + off:abase + off + nb].bitcast(dt)
    sq_ap = carve(0, NG * 512, F32)
    zsig_ap = carve(16384, NGI * TC, BF16)
    lntmp_ap = carve(16384 + 8192, 512, F32)
    rows_ap = carve(16384 + 8192 + 2048, 1536, F32, rows=1)
    H = carve(0, TC * NST, BF16)
    convA = carve(32768, 512, F32)
    convB = carve(32768 + 2048, 512, F32)
    y_sb = carve(32768 + 4096, 512, F32)

    cm.set_ln_custom(sq_ap, lntmp_ap, rows_ap)
    xn = ar.take(NG * 512, BF16)

    wino_in = kb.din("wino", [DM, 2 * DI], F32)
    bino_in = kb.din("bino", [128, 32], F32)
    wini_in = kb.din("wini", [DI, 2 * DI], F32)
    convw_in = kb.din("convw", [128, 16 * 4], F32)
    convb_in = kb.din("convb", [128, 16], F32)
    wxp_in = kb.din("wxp", [DI, 160], F32)
    wdt_in = kb.din("wdt", [128, DI], F32)
    bdt_in = kb.din("bdt", [128, 16], F32)
    alog_in = kb.din("alog", [128, 16 * 16], F32)
    dskip_in = kb.din("dskip", [128, 16], F32)
    woi_in = kb.din("woi", [DI, DI], F32)
    woo_in = kb.din("woo", [DI, DM], F32)
    boo_in = kb.din("boo", [128, 8], F32)
    rowsel_in = kb.din("rowsel", [32, 32 * 128], BF16)

    bino = ar.take(32, F32)
    convw = ar.take(64, F32)
    convb = ar.take(16, F32)
    bdt = ar.take(16, F32)
    dskip = ar.take(16, F32)
    boo = ar.take(8, F32)
    A_sb = ar.take(256, F32)
    wxp = ar.take(16 * 160, BF16)
    wdt = ar.take(DI, BF16)
    rowsel = ar.take(32 * 128, BF16, rows=32)
    hcar = ar.take(256, F32)

    xproj = ar.take(NGI * TC, BF16)
    zsig = zsig_ap
    XMW = 516
    xm = ar.take(NGI * XMW, BF16)
    zmsil = ar.take(NGI * TC, BF16)
    xc = ar.take(NGI * TC, BF16)
    dtb = ar.take(TC, BF16)
    BC = ar.take(TC, BF16, rows=32)
    tail = ar.take(16 * 3, BF16)
    delta = xm[:, 0:1024].bitcast(F32)
    w_sb = xm[:, 1024:2048].bitcast(F32)
    a_sb = xm[:, 2048:3072].bitcast(F32)
    b_sb = xm[:, 3072:4096].bitcast(F32)
    sptA = xm[:, 4096:5120].bitcast(F32)
    sptB = xm[:, 5120:6144].bitcast(F32)
    sptC = xm[:, 6144:7168].bitcast(F32)

    slab = ar.take(2 * 16 * 128, BF16)
    sv = slab[:, :].rearrange("p (b kg j) -> p b kg j", b=2, kg=16)

    ps_a = ps["a"]
    ps_b = ps["b"]
    ps_bc = ps["f1"][0:32, :]
    ps_d = ps["f2"]

    consts = []
    for name, dst, src in [("bino", bino, bino_in), ("convw", convw, convw_in),
                           ("convb", convb, convb_in), ("bdt", bdt, bdt_in),
                           ("dskip", dskip, dskip_in), ("boo", boo, boo_in),
                           ("rowsel", rowsel, rowsel_in)]:
        consts.append(kb.dma("scalar", dst[:], src, chan="c_" + name))
    wxpl = kb.dma("gpsimd", wxp[:].rearrange("p (kg f) -> p kg f", f=160),
                  wxp_in.rearrange("(kg p) f -> p kg f", p=128), chan="c_wxp")
    wdtl = kb.dma("gpsimd", wdt[:], wdt_in, chan="c_wdt")
    alogl = kb.dma("scalar", A_sb[:], alog_in, chan="c_alog")
    ae = kb.act(A_sb[:], A_sb[:], AF.Exp, waits=[alogl])
    A_done = kb.ts("vector", A_sb[:], A_sb[:], -1.0, waits=[ae])
    hz = kb.memset("vector", hcar[:], 0.0)

    wino_v = wino_in.rearrange("(kg p) f -> p kg f", p=128)
    wini_v = wini_in.rearrange("(kg p) f -> p kg f", p=128)
    woi_v = woi_in.rearrange("(kg p) f -> p kg f", p=128)
    woo_v = woo_in.rearrange("(kg p) f -> p kg f", p=128)

    slab_guard = {0: None, 1: None}
    bank_guard = {0: None, 1: None}
    dbc_guard = None
    dd_guard = None

    state = {"conv_tail": None, "stageB_last": None, "scanH_free": None,
             "stageA_last": None, "lnx": None}

    def stream_mm(st, wview, n_k, m_range, rhs_of, evac, first_waits=(), tag=""):
        toks = []
        for mi, m in enumerate(m_range):
            sb = mi % 2
            ld = kb.dma("gpsimd", sv[:, sb, 0:n_k, :],
                        wview[:, 0:n_k, m * 128:(m + 1) * 128],
                        chan=f"sl{sb}", waits=[slab_guard[sb]])
            ps = (ps_a, ps_b)[mi % 2]
            mml = None
            for kg in range(n_k):
                mml = kb.mm(ps[:], sv[:, sb, kg], rhs_of(kg), kg == 0, kg == n_k - 1,
                            waits=([ld, bank_guard[mi % 2]] + list(first_waits))
                            if kg == 0 else ())
            slab_guard[sb] = mml
            tok = evac(m, ps, mml)
            bank_guard[mi % 2] = tok
            toks.append(tok)
        return toks

    for c in range(NTT):
        cs = slice(c * TC, (c + 1) * TC)
        ln_tok = cm.emit_ln_tile(
            c, lambda g: xn[:, g * 512:(g + 1) * 512],
            extra=[state["stageA_last"], state["scanH_free"],
                   state.get("stageF_last")])
        state["lnx"] = ln_tok

        def evacA(m, ps, mmtok):
            if m < 16:
                return kb.ts("vector", xproj[:, m * TC:(m + 1) * TC], ps[:],
                             bino[:, m:m + 1], None, ALU.add,
                             waits=[mmtok, consts[0], state.get("xproj_free")])
            return kb.act(zsig[:, (m - 16) * TC:(m - 15) * TC], ps[:], AF.Sigmoid,
                          bias=bino[:, m:m + 1],
                          waits=[mmtok, consts[0], state["stageF_last"]]
                          if "stageF_last" in state else [mmtok, consts[0]])
        atoks = stream_mm(state, wino_v, NG, range(32),
                          lambda kg: xn[:, kg * 512:(kg + 1) * 512],
                          evacA, first_waits=[ln_tok], tag="A")
        state["stageA_last"] = atoks[-1]

        tail_toks = []
        for g in range(NGI):
            xg = xm[:, g * XMW:g * XMW + XMW]
            if c == 0:
                tail_toks.append(kb.memset("vector", xg[:, 0:3], 0.0))
            else:
                tail_toks.append(kb.copy("vector", xg[:, 0:3],
                                         tail[:, g * 3:(g + 1) * 3],
                                         waits=[state["tailsv"][g],
                                                state.get("scan_w_done")]))
        def evacB(m, ps, mmtok):
            if m < 16:
                return kb.copy("vector", xm[:, m * XMW + 3:m * XMW + 515], ps[:],
                               waits=[mmtok, tail_toks[m], state.get("scan_w_done")])
            return kb.act(zmsil[:, (m - 16) * TC:(m - 15) * TC], ps[:], AF.Silu,
                          waits=[mmtok, state.get("zmsil_free")])
        btoks = stream_mm(state, wini_v, NGI, range(32),
                          lambda kg: xproj[:, kg * TC:(kg + 1) * TC],
                          evacB, first_waits=atoks, tag="B")
        state["stageB_last"] = btoks[-1]
        tailsv = []
        for g in range(NGI):
            tailsv.append(kb.copy("vector", tail[:, g * 3:(g + 1) * 3],
                                  xm[:, g * XMW + 512:g * XMW + 515],
                                  waits=[btoks[g]] +
                                        ([state["tailsv"][g]] if c else [])))
        state["tailsv"] = tailsv

        convdone = []
        for g in range(NGI):
            xg = xm[:, g * XMW:g * XMW + XMW]
            wof = lambda j: convw[:, g * 4 + j:g * 4 + j + 1]
            t1 = kb.ts("vector", convA[:, :], xg[:, 0:512], wof(0), None, ALU.mult,
                       waits=[btoks[g], tail_toks[g], consts[1],
                              state.get("convA_free")])
            t2 = kb.stt("vector", convB[:, :], xg[:, 1:513], wof(1), convA[:, :],
                        ALU.mult, ALU.add, waits=[t1, state.get("convB_free")])
            t3 = kb.stt("vector", convA[:, :], xg[:, 2:514], wof(2), convB[:, :],
                        ALU.mult, ALU.add, waits=[t2])
            t4 = kb.stt("vector", convB[:, :], xg[:, 3:515], wof(3), convA[:, :],
                        ALU.mult, ALU.add, waits=[t3])
            state["convA_free"] = t4
            xcg = kb.act(xc[:, g * TC:(g + 1) * TC], convB[:, :], AF.Silu,
                         bias=convb[:, g:g + 1],
                         waits=[t4, consts[2], state.get("yb_free", (None,) * 99)[g]
                                if isinstance(state.get("yb_free"), list) else None])
            state["convB_free"] = xcg
            convdone.append(xcg)
        state["convdone"] = convdone

        wxp_v = wxp[:].rearrange("p (kg f) -> p kg f", f=160)
        mmd = mmbc = None
        for kg in range(NGI):
            mmd = kb.mm(ps_a[:], wxp_v[:, kg, 0:128], xc[:, kg * TC:(kg + 1) * TC],
                        kg == 0, kg == NGI - 1,
                        waits=[wxpl, bank_guard[0]] + convdone if kg == 0 else ())
            mmbc = kb.mm(ps_bc[:], wxp_v[:, kg, 128:160], xc[:, kg * TC:(kg + 1) * TC],
                         kg == 0, kg == NGI - 1,
                         waits=[dbc_guard] if kg == 0 else ())
        dtb_t = kb.copy("vector", dtb[:], ps_a[:], waits=[mmd, state.get("dtb_free")])
        bank_guard[0] = dtb_t
        bc_t = kb.act(BC[:], ps_bc[:], AF.Copy, waits=[mmbc, state.get("bc_free")])

        Hv = H.rearrange("p (t n) -> p t n", n=NST)
        yb_free = [None] * NGI
        carries = state.get("carries", [None] * NGI)
        for g in range(NGI):
            mmdl = kb.mm(ps_d[:], wdt[:, g * 128:(g + 1) * 128], dtb[:],
                         True, True, waits=[wdtl, dtb_t, dd_guard])
            bcol = bdt[:, g:g + 1]
            s1 = kb.act(sptA[:, :], ps_d[:], AF.Abs, bias=bcol,
                        waits=[mmdl, consts[3], state.get("sptA_free")])
            s2 = kb.act(sptB[:, :], sptA[:, :], AF.Exp, scale=-1.0,
                        waits=[s1, state.get("sptB_free")])
            s3 = kb.act(sptA[:, :], sptB[:, :], AF.Ln, bias=1.0, waits=[s2])
            s4 = kb.act(sptC[:, :], ps_d[:], AF.Relu, bias=bcol,
                        waits=[s1, state.get("sptC_free")])
            dd_guard = s4
            dl = kb.tt("vector", delta[:, :], sptA[:, :], sptC[:, :], ALU.add,
                       waits=[s3, s4, state.get("delta_free")])
            state["sptA_free"] = dl
            state["sptC_free"] = dl
            wg = kb.tt("vector", w_sb[:, :], delta[:, :], xc[:, g * TC:(g + 1) * TC],
                       ALU.mult, waits=[dl, convdone[g], state.get("w_free")])
            scan_toks = []
            prev_scan = state.get("last_scan")
            prev_b = state.get("last_b")
            prev_cb = state.get("last_cb")
            for n in range(NST):
                an = kb.act(a_sb[:, :], delta[:, :], AF.Exp,
                            scale=A_sb[:, g * 16 + n:g * 16 + n + 1],
                            waits=[dl, A_done, prev_scan])
                mmb = kb.mm(ps_a[:], rowsel[:, n * 128:(n + 1) * 128], BC[:],
                            True, True, waits=[bc_t, consts[6], bank_guard[0],
                                               prev_b])
                bn = kb.tt("vector", b_sb[:, :], ps_a[:], w_sb[:, :], ALU.mult,
                           waits=[mmb, wg] + ([prev_scan] if prev_scan else []))
                bank_guard[0] = bn
                prev_b = bn
                sc = kb.sticky(kb.do("vector",
                           lambda e, o=Hv[:, :, n], a=a_sb[:, :], b=b_sb[:, :],
                           ini=hcar[:, g * 16 + n:g * 16 + n + 1]:
                           e.tensor_tensor_scan(o, a, b, ini, ALU.mult, ALU.add),
                           waits=[an, bn, hz, state.get("H_free"), carries[g]]))
                prev_scan = sc
                scan_toks.append(sc)
            carry = kb.copy("vector", hcar[:, g * 16:(g + 1) * 16],
                            H[:, (TC - 1) * NST:TC * NST], waits=[scan_toks[-1]])
            carries[g] = carry
            prod_toks = []
            prev_prod = None
            for n in range(NST):
                mmc = kb.mm(ps_b[:], rowsel[:, (16 + n) * 128:(17 + n) * 128], BC[:],
                            True, True, waits=[bc_t, bank_guard[1], prev_cb])
                pr = kb.tt("vector", Hv[:, :, n], Hv[:, :, n], ps_b[:], ALU.mult,
                           waits=[mmc, carry, scan_toks[n]])
                bank_guard[1] = pr
                prev_cb = pr
                prod_toks.append(pr)
            yt = kb.red(y_sb[:, :], Hv, ALU.add,
                        waits=prod_toks + [state.get("y_free")])
            state["H_free"] = yt
            td = kb.stt("vector", sptA[:, :], xc[:, g * TC:(g + 1) * TC],
                        dskip[:, g:g + 1], y_sb[:, :], ALU.mult, ALU.add,
                        waits=[yt, consts[4], dl])
            state["sptA_free"] = td
            ybt = kb.tt("vector", xc[:, g * TC:(g + 1) * TC], sptA[:, :],
                        zmsil[:, g * TC:(g + 1) * TC], ALU.mult,
                        waits=[td, btoks[16 + g]])
            yb_free[g] = ybt
            state["sptA_free"] = ybt
            state["y_free"] = td
            state["delta_free"] = an
            state["w_free"] = bn
            state["sptB_free"] = s3
            state["last_scan"] = prev_scan
            state["last_b"] = prev_b
            state["last_cb"] = prev_cb
        state["carries"] = carries
        state["dtb_free"] = mmdl
        state["bc_free"] = prev_cb
        dbc_guard = bc_t
        state["yb_free"] = yb_free
        state["zmsil_free"] = yb_free[-1]
        state["scan_w_done"] = yb_free[-1]

        def evacF(m, ps, mmtok):
            return kb.tt("vector", xproj[:, m * TC:(m + 1) * TC], ps[:],
                         zsig[:, m * TC:(m + 1) * TC], ALU.mult,
                         waits=[mmtok, atoks[16 + m]])
        ftoks = stream_mm(state, woi_v, NGI, range(16),
                          lambda kg: xc[:, kg * TC:(kg + 1) * TC],
                          evacF, first_waits=yb_free, tag="F")
        state["stageF_last"] = ftoks[-1]

        def evacG(m, ps, mmtok):
            xg = cm.x[:, m * T + c * TC: m * T + (c + 1) * TC]
            return kb.stt("vector", xg, ps[:], boo[:, m:m + 1], xg,
                          ALU.add, ALU.add, waits=[mmtok, consts[5]])
        gtoks = stream_mm(state, woo_v, NGI, range(8),
                          lambda kg: xproj[:, kg * TC:(kg + 1) * TC],
                          evacG, first_waits=[ftoks[-1]], tag="G")
        state["scanH_free"] = state["H_free"]
        state["xproj_free"] = gtoks[-1]

    return


def build_mamba(debug=False):
    kb, cm, ar, ps = make_ctx("mamba")
    cm.load_x()
    emit_mamba(kb, cm, ar, ps)
    cm.store_x([kb.val("V")])
    return kb, cm


def mamba_inputs(inp):
    p = "L0_attn_"
    sel = np.zeros((32, 32 * 128), np.float32)
    for n in range(32):
        sel[n, n * 128:(n + 1) * 128] = 1.0
    return {
        "wino": np.asarray(inp[p + "W_in_o"], np.float32),
        "bino": colmaj(inp[p + "b_in_o"]),
        "wini": np.asarray(inp[p + "W_in_i"], np.float32),
        "convw": np.ascontiguousarray(
            np.asarray(inp[p + "conv_w"], np.float32).reshape(16, 128, 4)
            .transpose(1, 0, 2).reshape(128, 64)),
        "convb": colmaj(inp[p + "conv_b"]),
        "wxp": np.asarray(inp[p + "W_xp"], np.float32),
        "wdt": np.asarray(inp[p + "W_dt"], np.float32),
        "bdt": colmaj(inp[p + "b_dt"]),
        "alog": np.ascontiguousarray(
            np.asarray(inp[p + "A_log"], np.float32).reshape(16, 128, 16)
            .transpose(1, 0, 2).reshape(128, 256)),
        "dskip": colmaj(inp[p + "D_skip"]),
        "woi": np.asarray(inp[p + "W_out_i"], np.float32),
        "woo": np.asarray(inp[p + "W_out_o"], np.float32),
        "boo": colmaj(inp[p + "b_out_o"]),
        "rowsel": np_bf16(sel),
        "onecol": np.ones((128, 1), np.float32),
        "onerow": np.ones((1, 128), np.float32),
    }


def run_mamba(x_cores, inp):
    kb, nc = get_prog("mamba")
    base = mamba_inputs(inp)
    in_maps = [dict(base, x_in=x) for x in x_cores]
    res = run_bass_kernel_spmd(nc, in_maps, core_ids=list(range(NCORES)))
    return [r["x_out"] for r in res.results]



def emit_attn(kb, cm, ar, ps):
    wq_in = kb.din("wqT", [DM, DM], F32)
    wk_in = kb.din("wkT", [DM, DM], F32)
    wv_in = kb.din("wvT", [DM, DM], F32)
    wo_in = kb.din("woT", [DM, DM], F32)
    bq_in = kb.din("bq", [128, 8], F32)
    bk_in = kb.din("bk", [128, 8], F32)
    bv_in = kb.din("bvrow", [1, DM], F32)
    bo_in = kb.din("bo", [128, 8], F32)
    mask_in = kb.din("cmask", [128, 128], F32)
    ident_in = kb.din("ident", [128, 128], BF16)
    onerowb_in = kb.din("onerowb", [1, 128], BF16)

    cm.set_ln(ar)
    qT = ar.take(2 * T, BF16)
    kT = ar.take(2 * T, BF16)
    vtm = ar.take(16 * 256, BF16)
    oT = ar.take(NG * T, BF16)
    srow = ar.take(T, F32)
    P = ar.take(T, BF16)
    Pt = ar.take(T, BF16)
    mask = ar.take(128, F32)
    ident = ar.take(128, BF16)
    bq = ar.take(8, F32)
    bk = ar.take(8, F32)
    bo = ar.take(8, F32)
    bvr = ar.take(DM, BF16, rows=1)
    mxc = ar.take(4, F32)
    slabq = ar.take(2 * 8 * 128, BF16)
    slabv = ar.take(8 * 256, BF16)
    svq = slabq[:, :].rearrange("p (b kg j) -> p b kg j", b=2, kg=8)
    svv = slabv[:, :].rearrange("p (kg j) -> p kg j", kg=8)

    ps_a = ps["a"]
    ps_b = ps["b"]
    ps_tr = ps["f2"][:, 0:256].bitcast(BF16)
    ps_o = ps["f1"]

    consts = {}
    for name, dst, src in [("bq", bq, bq_in), ("bk", bk, bk_in), ("bo", bo, bo_in),
                           ("mask", mask, mask_in), ("ident", ident, ident_in)]:
        consts[name] = kb.dma("scalar", dst[:], src, chan="c_" + name)
    consts["bvr"] = kb.dma("gpsimd", bvr[:], bv_in, chan="c_bvr")
    onerowb = ar.take(128, BF16, rows=1)
    consts["orb"] = kb.dma("scalar", onerowb[:], onerowb_in, chan="c_orb")
    bqs = kb.ts("vector", bq[:], bq[:], 0.125, waits=[consts["bq"]])

    cm.emit_ln()

    wq_v = wq_in.rearrange("(kg p) f -> p kg f", p=128)
    wk_v = wk_in.rearrange("(kg p) f -> p kg f", p=128)
    wv_v = wv_in.rearrange("(kg p) f -> p kg f", p=128)
    wo_v = wo_in.rearrange("(kg p) f -> p kg f", p=128)

    state = {}
    slab_guard = {0: None, 1: None}
    bank_guard = {0: None, 1: None}
    tr_guard = None
    o_guard = None

    def qk_build(w_v, dst, bias_sb, scale, hg, chan, extra=()):
        toks = []
        for mq in range(2):
            m = hg * 2 + mq
            sb = mq % 2
            bias_col = bias_sb[:, m:m + 1]
            ld = kb.dma("gpsimd", svq[:, sb], w_v[:, :, m * 128:(m + 1) * 128],
                        chan=f"{chan}{sb}", waits=[slab_guard[sb]])
            for t in range(NTT):
                ps = (ps_a, ps_b)[t % 2]
                mml = None
                for kg in range(NG):
                    mml = kb.mm(ps[:], svq[:, sb, kg],
                                cm.xn[:, kg * T + t * 512: kg * T + (t + 1) * 512],
                                kg == 0, kg == NG - 1,
                                waits=[ld, cm.ln_done, bank_guard[t % 2]] + list(extra)
                                if kg == 0 else ())
                if scale is None:
                    ev = kb.ts("vector", dst[:, mq * T + t * 512: mq * T + (t + 1) * 512],
                               ps[:], bias_col, None, ALU.add, waits=[mml])
                else:
                    ev = kb.ts("vector", dst[:, mq * T + t * 512: mq * T + (t + 1) * 512],
                               ps[:], scale, bias_col, ALU.mult, ALU.add, waits=[mml])
                bank_guard[t % 2] = ev
                toks.append(ev)
            slab_guard[sb] = mml
        return toks

    for hg in range(4):
        qtok = qk_build(wq_v, qT, bq, 0.125, hg, "slq", extra=[bqs])
        ktok = qk_build(wk_v, kT, bk, None, hg, "slk", extra=[consts["bk"]])
        vl = kb.dma("gpsimd", svv[:, :, :], wv_v[:, :, hg * 256:(hg + 1) * 256],
                    chan="slv", waits=[state.get("v_slab_free")])
        vtoks = []
        for tt_ in range(16):
            mml = None
            for kg in range(NG):
                mml = kb.mm(ps_a[:, 0:256], cm.xn[:, kg * T + tt_ * 128: kg * T + (tt_ + 1) * 128],
                            svv[:, kg], kg == 0, False,
                            waits=[vl, cm.ln_done, bank_guard[0]] if kg == 0 else ())
            mml = kb.mm(ps_a[:, 0:256], onerowb[:], bvr[0:1, hg * 256:(hg + 1) * 256],
                        False, True, waits=[consts["bvr"], consts["orb"]])
            ev = kb.copy("vector", vtm[:, tt_ * 256:(tt_ + 1) * 256], ps_a[:, 0:256],
                         waits=[mml])
            bank_guard[0] = ev
            vtoks.append(ev)
        state["v_slab_free"] = mml

        for hl in range(4):
            h = hg * 4 + hl
            qbase = (hl % 2) * 64
            qhalf = (hl // 2) * T
            for qc in range(16):
                L = (qc + 1) * 128
                nkt = (L + 511) // 512
                ev_toks = []
                for kt in range(nkt):
                    wdt_ = min(512, L - kt * 512)
                    ps = (ps_a, ps_b)[kt % 2]
                    mml = kb.mm(ps[:, 0:wdt_],
                                qT[qbase:qbase + 64, qhalf + qc * 128: qhalf + (qc + 1) * 128],
                                kT[qbase:qbase + 64, qhalf + kt * 512: qhalf + kt * 512 + wdt_],
                                True, True,
                                waits=[qtok[-1], ktok[-1], bank_guard[kt % 2]])
                    ev = kb.copy("vector", srow[:, kt * 512:kt * 512 + wdt_],
                                 ps[:, 0:wdt_], waits=[mml, state.get("srow_free")])
                    bank_guard[kt % 2] = ev
                    ev_toks.append(ev)
                mk_ = kb.tt("vector", srow[:, L - 128:L], srow[:, L - 128:L], mask[:],
                            ALU.add, waits=[consts["mask"], ev_toks[-1]])
                mx = kb.red(mxc[:, 0:1], srow[:, 0:L], ALU.max,
                            waits=[mk_] + ev_toks)
                ng = kb.ts("vector", mxc[:, 1:2], mxc[:, 0:1], -1.0, waits=[mx])
                pe_ = kb.act(P[:, 0:L], srow[:, 0:L], AF.Exp, bias=mxc[:, 1:2],
                             accum=mxc[:, 2:3], waits=[ng, state.get("P_free")])
                rc_ = kb.sticky(kb.do("vector", lambda e, o=mxc[:, 3:4], i=mxc[:, 2:3]:
                            e.reciprocal(o, i), waits=[pe_]))
                pn = kb.ts("vector", P[:, 0:L], P[:, 0:L], mxc[:, 3:4], None,
                           ALU.mult, waits=[rc_])
                pt_toks = []
                for ktile in range(qc + 1):
                    tr = kb.do("tensor", lambda e, o=ps_tr[:, 0:128],
                               i=P[:, ktile * 128:(ktile + 1) * 128], idn=ident[:]:
                               e.transpose(o, i, idn),
                               waits=[pn, consts["ident"], tr_guard])
                    tev = kb.copy("scalar", Pt[:, ktile * 128:(ktile + 1) * 128],
                                  ps_tr[:, 0:128], waits=[tr, state.get("Pt_free")])
                    tr_guard = tev
                    pt_toks.append(tev)
                mml = None
                for ktile in range(qc + 1):
                    mml = kb.mm(ps_o[0:64, 0:128],
                                vtm[:, ktile * 256 + hl * 64: ktile * 256 + hl * 64 + 64],
                                Pt[:, ktile * 128:(ktile + 1) * 128],
                                ktile == 0, ktile == qc,
                                waits=[pt_toks[ktile], vtoks[ktile],
                                       o_guard if ktile == 0 else None])
                g_o = h // 2
                off_o = (h % 2) * 64
                oev = kb.copy("vector",
                              oT[off_o:off_o + 64, g_o * T + qc * 128: g_o * T + (qc + 1) * 128],
                              ps_o[0:64, 0:128], waits=[mml])
                o_guard = oev
                state["srow_free"] = pe_
                state["P_free"] = tr
                state["Pt_free"] = mml
    o_done = kb.val("V")

    for m in range(8):
        sb = m % 2
        ld = kb.dma("gpsimd", svq[:, sb], wo_v[:, :, m * 128:(m + 1) * 128],
                    chan=f"slo{sb}", waits=[slab_guard[sb]])
        for t in range(NTT):
            ps = (ps_a, ps_b)[t % 2]
            mml = None
            for kg in range(NG):
                mml = kb.mm(ps[:], svq[:, sb, kg],
                            oT[:, kg * T + t * 512: kg * T + (t + 1) * 512],
                            kg == 0, kg == NG - 1,
                            waits=[ld, o_done, bank_guard[t % 2]] if kg == 0 else ())
            xg = cm.x[:, m * T + t * 512: m * T + (t + 1) * 512]
            ev = kb.stt("vector", xg, ps[:], bo[:, m:m + 1], xg, ALU.add, ALU.add,
                        waits=[mml, consts["bo"]])
            bank_guard[t % 2] = ev
        slab_guard[sb] = mml
    return


def build_attn(debug=False):
    kb, cm, ar, ps = make_ctx("attn")
    cm.load_x()
    emit_attn(kb, cm, ar, ps)
    cm.store_x([kb.val("V")])
    return kb, cm


def attn_inputs(inp):
    mask = np.zeros((128, 128), np.float32)
    mask[np.triu_indices(128, 1)] = -1e30
    return {
        "onerowb": np_bf16(np.ones((1, 128))),
        "wqT": tr(inp["L1_attn_Wq"]), "wkT": tr(inp["L1_attn_Wk"]),
        "wvT": tr(inp["L1_attn_Wv"]), "woT": tr(inp["L1_attn_Wo"]),
        "bq": colmaj(inp["L1_attn_bq"]), "bk": colmaj(inp["L1_attn_bk"]),
        "bvrow": np.asarray(inp["L1_attn_bv"], np.float32).reshape(1, DM),
        "bo": colmaj(inp["L1_attn_bo"]),
        "cmask": mask, "ident": np_bf16(np.eye(128)),
        "onecol": np.ones((128, 1), np.float32),
        "onerow": np.ones((1, 128), np.float32),
    }


def run_attn(x_cores, inp):
    kb, nc = get_prog("attn")
    base = attn_inputs(inp)
    in_maps = [dict(base, x_in=x) for x in x_cores]
    res = run_bass_kernel_spmd(nc, in_maps, core_ids=list(range(NCORES)))
    return [r["x_out"] for r in res.results]



def emit_moe(kb, cm, ar, ps, sim1=False):
    kb.notrust = True
    ah, abase = ar.take_raw(44 * 1024)
    def carve(off, n, dt, rows=128):
        nb = n * mybir.dt.size(dt)
        return ah[0:rows, abase + off:abase + off + nb].bitcast(dt)
    sq_ap = carve(0, NG * 512, F32)
    Gfm = carve(0, NG * EROWS, BF16)
    lntmp = ar.take(512, F32)
    rows_ap = ar.take(1536, F32, rows=1)
    cm.set_ln_custom(sq_ap, lntmp, rows_ap, with_stats=True)
    cm.xn = ar.take(NG * T, BF16)

    if not hasattr(cm, "moe_io"):
        cm.moe_io = (kb.din("routerw", [DM, 8], F32),
                     kb.din("moew1", [8, DM, 4096], F32),
                     kb.din("moeb1", [128, 8 * 32], F32),
                     kb.din("moew2", [8, 4096, DM], F32),
                     kb.din("moeb2", [128, 8 * 8], F32),
                     kb.din("presel", [8, 1], F32),
                     kb.din("cumt", [128, 128], F32),
                     kb.din("e8", [128, 8], F32),
                     kb.din("identb", [128, 128], BF16),
                     kb.din("identf", [128, 128], F32),
                     kb.dtmp("G_d", [EROWS, DM], BF16),
                     kb.dtmp("Go_d", [EROWS, DM], BF16),
                     kb.dtmp("cc_in", [1, 8], F32),
                     kb.dtmp("cc_out", [8, 8], F32))
    (rw_in, w1_in, b1_in, w2_in, b2_in, presel_in, cumt_in, e8_in, ident_in,
     identf_in, G_d, Go_d, cc_in, cc_out) = cm.moe_io
    h_sb = ar.take(32 * KCAP, BF16)
    xntm = ar.take(DM, BF16)
    moetm = ar.take(DM, BF16)
    gtm = ar.take(DM, BF16)
    slab1 = ar.take(2 * 8 * 128, BF16)
    slab2 = ar.take(2 * 32 * 128, BF16)
    sv1 = slab1[:, :].rearrange("p (b kg j) -> p b kg j", b=2, kg=8)
    sv2 = slab2[:, :].rearrange("p (b kg j) -> p b kg j", b=2, kg=32)
    rw_sb = ar.take(8 * 8, F32)
    sw_sb = ar.take(8, F32)
    stat16 = ar.take(2 * 128, F32, rows=16)
    mtm = ar.take(16, F32)
    rtm = ar.take(16, F32)
    logit = ar.take(8, F32)
    emask = ar.take(16 * 8, F32)
    cum = ar.take(16 * 8, F32)
    gate = ar.take(16, F32)
    gk = ar.take(16, F32)
    growi = ar.take(16, I32)
    runrow = ar.take(16, F32, rows=1)
    ag_sb = ar.take(8, F32, rows=8)
    offs128 = ar.take(8, F32)
    scr = ar.take(8 * 4, F32)
    cumt = ar.take(128, F32)
    e8 = ar.take(8, F32)
    identb = ar.take(128, BF16)
    identf = ar.take(128, F32)
    presel = ar.take(1, F32, rows=8)
    b1_sb = ar.take(8 * 32, F32)
    b2_sb = ar.take(8 * 8, F32)

    ps_a = ps["a"]
    ps_b = ps["b"]
    ps_c = ps["f1"]
    ps_tr = ps["f2"][:, 0:256].bitcast(BF16)

    consts = {}
    rwload = (rw_sb[:].rearrange("p (kg e) -> p kg e", e=8),
              rw_in.rearrange("(kg p) e -> p kg e", p=128))
    for name, dst, src in [("rw", *rwload),
                           ("cumt", cumt, cumt_in), ("e8", e8, e8_in),
                           ("identb", identb, ident_in), ("identf", identf, identf_in),
                           ("presel", presel, presel_in), ("b1", b1_sb, b1_in),
                           ("b2", b2_sb, b2_in)]:
        consts[name] = kb.dma("scalar", dst if name == "rw" else dst[:], src,
                              chan="c_" + name)
    zz = kb.memset("vector", ag_sb[:], 0.0)
    rwv = rw_sb[:].rearrange("p (kg e) -> p kg e", e=8)

    gz = kb.memset("vector", gtm[:], 0.0)
    gzt = []
    for st in range(EROWS // 128):
        gzt.append(kb.dma("sync", G_d[st * 128:(st + 1) * 128, :], gtm[:],
                          chan="gz", waits=[gz]))

    cm.emit_ln()

    swm = None
    for kg in range(NG):
        swm = kb.mm(ps_c[0:1, 0:8], cm.onecol, rwv[:, kg], kg == 0, kg == NG - 1,
                    waits=[consts["rw"], cm.ones_loaded[0]] if kg == 0 else ())
    swr = kb.copy("vector", runrow[0:1, 8:16], ps_c[0:1, 0:8], waits=[swm])
    swb = kb.mm(ps_c[:, 0:8], cm.onerow, runrow[0:1, 8:16], True, True,
                waits=[swr, cm.ones_loaded[1]])
    swe = kb.copy("vector", sw_sb[:], ps_c[:, 0:8], waits=[swb])

    stl = kb.dma("sync", stat16[:].rearrange("t (h p) -> t h p", h=2),
                 cm.stat_d.rearrange("o (h tt p) -> (o tt) h p", p=128, h=2),
                 chan="statl", waits=cm.stat_w)
    t_m = kb.do("tensor", lambda e, o=ps_a[:, 0:16], i=stat16[0:16, 0:128],
                idn=identf[0:16, 0:16]: e.transpose(o, i, idn),
                waits=[stl, consts["identf"]])
    ev_m = kb.copy("vector", mtm[:], ps_a[:, 0:16], waits=[t_m])
    t_r = kb.do("tensor", lambda e, o=ps_a[:, 16:32], i=stat16[0:16, 128:256],
                idn=identf[0:16, 0:16]: e.transpose(o, i, idn),
                waits=[stl, ev_m])
    ev_r = kb.copy("vector", rtm[:], ps_a[:, 16:32], waits=[t_r])

    zrun = kb.memset("vector", runrow[0:1, 0:8], 0.0)
    emv = emask[:].rearrange("p (tt e) -> p tt e", e=8)
    cuv = cum[:].rearrange("p (tt e) -> p tt e", e=8)
    rtoks = []
    run_tok = zrun
    prev_ps = None
    for tt_ in range(16):
        mml = None
        for kg in range(NG):
            mml = kb.mm(ps_b[:, 0:8], cm.x[:, kg * T + tt_ * 128:kg * T + (tt_ + 1) * 128],
                        rwv[:, kg], kg == 0, kg == NG - 1,
                        waits=[consts["rw"], cm.ln_done, prev_ps] if kg == 0 else ())
        s1 = kb.ts("vector", scr[:, 0:8], sw_sb[:], mtm[:, tt_:tt_ + 1], None,
                   ALU.mult, waits=[swe, ev_m, state_none(rtoks)])
        s2 = kb.tt("vector", scr[:, 0:8], ps_b[:, 0:8], scr[:, 0:8], ALU.subtract,
                   waits=[mml, s1])
        prev_ps = s2
        lg = kb.ts("vector", logit[:], scr[:, 0:8], rtm[:, tt_:tt_ + 1], None,
                   ALU.mult, waits=[s2, ev_r, rtoks[-1][0] if rtoks else None])
        mx = kb.red(scr[:, 8:9], logit[:], ALU.max, waits=[lg])
        ng = kb.ts("vector", scr[:, 9:10], scr[:, 8:9], -1.0, waits=[mx])
        pe_ = kb.act(scr[:, 16:24], logit[:], AF.Exp, bias=scr[:, 9:10],
                     accum=scr[:, 10:11], waits=[ng])
        rc = kb.sticky(kb.do("vector", lambda e, o=gate[:, tt_:tt_ + 1], i=scr[:, 10:11]:
                   e.reciprocal(o, i), waits=[pe_]))
        em = kb.ts("vector", emv[:, tt_], logit[:], scr[:, 8:9], None, ALU.is_ge,
                   waits=[mx, lg])
        c1 = kb.mm(ps_c[:, 0:8], cumt[:], emv[:, tt_], True, False,
                   waits=[em, consts["cumt"], prev_cum_ev(rtoks), run_tok, swe])
        c2 = kb.mm(ps_c[:, 0:8], cm.onerow, runrow[0:1, 0:8], False, True,
                   waits=[run_tok])
        cv = kb.copy("vector", cuv[:, tt_], ps_c[:, 0:8], waits=[c2])
        c3 = kb.mm(ps_c[0:1, 16:24], cm.onecol, emv[:, tt_], True, True,
                   waits=[em, cv])
        tr_ = kb.copy("vector", runrow[0:1, 8:16], ps_c[0:1, 16:24], waits=[c3, swe])
        run_tok = kb.tt("vector", runrow[0:1, 0:8], runrow[0:1, 0:8],
                        runrow[0:1, 8:16], ALU.add, waits=[tr_])
        rtoks.append((em, cv))
    ccw = kb.dma("gpsimd", cc_in, runrow[0:1, 0:8], chan="ccw", waits=[run_tok])
    groups = [[0]] if sim1 else [list(range(NCORES))]
    kb.sem("C_cc")
    ccn = kb.dma_uses["cc"] = kb.dma_uses.get("cc", 0) + 1
    cc_out_ap = cc_out[0:1, :] if sim1 else cc_out

    def ccfn(e, g=groups, sem=kb.semh["C_cc"], oap=cc_out_ap):
        e.collective_compute("AllGather", ALU.bypass, replica_groups=g,
                             ins=[cc_in], outs=[oap]).then_inc(sem, 1)
    kb.do("gpsimd", ccfn, waits=[ccw])
    cctok = ("C_cc", ccn)
    agl = kb.dma("gpsimd", ag_sb[0:1 if sim1 else 8, :],
                 cc_out[0:1 if sim1 else 8, :], chan="agl", waits=[cctok, zz])
    of1 = kb.mm(ps_c[0:1, 0:8], presel[:], ag_sb[:], True, True,
                waits=[agl, consts["presel"], run_tok])
    of2 = kb.copy("vector", runrow[0:1, 8:16], ps_c[0:1, 0:8], waits=[of1, run_tok])
    of3 = kb.mm(ps_c[:, 0:8], cm.onerow, runrow[0:1, 8:16], True, True, waits=[of2])
    of4 = kb.copy("vector", offs128[:], ps_c[:, 0:8], waits=[of3])

    gtoks = []
    for tt_ in range(16):
        g1 = kb.tt("vector", scr[:, 0:8], cuv[:, tt_], offs128[:], ALU.add,
                   waits=[of4, rtoks[tt_][1]] + ([gtoks[-1]] if gtoks else []))
        g2 = kb.tt("vector", scr[:, 0:8], scr[:, 0:8], emv[:, tt_], ALU.mult,
                   waits=[g1])
        g3 = kb.red(scr[:, 8:9], scr[:, 0:8], ALU.add, waits=[g2])
        g4 = kb.ts("vector", scr[:, 9:10], scr[:, 8:9], float(T * NCORES // 8), None,
                   ALU.is_le, waits=[g3])
        g5 = kb.tt("vector", scr[:, 0:8], cuv[:, tt_], emv[:, tt_], ALU.mult,
                   waits=[g2])
        g6 = kb.red(scr[:, 10:11], scr[:, 0:8], ALU.add, waits=[g5])
        g7 = kb.ts("vector", scr[:, 11:12], scr[:, 10:11], -1.0, None, ALU.add,
                   waits=[g6])
        g8 = kb.ts("vector", scr[:, 12:13], scr[:, 11:12], float(KCAP - 1), None,
                   ALU.is_le, waits=[g7])
        g9 = kb.tt("vector", scr[:, 0:8], emv[:, tt_], e8[:], ALU.mult,
                   waits=[g5, consts["e8"]])
        g10 = kb.red(scr[:, 13:14], scr[:, 0:8], ALU.add, waits=[g9])
        g11 = kb.stt("vector", scr[:, 14:15], scr[:, 13:14], float(KCAP),
                     scr[:, 11:12], ALU.mult, ALU.add, waits=[g10, g7])
        g12 = kb.ts("vector", scr[:, 15:16], scr[:, 12:13], -1e6, 1e6,
                    ALU.mult, ALU.add, waits=[g8])
        g13 = kb.tt("vector", scr[:, 14:15], scr[:, 14:15], scr[:, 15:16],
                    ALU.add, waits=[g11, g12])
        g14 = kb.copy("vector", growi[:, tt_:tt_ + 1], scr[:, 14:15], waits=[g13])
        g15 = kb.tt("vector", gk[:, tt_:tt_ + 1], gate[:, tt_:tt_ + 1],
                    scr[:, 9:10], ALU.mult, waits=[g4])
        g16 = kb.ts("vector", gk[:, tt_:tt_ + 1], gk[:, tt_:tt_ + 1], 0.5,
                    waits=[g15])
        gtoks.append(g14)
        gtoks.append(g16)

    tr_guard = None
    sc_toks = []
    prev_sc = None
    for tt_ in range(16):
        evs = []
        for kg in range(NG):
            trx = kb.do("tensor", lambda e, o=ps_tr[:, 0:128],
                        i=cm.xn[:, kg * T + tt_ * 128:kg * T + (tt_ + 1) * 128],
                        idn=identb[:]: e.transpose(o, i, idn),
                        waits=[cm.ln_done, consts["identb"], tr_guard])
            ev = kb.copy("scalar", xntm[:, kg * 128:(kg + 1) * 128], ps_tr[:, 0:128],
                         waits=[trx, prev_sc])
            tr_guard = ev
            evs.append(ev)
        sc = kb.idma(G_d, growi[:, tt_:tt_ + 1], xntm[:], None, chan="scat",
                     bounds=EROWS - 1,
                     waits=evs + [gtoks[2 * tt_]] + gzt)
        prev_sc = sc
        sc_toks.append(sc)

    gl_toks = []
    gfm_evs = []
    for st in range(EROWS // 128):
        gl = kb.dma("sync", gtm[:], G_d[st * 128:(st + 1) * 128, :], chan="gl",
                    waits=sc_toks + [gz] + ([gfm_evs[-1]] if gfm_evs else []))
        for kg in range(NG):
            trx = kb.do("tensor", lambda e, o=ps_tr[:, 0:128],
                        i=gtm[:, kg * 128:(kg + 1) * 128], idn=identb[:]:
                        e.transpose(o, i, idn), waits=[gl, tr_guard])
            ev = kb.copy("scalar", Gfm[:, kg * EROWS + st * 128:kg * EROWS + (st + 1) * 128],
                         ps_tr[:, 0:128], waits=[trx])
            tr_guard = ev
            gfm_evs.append(ev)
    gfm_done = gfm_evs[-1]

    w1v = [w1_in[e].rearrange("(kg p) f -> p kg f", p=128) for e in range(8)]
    w2v = [w2_in[e].rearrange("(kg p) f -> p kg f", p=128) for e in range(8)]
    slab_guard = {("1", 0): None, ("1", 1): None, ("2", 0): None, ("2", 1): None}
    bank_guard = {0: None, 1: None}
    hv = h_sb[:].rearrange("p (kg j) -> p kg j", j=KCAP)
    out_evs = []
    for e_ in range(8):
        esl = slice(e_ * KCAP, (e_ + 1) * KCAP)
        h_toks = []
        for m in range(32):
            sb = m % 2
            ld = kb.dma("gpsimd", sv1[:, sb], w1v[e_][:, :, m * 128:(m + 1) * 128],
                        chan=f"m1{sb}", waits=[slab_guard[("1", sb)]])
            ps = (ps_a, ps_b)[m % 2]
            mml = None
            for kg in range(NG):
                mml = kb.mm(ps[:, 0:KCAP], sv1[:, sb, kg],
                            Gfm[:, kg * EROWS + e_ * KCAP:kg * EROWS + (e_ + 1) * KCAP],
                            kg == 0, kg == NG - 1,
                            waits=[ld, gfm_done, bank_guard[m % 2]] +
                                  (out_evs if m == 0 and kg == 0 else [])
                            if kg == 0 else ())
            slab_guard[("1", sb)] = mml
            ev = kb.act(hv[:, m], ps[:, 0:KCAP], AF.Relu,
                        bias=b1_sb[:, e_ * 32 + m:e_ * 32 + m + 1],
                        waits=[mml, consts["b1"], state_h_free(out_evs, h_toks)])
            bank_guard[m % 2] = ev
            h_toks.append(ev)
        out_evs = []
        for m in range(8):
            sb = m % 2
            ld = kb.dma("gpsimd", sv2[:, sb], w2v[e_][:, :, m * 128:(m + 1) * 128],
                        chan=f"m2{sb}", waits=[slab_guard[("2", sb)]])
            ps = (ps_a, ps_b)[m % 2]
            mml = None
            for kg in range(32):
                mml = kb.mm(ps[:, 0:KCAP], sv2[:, sb, kg], hv[:, kg],
                            kg == 0, kg == 31,
                            waits=([ld, bank_guard[m % 2]] + h_toks)
                            if kg == 0 else ())
            slab_guard[("2", sb)] = mml
            ev = kb.ts("vector", Gfm[:, m * EROWS + e_ * KCAP:m * EROWS + (e_ + 1) * KCAP],
                       ps[:, 0:KCAP], b2_sb[:, e_ * 8 + m:e_ * 8 + m + 1], None,
                       ALU.add, waits=[mml, consts["b2"], h_toks[-1]])
            bank_guard[m % 2] = ev
            out_evs.append(ev)

    go_st = []
    for st in range(EROWS // 128):
        evs = []
        for kg in range(NG):
            trx = kb.do("tensor", lambda e, o=ps_tr[:, 0:128],
                        i=Gfm[:, kg * EROWS + st * 128:kg * EROWS + (st + 1) * 128],
                        idn=identb[:]: e.transpose(o, i, idn),
                        waits=[tr_guard] + out_evs)
            ev = kb.copy("scalar", gtm[:, kg * 128:(kg + 1) * 128], ps_tr[:, 0:128],
                         waits=[trx, go_st[-1] if go_st else None])
            tr_guard = ev
            evs.append(ev)
        go_st.append(kb.dma("sync", Go_d[st * 128:(st + 1) * 128, :], gtm[:],
                            chan="gost", waits=evs))

    prev_add = None
    for tt_ in range(16):
        mz = kb.memset("vector", moetm[:], 0.0, waits=[prev_add])
        ga = kb.idma(moetm[:], None, Go_d, growi[:, tt_:tt_ + 1], chan="gab",
                     bounds=EROWS - 1, waits=[mz] + go_st + [gtoks[2 * tt_]])
        gm = kb.ts("vector", moetm[:], moetm[:], gk[:, tt_:tt_ + 1], None,
                   ALU.mult, waits=[ga, gtoks[2 * tt_ + 1]])
        for kg in range(NG):
            trx = kb.do("tensor", lambda e, o=ps_tr[:, 0:128],
                        i=moetm[:, kg * 128:(kg + 1) * 128], idn=identb[:]:
                        e.transpose(o, i, idn), waits=[gm, tr_guard])
            xg = cm.x[:, kg * T + tt_ * 128:kg * T + (tt_ + 1) * 128]
            prev_add = kb.tt("vector", xg, xg, ps_tr[:, 0:128], ALU.add,
                             waits=[trx])
            tr_guard = prev_add
    kb.notrust = False
    return


def build_moe(sim1=False, debug=False):
    kb, cm, ar, ps = make_ctx("moe")
    cm.load_x()
    emit_moe(kb, cm, ar, ps, sim1=sim1)
    cm.store_x([kb.val("V")])
    return kb, cm


def state_none(rtoks):
    return rtoks[-1][0] if rtoks else None


def prev_cum_ev(rtoks):
    return rtoks[-1][1] if rtoks else None


def state_h_free(out_evs, h_toks):
    return h_toks[-1] if h_toks else (out_evs[-1] if out_evs else None)


def moe_inputs(inp, core_id):
    presel = np.zeros((8, 1), np.float32)
    presel[:core_id] = 1.0
    b1 = np.concatenate([colmaj(inp["L1_ffn_b1"][e]) for e in range(8)], axis=1)
    b2 = np.concatenate([colmaj(inp["L1_ffn_b2"][e]) for e in range(8)], axis=1)
    return {
        "routerw": np.asarray(inp["L1_ffn_router_w"], np.float32),
        "moew1": np.asarray(inp["L1_ffn_w1"], np.float32),
        "moeb1": b1,
        "moew2": np.asarray(inp["L1_ffn_w2"], np.float32),
        "moeb2": b2,
        "presel": presel,
        "cumt": np.triu(np.ones((128, 128), np.float32)),
        "e8": np.broadcast_to(np.arange(8, dtype=np.float32), (128, 8)).copy(),
        "identb": np_bf16(np.eye(128)),
        "identf": np.eye(128, dtype=np.float32),
        "onecol": np.ones((128, 1), np.float32),
        "onerow": np.ones((1, 128), np.float32),
    }


def run_moe(x_cores, inp):
    kb, nc = get_prog("moe")
    in_maps = [dict(moe_inputs(inp, c), x_in=x_cores[c]) for c in range(NCORES)]
    res = run_bass_kernel_spmd(nc, in_maps, core_ids=list(range(NCORES)))
    return [r["x_out"] for r in res.results]



def build_fused(sim1=False):
    kb, cm, ar, ps = make_ctx("fused")
    cm.load_x()
    emit_ffn0(kb, cm, ar, ps, quant=True)
    kb.barrier(); ar.reset()
    emit_mamba(kb, cm, ar, ps)
    kb.barrier(); ar.reset()
    emit_ffn0(kb, cm, ar, ps, quant=False)
    kb.barrier(); ar.reset()
    emit_moe(kb, cm, ar, ps, sim1=sim1)
    kb.barrier(); ar.reset()
    emit_attn(kb, cm, ar, ps)
    kb.barrier(); ar.reset()
    emit_moe(kb, cm, ar, ps, sim1=sim1)
    cm.store_x([kb.val("V")])
    return kb, cm


def fused_inputs(inp, core_id):
    d = {}
    d.update(ffn0_inputs(inp))
    d.update(mamba_inputs(inp))
    d.update(attn_inputs(inp))
    d.update(moe_inputs(inp, core_id))
    return d


def run_fused(x_cores, inp):
    kb, nc = get_prog("fused")
    in_maps = [dict(fused_inputs(inp, c), x_in=x_cores[c]) for c in range(NCORES)]
    res = run_bass_kernel_spmd(nc, in_maps, core_ids=list(range(NCORES)))
    return [r["x_out"] for r in res.results]



_cache = {}


def get_prog(name, **kw):
    key = (name, tuple(sorted(kw.items())))
    if key not in _cache:
        kb, cm = {"ffn0": build_ffn0, "mamba": build_mamba, "attn": build_attn, "moe": build_moe, "fused": build_fused}[name.split(":")[0]](**kw)
        nc = kb.build()
        _cache[key] = (kb, nc)
    return _cache[key]


def ffn0_inputs(inp):
    return {
        "w1t": tr(inp["L0_ffn_W1"]),
        "w2t": tr(inp["L0_ffn_W2"]),
        "b1c": colmaj(inp["L0_ffn_b1"]),
        "b2c": colmaj(inp["L0_ffn_b2"]),
        "onecol": np.ones((128, 1), np.float32),
        "onerow": np.ones((1, 128), np.float32),
    }


def run_ffn0(x_cores, inp):
    kb, nc = get_prog("ffn0")
    base = ffn0_inputs(inp)
    in_maps = [dict(base, x_in=x) for x in x_cores]
    res = run_bass_kernel_spmd(nc, in_maps, core_ids=list(range(NCORES)))
    return [r["x_out"] for r in res.results]





def kernel(**inputs):
    x = np.asarray(inputs["L0_attn_x"], np.float32)
    xs = [np.ascontiguousarray(x[c].T) for c in range(NCORES)]
    xs = run_fused(xs, inputs)
    return np.ascontiguousarray(np.stack([np.asarray(o).T for o in xs])).astype(np.float32)


# version 07 (final)
# speedup vs baseline: 1.3579x; 1.1290x over previous
import sys
sys.path.insert(0, "/opt/trn_rl_repo")
import contextlib
import numpy as np
import ml_dtypes
import concourse.bass as bass
import concourse.mybir as mybir
from concourse.bass import IndirectOffsetOnAxis
from concourse.bass_utils import run_bass_kernel_spmd

F32 = mybir.dt.float32
BF16 = mybir.dt.bfloat16
I32 = mybir.dt.int32
AF = mybir.ActivationFunctionType
ALU = mybir.AluOpType
AX = mybir.AxisListType

T = 2048
DM = 1024
NG = DM // 128
NTT = T // 512
NCORES = 8
KCAP = 352
EROWS = 8 * KCAP

ENGINES = ("sync", "tensor", "vector", "scalar", "gpsimd")
QSEM = {"sync": "Ds", "gpsimd": "Dg", "scalar": "Da"}
MSEM = {"tensor": "T", "vector": "V", "scalar": "S", "gpsimd": "P", "sync": "Y"}


class KB:
    def __init__(self, name):
        self.name = name
        self.nc = bass.Bass("TRN2", target_bir_lowering=False)
        self.es = contextlib.ExitStack()
        self.streams = {e: [] for e in ENGINES}
        self.semh = {}
        self.cnt = {}
        self.needed = {}
        self._sticky = {}
        self._stickyc = {}
        self.dma_uses = {}

    def sem(self, name):
        if name not in self.semh:
            self.semh[name] = self.es.enter_context(self.nc.semaphore(name))
            self.cnt[name] = 0
        return name

    def sbuf(self, name, shape, dt):
        return self.es.enter_context(self.nc.sbuf_tensor(name, shape, dt))

    def psum(self, name, shape, dt=F32):
        return self.es.enter_context(self.nc.psum_tensor(name, shape, dt))

    def din(self, name, shape, dt=F32):
        return self.nc.dram_tensor(name, shape, dt, kind="ExternalInput").ap()

    def dout(self, name, shape, dt=F32):
        return self.nc.dram_tensor(name, shape, dt, kind="ExternalOutput").ap()

    def dtmp(self, name, shape, dt):
        return self.nc.dram_tensor(name, shape, dt).ap()

    notrust = False

    def do(self, eng, fn, *, waits=()):
        seq = len(self.streams[eng]) + 1
        self._register_waits(waits)
        self.streams[eng].append({"fn": fn, "waits": tuple(w for w in waits if w),
                                  "nt": self.notrust})
        return ("@" + eng, seq)

    def dma(self, q, out, in_, *, chan, waits=(), **kw):
        u = self.dma_uses[chan] = self.dma_uses.get(chan, 0) + 1
        sname = "C_" + chan
        self.sem(sname)
        tgt = (sname, 16 * u)

        def fn(e, o=out, i=in_, kw=kw, sem=self.semh[sname]):
            e.dma_start(out=o, in_=i, **kw).then_inc(sem, 16)
        self.do(q, fn, waits=waits)
        return tgt

    def _breg(self, v):
        if not hasattr(self, "_bregs"):
            self._bregs = {}
        if v not in self._bregs:
            slot = {}
            self._bregs[v] = slot

            def prep(e, v=v, slot=slot):
                slot["reg"] = e.to_reg(v)
            self.streams["gpsimd"].append({"fn": None, "raw": prep, "waits": ()})
        return self._bregs[v]

    def idma(self, out, out_idx, in_, in_idx, *, chan, bounds=None, waits=()):
        u = self.dma_uses[chan] = self.dma_uses.get(chan, 0) + 1
        sname = "C_" + chan
        self.sem(sname)
        tgt = (sname, 16 * u)
        slot = self._breg(bounds) if bounds is not None else None

        def fn(e, o=out, oi=out_idx, i=in_, ii=in_idx, s=slot,
               sem=self.semh[sname]):
            oo = IndirectOffsetOnAxis(oi, 0) if oi is not None else None
            io = IndirectOffsetOnAxis(ii, 0) if ii is not None else None
            e.indirect_dma_start(o, oo, i, io,
                                 bounds_check=None if s is None else s["reg"],
                                 oob_is_err=False).then_inc(sem, 16)
        self.do("gpsimd", fn, waits=waits)
        return tgt

    def _register_waits(self, waits):
        for w in waits:
            if w is None:
                continue
            s, v = w
            if s.startswith("@"):
                self.needed.setdefault(s[1:], set()).add(v)
            else:
                self.sem(s)

    def wait(self, eng, waits):
        self._register_waits(waits)
        self.streams[eng].append({"fn": None, "waits": tuple(w for w in waits if w)})

    def val(self, eng):
        eng = {v: k for k, v in MSEM.items()}.get(eng, eng)
        n = len(self.streams[eng])
        if n == 0:
            return None
        return ("@" + eng, n)

    @staticmethod
    def _in_psum(*aps):
        for a in aps:
            try:
                if "PSUM" in str(a.space).upper() or "Psum" in str(a.space):
                    return True
            except Exception:
                pass
        return False

    def psticky(self, tok, *aps):
        return tok

    def mm(self, ps, lhsT, rhs, start, stop, *, waits=()):
        return self.do("tensor",
                       lambda e, o=ps, l=lhsT, r=rhs, s1=start, s2=stop:
                       e.matmul(o, l, r, start=s1, stop=s2), waits=waits)

    def tt(self, eng, out, a, b, op, *, waits=()):
        return self.psticky(self.do(eng, lambda e, o=out, x=a, y=b, p=op:
                            e.tensor_tensor(o, x, y, p), waits=waits), a, b)

    def ts(self, eng, out, a, s1, s2=None, op0=ALU.mult, op1=None, *, waits=()):
        if s2 is None:
            return self.psticky(self.do(eng, lambda e, o=out, x=a, v=s1, p=op0:
                                e.tensor_scalar(o, x, v, None, p), waits=waits), a)
        return self.psticky(self.do(
            eng, lambda e, o=out, x=a, v1=s1, v2=s2, p0=op0, p1=op1:
            e.tensor_scalar(o, x, v1, v2, p0, p1), waits=waits), a)

    def stt(self, eng, out, in0, scalar, in1, op0, op1, *, waits=()):
        return self.psticky(self.do(
            eng, lambda e, o=out, a=in0, s=scalar, b=in1, p0=op0, p1=op1:
            e.scalar_tensor_tensor(o, a, s, b, p0, p1), waits=waits), in0, in1)

    def act(self, out, in_, func, *, bias=None, scale=None, accum=None, waits=()):
        def fn(e, o=out, i=in_, f=func, b=bias, s=scale, ac=accum):
            kw = {}
            if b is not None:
                kw["bias"] = b
            if s is not None:
                kw["scale"] = s
            if ac is not None:
                kw["accum_out"] = ac
            return e.activation(o, i, f, **kw)
        return self.do("scalar", fn, waits=waits)

    def copy(self, eng, out, in_, *, waits=()):
        if eng == "scalar":
            return self.do(eng, lambda e, o=out, i=in_: e.copy(o, i), waits=waits)
        tok = self.do(eng, lambda e, o=out, i=in_: e.tensor_copy(o, i), waits=waits)
        try:
            cast = out.dtype != in_.dtype
        except Exception:
            cast = True
        return self.sticky(tok) if cast else tok

    def sticky(self, tok):
        if tok is not None and tok[0].startswith("@"):
            eng = tok[0][1:]
            self._sticky.setdefault(eng, set()).add(tok[1])
        return tok

    def red(self, out, in_, op, *, absv=None, waits=()):
        return self.sticky(self.do(
            "vector", lambda e, o=out, i=in_, p=op, a=absv:
            e.tensor_reduce(o, i, AX.X, p, apply_absolute_value=a), waits=waits))

    def memset(self, eng, ap, v=0.0, *, waits=()):
        return self.do(eng, lambda e, a=ap, vv=v: e.memset(a, vv), waits=waits)

    def barrier(self):
        marks = [self.do(e, lambda en: en.nop()) for e in ENGINES]
        for e in ENGINES:
            self.wait(e, marks)

    TRUST = ("vector",)

    def build(self):
        needed = {}
        for eng in ENGINES:
            stick = self._sticky.get(eng, set())
            stickc = self._stickyc.get(eng, set())
            for idx, entry in enumerate(self.streams[eng], 1):
                consumer_sticky = idx in stickc
                rw = []
                for w in entry["waits"]:
                    s, v = w
                    if s.startswith("@"):
                        e2 = s[1:]
                        if (e2 == eng and eng in self.TRUST and v not in stick
                                and not consumer_sticky and not entry.get("nt")):
                            continue
                        needed.setdefault(e2, set()).add(v)
                    rw.append(w)
                entry["_rw"] = tuple(rw)
        self.needed = needed
        values = {}
        for eng in ENGINES:
            need = sorted(self.needed.get(eng, ()))
            values[eng] = {s: i + 1 for i, s in enumerate(need)}
            if need:
                self.sem(MSEM[eng])

        def resolve(w):
            s, v = w
            if s.startswith("@"):
                e2 = s[1:]
                return (MSEM[e2], values[e2][v])
            return (s, v)

        def body(e, eng):
            vals = values[eng]
            msem = self.semh.get(MSEM[eng])
            for idx, entry in enumerate(self.streams[eng], 1):
                merged = {}
                for w in entry["_rw"]:
                    sn, v = resolve(w)
                    if v > merged.get(sn, -1):
                        merged[sn] = v
                for sn, v in merged.items():
                    e.wait_ge(self.semh[sn], v)
                if entry.get("raw") is not None:
                    entry["raw"](e)
                elif entry["fn"] is not None:
                    entry["fn"](e)
                if idx in vals:
                    e.drain(fusable=True).then_inc(msem, 1)

        with self.nc.Block() as block:
            for eng in ENGINES:
                getattr(block, eng)(lambda e, eng=eng: body(e, eng))
        self.nc.finalize()
        return self.nc



def np_bf16(x):
    return np.asarray(x).astype(ml_dtypes.bfloat16)


def colmaj(b, p=128):
    b = np.asarray(b, np.float32)
    return np.ascontiguousarray(b.reshape(-1, p).T)


def tr(a):
    return np.ascontiguousarray(np.asarray(a, np.float32).T)



ARENA_NB = 143 * 1024


class Arena:

    def __init__(self, handle, nb=ARENA_NB):
        self.h = handle
        self.nb = nb
        self.off = 0

    def take(self, n, dt, rows=128):
        sz = n * mybir.dt.size(dt)
        off = (self.off + 63) & ~63
        self.off = off + sz
        assert self.off <= self.nb, f"arena overflow: {self.off} > {self.nb}"
        return self.h[0:rows, off:off + sz].bitcast(dt)

    def take_raw(self, nbytes):
        off = (self.off + 63) & ~63
        self.off = off + nbytes
        assert self.off <= self.nb, f"arena overflow: {self.off} > {self.nb}"
        return (self.h, off)

    def reset(self):
        self.off = 0


class Common:

    def __init__(self, kb):
        self.kb = kb
        self.x_in = kb.din("x_in", [DM, T], F32)
        self.x_out = kb.dout("x_out", [DM, T], F32)
        self.onecol_in = kb.din("onecol", [128, 1], F32)
        self.onerow_in = kb.din("onerow", [1, 128], F32)
        self.x = kb.sbuf("x_sb", [128, NG * T], F32)
        self.onecol = kb.sbuf("onecol_sb", [128, 1], F32)[:, :]
        self.onerow = kb.sbuf("onerow_sb", [1, 128], F32)[:, :]
        self.stat_d = kb.dtmp("stat_d", [1, 2 * T], F32)
        self.with_stats = False

    def set_ln(self, ar, *, xn=True, with_stats=False):
        self.sq = ar.take(NG * 512, F32)
        self.ln_tmp = ar.take(512, F32)
        self.rows = ar.take(3 * 512, F32, rows=1)
        if xn:
            self.xn = ar.take(NG * T, BF16)
        self.with_stats = with_stats
        self._ln_prev = None
        self.stat_w = None

    def set_ln_custom(self, sq, ln_tmp, rows, with_stats=False):
        self.sq, self.ln_tmp, self.rows = sq, ln_tmp, rows
        self.with_stats = with_stats
        self._ln_prev = None
        self.stat_w = None

    def load_x(self):
        kb = self.kb
        self.x_loaded = []
        for g in range(NG):
            self.x_loaded.append(
                kb.dma("sync", self.x[:, g * T:(g + 1) * T],
                       self.x_in[g * 128:(g + 1) * 128, :], chan=f"x{g}"))
        self.ones_loaded = [
            kb.dma("scalar", self.onecol[:], self.onecol_in, chan="one_c"),
            kb.dma("scalar", self.onerow[:], self.onerow_in, chan="one_r"),
        ]

    def store_x(self, waits):
        kb = self.kb
        outs = []
        for g in range(NG):
            outs.append(kb.dma("sync", self.x_out[g * 128:(g + 1) * 128, :],
                               self.x[:, g * T:(g + 1) * T], chan="xo",
                               waits=waits if g == 0 else ()))
        kb.wait("sync", outs)

    def emit_ln_tile(self, t, xn_of, extra=()):
        kb = self.kb
        mrow = self.rows[0:1, 0:512]
        rrow = self.rows[0:1, 512:1024]
        trow = self.rows[0:1, 1024:1536]
        prev = getattr(self, "_ln_prev", None)
        mm_last = None
        for g in range(NG):
            xg = self.x[:, g * T + t * 512: g * T + (t + 1) * 512]
            sqg = self.sq[:, g * 512:(g + 1) * 512]
            w = [self.x_loaded[g]] if t == 0 else []
            if g == 0:
                w += [prev] + list(extra)
            sd = kb.act(sqg, xg, AF.Square, waits=w)
            kb.mm(self.ps_s1[0:1, :], self.onecol, xg, g == 0, g == NG - 1,
                  waits=([self.x_loaded[g]] if t == 0 else []) +
                        ([self.ones_loaded[0]] if (t == 0 and g == 0) else []) +
                        ([prev] + list(extra) if g == 0 else []))
            mm_last = kb.mm(self.ps_s2[0:1, :], self.onecol, sqg, g == 0, g == NG - 1,
                            waits=[sd])
        msqrow = self.ln_tmp[0:1, 0:512]
        stat_prev = getattr(self, "stat_w", None) if self.with_stats else None
        m1 = kb.ts("vector", mrow, self.ps_s1[0:1, :], 1.0 / DM,
                   waits=[mm_last] + (stat_prev or []))
        t1 = kb.ts("vector", trow, self.ps_s2[0:1, :], 1.0 / DM)
        sqm = kb.act(msqrow, mrow, AF.Square, waits=[m1])
        vv = kb.tt("vector", trow, trow, msqrow, ALU.subtract, waits=[sqm, t1])
        vv = kb.ts("vector", trow, trow, 1e-5, None, ALU.add, waits=[vv])
        l1 = kb.act(trow, trow, AF.Ln, waits=[vv])
        l2 = kb.act(rrow, trow, AF.Exp, scale=-0.5, waits=[l1])
        bm = kb.mm(self.ps_mb[:], self.onerow, mrow, True, True,
                   waits=[m1, kb.val("V"), self.ones_loaded[1]])
        br = kb.mm(self.ps_rb[:], self.onerow, rrow, True, True, waits=[l2])
        if self.with_stats:
            self.stat_w = [
                kb.dma("gpsimd", self.stat_d[0:1, t * 512:(t + 1) * 512], mrow,
                       chan="stat", waits=[m1]),
                kb.dma("gpsimd", self.stat_d[0:1, T + t * 512:T + (t + 1) * 512],
                       rrow, chan="stat", waits=[l2]),
            ]
        last = None
        for g in range(NG):
            xg = self.x[:, g * T + t * 512: g * T + (t + 1) * 512]
            s = kb.tt("vector", self.ln_tmp[:], xg, self.ps_mb[:], ALU.subtract,
                      waits=[bm, br] + ([last] if last else []))
            last = kb.tt("vector", xn_of(g), self.ln_tmp[:], self.ps_rb[:], ALU.mult,
                         waits=[s])
        self._ln_prev = last
        return last

    def emit_ln(self):
        for t in range(NTT):
            self.emit_ln_tile(
                t, lambda g, t=t: self.xn[:, g * T + t * 512: g * T + (t + 1) * 512])
        self.ln_done = self.kb.val("V")



def make_ctx(name):
    kb = KB(name)
    cm = Common(kb)
    ar = Arena(kb.sbuf("big_arena", [128, ARENA_NB], mybir.dt.uint8))
    ps = {
        "s1": kb.psum("ps_s1", [128, 512]),
        "s2": kb.psum("ps_s2", [128, 512]),
        "mb": kb.psum("ps_mb", [128, 512]),
        "rb": kb.psum("ps_rb", [128, 512]),
        "a": kb.psum("ps_a", [128, 512]),
        "b": kb.psum("ps_b", [128, 512]),
        "f1": kb.psum("ps_f1", [128, 512]),
        "f2": kb.psum("ps_f2", [128, 512]),
    }
    cm.ps_s1, cm.ps_s2 = ps["s1"], ps["s2"]
    cm.ps_mb, cm.ps_rb = ps["mb"], ps["rb"]
    return kb, cm, ar, ps


def emit_ffn0(kb, cm, ar, ps, quant=True):
    if not hasattr(cm, "ffn0_io"):
        cm.ffn0_io = (kb.din("w1t", [DM, 4096], F32), kb.din("w2t", [4096, DM], F32),
                      kb.din("b1c", [128, 32], F32), kb.din("b2c", [128, 8], F32),
                      kb.dtmp("w1q_d", [DM, 4096], BF16),
                      kb.dtmp("w2q_d", [4096, DM], BF16))
    w1t_in, w2t_in, b1_in, b2_in, w1q_d, w2q_d = cm.ffn0_io

    cm.set_ln(ar)
    wq = ar.take(4096, F32)
    sgn = ar.take(4096, BF16)
    msk = ar.take(4096, BF16)
    acc = ar.take(4, F32)
    rcol = ar.take(2, F32)
    h_sb = ar.take(32 * 512, BF16)
    qbf = h_sb
    slab1 = ar.take(2 * 8 * 128, BF16)
    slab2 = ar.take(2 * 32 * 128, BF16)
    b1_sb = ar.take(32, F32)
    b2h_sb = ar.take(8, F32)
    ps_a = ps["a"]
    ps_b = ps["b"]

    bl1 = kb.dma("scalar", b1_sb[:], b1_in, chan="b1")
    bl2 = kb.dma("scalar", b2h_sb[:], b2_in, chan="b2")
    b2h = kb.ts("vector", b2h_sb[:], b2h_sb[:], 0.5, waits=[bl2])

    cm.emit_ln()

    if not quant:
        q1done, q2done = cm.ffn0_qdone
    else:
        q1done, q2done = _emit_ffn0_quant(
            kb, cm, w1t_in, w2t_in, w1q_d, w2q_d, wq, sgn, msk, acc, rcol, qbf)
        cm.ffn0_qdone = (q1done, q2done)
    _emit_ffn0_mm(kb, cm, ps_a, ps_b, slab1, slab2, w1q_d, w2q_d, h_sb,
                  b1_sb, b2h_sb, bl1, b2h, q1done, q2done)


def _emit_ffn0_quant(kb, cm, w1t_in, w2t_in, w1q_d, w2q_d, wq, sgn, msk, acc,
                     rcol, qbf):
    z1 = kb.memset("vector", acc[:, 0:1], 0.0)
    z2 = kb.memset("vector", acc[:, 1:2], 0.0)

    def qpass1(w_in, nch, width, acc_col, zm, prev=None):
        for ch in range(nch):
            buf = wq[:, 0:width]
            ld = kb.dma("sync", buf, w_in[ch * 128:(ch + 1) * 128, :],
                        chan="wq", waits=[prev] if prev else [])
            col = rcol[:, 0:1]
            r = kb.red(col, buf, ALU.add, absv=True, waits=[ld, zm])
            prev = kb.tt("vector", acc_col, acc_col, col, ALU.add, waits=[r])
        return prev

    a1 = qpass1(w1t_in, 8, 4096, acc[:, 0:1], z1)
    a2 = qpass1(w2t_in, 32, 1024, acc[:, 1:2], z2, prev=a1)

    t1 = kb.mm(cm.ps_s1[0:1, 0:1], acc[:, 0:1], cm.onecol, True, True,
               waits=[a1, cm.ones_loaded[0]])
    t2 = kb.mm(cm.ps_s2[0:1, 0:1], acc[:, 1:2], cm.onecol, True, True, waits=[a2])
    r1 = kb.ts("vector", cm.rows[0:1, 0:1], cm.ps_s1[0:1, 0:1],
               0.5 / (4096 * DM), 0.5e-5, ALU.mult, ALU.add, waits=[t1])
    r2 = kb.ts("vector", cm.rows[0:1, 1:2], cm.ps_s2[0:1, 0:1],
               0.5 / (4096 * DM), 0.5e-5, ALU.mult, ALU.add, waits=[t2])
    bt1 = kb.mm(cm.ps_s1[:, 0:1], cm.onerow, cm.rows[0:1, 0:1], True, True,
                waits=[r1, cm.ones_loaded[1]])
    bt2 = kb.mm(cm.ps_s2[:, 0:1], cm.onerow, cm.rows[0:1, 1:2], True, True, waits=[r2])
    c1 = kb.copy("vector", acc[:, 2:3], cm.ps_s1[:, 0:1], waits=[bt1])
    c2 = kb.copy("vector", acc[:, 3:4], cm.ps_s2[:, 0:1], waits=[bt2])

    def qpass2(w_in, wq_d, nch, width, thr_col, thr_done, tag, prior=None):
        st = None
        sts = {0: prior[0] if prior else None, 1: prior[1] if prior else None}
        for ch in range(nch):
            buf = wq[:, 0:width]
            qb = qbf[:, (ch % 2) * 4096:(ch % 2) * 4096 + width]
            w = [kb.val("V"), kb.val("S")]
            ld = kb.dma("sync", buf, w_in[ch * 128:(ch + 1) * 128, :],
                        chan="wq", waits=w)
            sg = kb.act(sgn[:, 0:width], buf, AF.Sign, waits=[ld])
            ab = kb.act(buf, buf, AF.Abs, waits=[sg])
            mk = kb.ts("vector", msk[:, 0:width], buf, thr_col, None,
                       ALU.is_ge, waits=[ab, thr_done])
            qq = kb.tt("vector", qb, sgn[:, 0:width], msk[:, 0:width], ALU.mult,
                       waits=[sg, mk, sts[ch % 2]])
            st = kb.dma("sync", wq_d[ch * 128:(ch + 1) * 128, :], qb,
                        chan=f"{tag}{ch % 2}", waits=[qq])
            sts[ch % 2] = st
        return [sts[0], sts[1]]

    q1done = qpass2(w1t_in, w1q_d, 8, 4096, acc[:, 2:3], c1, "q1s")
    q2done = qpass2(w2t_in, w2q_d, 32, 1024, acc[:, 3:4], c2, "q2s", prior=q1done)
    return q1done, q2done


def _emit_ffn0_mm(kb, cm, ps_a, ps_b, slab1, slab2, w1q_d, w2q_d, h_sb,
                  b1_sb, b2h_sb, bl1, b2h, q1done, q2done):
    sv1 = slab1[:, :].rearrange("p (b kg j) -> p b kg j", b=2, kg=8)
    sv2 = slab2[:, :].rearrange("p (b kg j) -> p b kg j", b=2, kg=32)
    w1qv = w1q_d.rearrange("(kg p) f -> p kg f", p=128)
    w2qv = w2q_d.rearrange("(kg p) f -> p kg f", p=128)
    bank_free = {0: None, 1: None}
    slab_last_use = {}
    for t in range(NTT):
        h_ready = []
        for m in range(32):
            sb = m % 2
            ld = kb.dma("sync", sv1[:, sb], w1qv[:, :, m * 128:(m + 1) * 128],
                        chan=f"sl1{sb}",
                        waits=(q1done if t == 0 else []) +
                              [slab_last_use.get(("1", sb))])
            ps = (ps_a, ps_b)[m % 2]
            mmw = [ld, cm.ln_done if m == 0 else None, bank_free[m % 2]]
            mml = None
            for kg in range(NG):
                mml = kb.mm(ps[:], sv1[:, sb, kg],
                            cm.xn[:, kg * T + t * 512: kg * T + (t + 1) * 512],
                            kg == 0, kg == NG - 1, waits=mmw if kg == 0 else ())
            slab_last_use[("1", sb)] = mml
            ev = kb.act(h_sb[:, m * 512:(m + 1) * 512], ps[:], AF.Relu,
                        bias=b1_sb[:, m:m + 1],
                        waits=[mml] + ([bl1] + q1done + q2done if t == 0 else []))
            bank_free[m % 2] = ev
            h_ready.append(ev)
        for m in range(8):
            sb = m % 2
            ld = kb.dma("sync", sv2[:, sb], w2qv[:, :, m * 128:(m + 1) * 128],
                        chan=f"sl2{sb}",
                        waits=(q2done if t == 0 else []) +
                              [slab_last_use.get(("2", sb))])
            ps = (ps_a, ps_b)[m % 2]
            mml = None
            for kg in range(32):
                mml = kb.mm(ps[:], sv2[:, sb, kg], h_sb[:, kg * 512:(kg + 1) * 512],
                            kg == 0, kg == 31,
                            waits=([ld, bank_free[m % 2]] + h_ready) if kg == 0 else ())
            slab_last_use[("2", sb)] = mml
            tv = kb.ts("vector", cm.ln_tmp[:], ps[:], 0.5, b2h_sb[:, m:m + 1],
                       ALU.mult, ALU.add, waits=[mml, b2h, kb.val("V")])
            xg = cm.x[:, m * T + t * 512: m * T + (t + 1) * 512]
            upd = kb.tt("vector", xg, xg, cm.ln_tmp[:], ALU.add, waits=[tv])
            bank_free[m % 2] = upd



def build_ffn0(debug=False):
    kb, cm, ar, ps = make_ctx("ffn0")
    cm.load_x()
    emit_ffn0(kb, cm, ar, ps)
    cm.store_x([kb.val("V")])
    return kb, cm


DI = 2048
NGI = DI // 128
NST = 16
TC = 512


def emit_mamba(kb, cm, ar, ps):
    ah, abase = ar.take_raw(41 * 1024)
    def carve(off, n, dt, rows=128):
        nb = n * mybir.dt.size(dt)
        return ah[0:rows, abase + off:abase + off + nb].bitcast(dt)
    sq_ap = carve(0, NG * 512, F32)
    zsig_ap = carve(16384, NGI * TC, BF16)
    lntmp_ap = carve(16384 + 8192, 512, F32)
    rows_ap = carve(16384 + 8192 + 2048, 1536, F32, rows=1)
    H = carve(0, TC * NST, BF16)
    convA = carve(32768, 512, F32)
    convB = carve(32768 + 2048, 512, F32)
    y_sb = carve(32768 + 4096, 512, F32)

    cm.set_ln_custom(sq_ap, lntmp_ap, rows_ap)
    xn = ar.take(NG * 512, BF16)

    wino_in = kb.din("wino", [DM, 2 * DI], F32)
    bino_in = kb.din("bino", [128, 32], F32)
    wini_in = kb.din("wini", [DI, 2 * DI], F32)
    convw_in = kb.din("convw", [128, 16 * 4], F32)
    convb_in = kb.din("convb", [128, 16], F32)
    wxp_in = kb.din("wxp", [DI, 160], F32)
    wdt_in = kb.din("wdt", [128, DI], F32)
    bdt_in = kb.din("bdt", [128, 16], F32)
    alog_in = kb.din("alog", [128, 16 * 16], F32)
    dskip_in = kb.din("dskip", [128, 16], F32)
    woi_in = kb.din("woi", [DI, DI], F32)
    woo_in = kb.din("woo", [DI, DM], F32)
    boo_in = kb.din("boo", [128, 8], F32)
    rowsel_in = kb.din("rowsel", [32, 32 * 128], BF16)

    bino = ar.take(32, F32)
    convw = ar.take(64, F32)
    convb = ar.take(16, F32)
    bdt = ar.take(16, F32)
    dskip = ar.take(16, F32)
    boo = ar.take(8, F32)
    A_sb = ar.take(256, F32)
    wxp = ar.take(16 * 160, BF16)
    wdt = ar.take(DI, BF16)
    rowsel = ar.take(32 * 128, BF16, rows=32)
    hcar = ar.take(256, F32)

    xproj = ar.take(NGI * TC, BF16)
    zsig = zsig_ap
    XMW = 516
    xm = ar.take(NGI * XMW, BF16)
    zmsil = ar.take(NGI * TC, BF16)
    xc = ar.take(NGI * TC, BF16)
    dtb = ar.take(TC, BF16)
    BC = ar.take(TC, BF16, rows=32)
    tail = ar.take(16 * 3, BF16)
    delta = convA
    w_sb = convB
    b_sb = carve(39936, 512, F32)
    sptA = y_sb
    a_flat = xm[:, 0:8192].bitcast(F32)

    slab = ar.take(2 * 16 * 128, BF16)
    sv = slab[:, :].rearrange("p (b kg j) -> p b kg j", b=2, kg=16)

    ps_a = ps["a"]
    ps_b = ps["b"]
    ps_bc = ps["f1"][0:32, :]
    ps_d = ps["f2"]

    consts = []
    for name, dst, src in [("bino", bino, bino_in), ("convw", convw, convw_in),
                           ("convb", convb, convb_in), ("bdt", bdt, bdt_in),
                           ("dskip", dskip, dskip_in), ("boo", boo, boo_in),
                           ("rowsel", rowsel, rowsel_in)]:
        consts.append(kb.dma("scalar", dst[:], src, chan="c_" + name))
    wxpl = kb.dma("gpsimd", wxp[:].rearrange("p (kg f) -> p kg f", f=160),
                  wxp_in.rearrange("(kg p) f -> p kg f", p=128), chan="c_wxp")
    wdtl = kb.dma("gpsimd", wdt[:], wdt_in, chan="c_wdt")
    alogl = kb.dma("scalar", A_sb[:], alog_in, chan="c_alog")
    ae = kb.act(A_sb[:], A_sb[:], AF.Exp, waits=[alogl])
    A_done = kb.ts("vector", A_sb[:], A_sb[:], -1.0, waits=[ae])
    hz = kb.memset("vector", hcar[:], 0.0)

    wino_v = wino_in.rearrange("(kg p) f -> p kg f", p=128)
    wini_v = wini_in.rearrange("(kg p) f -> p kg f", p=128)
    woi_v = woi_in.rearrange("(kg p) f -> p kg f", p=128)
    woo_v = woo_in.rearrange("(kg p) f -> p kg f", p=128)

    slab_guard = {0: None, 1: None}
    bank_guard = {0: None, 1: None}
    dbc_guard = None
    dd_guard = None

    state = {"conv_tail": None, "stageB_last": None, "scanH_free": None,
             "stageA_last": None, "lnx": None}

    def stream_mm(st, wview, n_k, m_range, rhs_of, evac, first_waits=(), tag=""):
        toks = []
        for mi, m in enumerate(m_range):
            sb = mi % 2
            ld = kb.dma("gpsimd", sv[:, sb, 0:n_k, :],
                        wview[:, 0:n_k, m * 128:(m + 1) * 128],
                        chan=f"sl{sb}", waits=[slab_guard[sb]])
            ps = (ps_a, ps_b)[mi % 2]
            mml = None
            for kg in range(n_k):
                mml = kb.mm(ps[:], sv[:, sb, kg], rhs_of(kg), kg == 0, kg == n_k - 1,
                            waits=([ld, bank_guard[mi % 2]] + list(first_waits))
                            if kg == 0 else ())
            slab_guard[sb] = mml
            tok = evac(m, ps, mml)
            bank_guard[mi % 2] = tok
            toks.append(tok)
        return toks

    for c in range(NTT):
        cs = slice(c * TC, (c + 1) * TC)
        ln_tok = cm.emit_ln_tile(
            c, lambda g: xn[:, g * 512:(g + 1) * 512],
            extra=[state["stageA_last"], state["scanH_free"],
                   state.get("stageF_last")])
        state["lnx"] = ln_tok

        def evacA(m, ps, mmtok):
            if m < 16:
                return kb.ts("vector", xproj[:, m * TC:(m + 1) * TC], ps[:],
                             bino[:, m:m + 1], None, ALU.add,
                             waits=[mmtok, consts[0], state.get("xproj_free")])
            return kb.act(zsig[:, (m - 16) * TC:(m - 15) * TC], ps[:], AF.Sigmoid,
                          bias=bino[:, m:m + 1],
                          waits=[mmtok, consts[0], state["stageF_last"]]
                          if "stageF_last" in state else [mmtok, consts[0]])
        atoks = stream_mm(state, wino_v, NG, range(32),
                          lambda kg: xn[:, kg * 512:(kg + 1) * 512],
                          evacA, first_waits=[ln_tok], tag="A")
        state["stageA_last"] = atoks[-1]

        tail_toks = []
        for g in range(NGI):
            xg = xm[:, g * XMW:g * XMW + XMW]
            if c == 0:
                tail_toks.append(kb.memset("vector", xg[:, 0:3], 0.0))
            else:
                tail_toks.append(kb.copy("vector", xg[:, 0:3],
                                         tail[:, g * 3:(g + 1) * 3],
                                         waits=[state["tailsv"][g],
                                                state.get("scan_w_done")]))
        def evacB(m, ps, mmtok):
            if m < 16:
                return kb.copy("vector", xm[:, m * XMW + 3:m * XMW + 515], ps[:],
                               waits=[mmtok, tail_toks[m], state.get("scan_w_done")])
            return kb.act(zmsil[:, (m - 16) * TC:(m - 15) * TC], ps[:], AF.Silu,
                          waits=[mmtok, state.get("zmsil_free")])
        btoks = stream_mm(state, wini_v, NGI, range(32),
                          lambda kg: xproj[:, kg * TC:(kg + 1) * TC],
                          evacB, first_waits=atoks, tag="B")
        state["stageB_last"] = btoks[-1]
        tailsv = []
        for g in range(NGI):
            tailsv.append(kb.copy("vector", tail[:, g * 3:(g + 1) * 3],
                                  xm[:, g * XMW + 512:g * XMW + 515],
                                  waits=[btoks[g]] +
                                        ([state["tailsv"][g]] if c else [])))
        state["tailsv"] = tailsv

        convdone = []
        for g in range(NGI):
            xg = xm[:, g * XMW:g * XMW + XMW]
            wof = lambda j: convw[:, g * 4 + j:g * 4 + j + 1]
            t1 = kb.ts("vector", convA[:, :], xg[:, 0:512], wof(0), None, ALU.mult,
                       waits=[btoks[g], tail_toks[g], consts[1],
                              state.get("convA_free")])
            t2 = kb.stt("vector", convB[:, :], xg[:, 1:513], wof(1), convA[:, :],
                        ALU.mult, ALU.add, waits=[t1, state.get("convB_free")])
            t3 = kb.stt("vector", convA[:, :], xg[:, 2:514], wof(2), convB[:, :],
                        ALU.mult, ALU.add, waits=[t2])
            t4 = kb.stt("vector", convB[:, :], xg[:, 3:515], wof(3), convA[:, :],
                        ALU.mult, ALU.add, waits=[t3])
            state["convA_free"] = t4
            xcg = kb.act(xc[:, g * TC:(g + 1) * TC], convB[:, :], AF.Silu,
                         bias=convb[:, g:g + 1],
                         waits=[t4, consts[2], state.get("yb_free", (None,) * 99)[g]
                                if isinstance(state.get("yb_free"), list) else None])
            state["convB_free"] = xcg
            convdone.append(xcg)
        state["convdone"] = convdone

        wxp_v = wxp[:].rearrange("p (kg f) -> p kg f", f=160)
        mmd = mmbc = None
        for kg in range(NGI):
            mmd = kb.mm(ps_a[:], wxp_v[:, kg, 0:128], xc[:, kg * TC:(kg + 1) * TC],
                        kg == 0, kg == NGI - 1,
                        waits=[wxpl, bank_guard[0]] + convdone if kg == 0 else ())
            mmbc = kb.mm(ps_bc[:], wxp_v[:, kg, 128:160], xc[:, kg * TC:(kg + 1) * TC],
                         kg == 0, kg == NGI - 1,
                         waits=[dbc_guard] if kg == 0 else ())
        dtb_t = kb.copy("vector", dtb[:], ps_a[:], waits=[mmd, state.get("dtb_free")])
        bank_guard[0] = dtb_t
        bc_t = kb.act(BC[:], ps_bc[:], AF.Copy, waits=[mmbc, state.get("bc_free")])

        Hv = H.rearrange("p (t n) -> p t n", n=NST)
        Bb_all = xproj[:, 0:NGI * TC]
        CbA = slab[:, 0:8 * TC]
        CbB = xn[:, 0:8 * TC]
        def Cb_slice(n):
            return (CbA if n < 8 else CbB)[:, (n % 8) * TC:(n % 8 + 1) * TC]
        bc_evs = []
        for n in range(32):
            ps = (ps_a, ps_b)[n % 2]
            mmx = kb.mm(ps[:], rowsel[:, n * 128:(n + 1) * 128], BC[:],
                        True, True,
                        waits=[bc_t, consts[6], bank_guard[n % 2],
                               slab_guard[0] if n == 16 else None,
                               slab_guard[1] if n == 16 else None])
            dst = Bb_all[:, n * TC:(n + 1) * TC] if n < 16 else Cb_slice(n - 16)
            ev = kb.copy("scalar", dst, ps[:], waits=[mmx])
            bank_guard[n % 2] = ev
            bc_evs.append(ev)
        bb_done = bc_evs[15]
        cb_done = bc_evs[31]
        yb_free = [None] * NGI
        carries = state.get("carries", [None] * NGI)
        for g in range(NGI):
            mmdl = kb.mm(ps_d[:], wdt[:, g * 128:(g + 1) * 128], dtb[:],
                         True, True, waits=[wdtl, dtb_t, dd_guard])
            bcol = bdt[:, g:g + 1]
            s1 = kb.act(sptA[:, :], ps_d[:], AF.Abs, bias=bcol,
                        waits=[mmdl, consts[3], state.get("sptA_free")])
            s2 = kb.act(sptA[:, :], sptA[:, :], AF.Exp, scale=-1.0, waits=[s1])
            s3 = kb.act(sptA[:, :], sptA[:, :], AF.Ln, bias=1.0, waits=[s2])
            s4 = kb.act(delta[:, :], ps_d[:], AF.Relu, bias=bcol,
                        waits=[s1, state.get("delta_free")])
            dd_guard = s4
            dl = kb.tt("vector", delta[:, :], sptA[:, :], delta[:, :], ALU.add,
                       waits=[s3, s4])
            wg = kb.tt("vector", w_sb[:, :], delta[:, :], xc[:, g * TC:(g + 1) * TC],
                       ALU.mult, waits=[dl, convdone[g]])
            scan_toks = []
            a_guard = state.get("a_war")
            for h in range(2):
                a_done = None
                for j in range(8):
                    n = h * 8 + j
                    a_done = kb.act(a_flat[:, j * 512:(j + 1) * 512], delta[:, :],
                                    AF.Exp, scale=A_sb[:, g * 16 + n:g * 16 + n + 1],
                                    waits=[dl, A_done, a_guard] if j == 0 else ())
                for j in range(8):
                    n = h * 8 + j
                    bn = kb.tt("vector", b_sb[:, :],
                               Bb_all[:, n * TC:(n + 1) * TC], w_sb[:, :], ALU.mult,
                               waits=[bb_done, wg])
                    sc = kb.sticky(kb.do("vector",
                               lambda e, o=Hv[:, :, n], a=a_flat[:, j * 512:(j + 1) * 512],
                               b=b_sb[:, :],
                               ini=hcar[:, g * 16 + n:g * 16 + n + 1]:
                               e.tensor_tensor_scan(o, a, b, ini, ALU.mult, ALU.add),
                               waits=[a_done, bn, hz, state.get("H_free"),
                                      carries[g]]))
                    scan_toks.append(sc)
                a_guard = sc
            state["a_war"] = a_guard
            carry = kb.copy("vector", hcar[:, g * 16:(g + 1) * 16],
                            H[:, (TC - 1) * NST:TC * NST], waits=[scan_toks[-1]])
            carries[g] = carry
            prod_toks = []
            for n in range(NST):
                pr = kb.tt("vector", Hv[:, :, n], Hv[:, :, n], Cb_slice(n), ALU.mult,
                           waits=[cb_done, carry, scan_toks[-1]])
                prod_toks.append(pr)
            yt = kb.red(y_sb[:, :], Hv, ALU.add, waits=prod_toks)
            state["H_free"] = yt
            td = kb.stt("vector", delta[:, :], xc[:, g * TC:(g + 1) * TC],
                        dskip[:, g:g + 1], y_sb[:, :], ALU.mult, ALU.add,
                        waits=[yt, consts[4]])
            ybt = kb.tt("vector", xc[:, g * TC:(g + 1) * TC], delta[:, :],
                        zmsil[:, g * TC:(g + 1) * TC], ALU.mult,
                        waits=[td, btoks[16 + g]])
            yb_free[g] = ybt
            state["sptA_free"] = td
            state["delta_free"] = ybt
            state["lastb_pool"] = prod_toks[-1]
        state["carries"] = carries
        state["dtb_free"] = mmdl
        state["bc_free"] = cb_done
        dbc_guard = bc_t
        state["yb_free"] = yb_free
        state["zmsil_free"] = yb_free[-1]
        state["scan_w_done"] = yb_free[-1]

        slab_guard[0] = state["lastb_pool"]
        slab_guard[1] = state["lastb_pool"]
        def evacF(m, ps, mmtok):
            return kb.tt("vector", xproj[:, m * TC:(m + 1) * TC], ps[:],
                         zsig[:, m * TC:(m + 1) * TC], ALU.mult,
                         waits=[mmtok, atoks[16 + m], state.get("lastb_pool")])
        ftoks = stream_mm(state, woi_v, NGI, range(16),
                          lambda kg: xc[:, kg * TC:(kg + 1) * TC],
                          evacF, first_waits=yb_free, tag="F")
        state["stageF_last"] = ftoks[-1]

        def evacG(m, ps, mmtok):
            xg = cm.x[:, m * T + c * TC: m * T + (c + 1) * TC]
            return kb.stt("vector", xg, ps[:], boo[:, m:m + 1], xg,
                          ALU.add, ALU.add, waits=[mmtok, consts[5]])
        gtoks = stream_mm(state, woo_v, NGI, range(8),
                          lambda kg: xproj[:, kg * TC:(kg + 1) * TC],
                          evacG, first_waits=[ftoks[-1]], tag="G")
        state["scanH_free"] = state["H_free"]
        state["xproj_free"] = gtoks[-1]

    return


def build_mamba(debug=False):
    kb, cm, ar, ps = make_ctx("mamba")
    cm.load_x()
    emit_mamba(kb, cm, ar, ps)
    cm.store_x([kb.val("V")])
    return kb, cm


def mamba_inputs(inp):
    p = "L0_attn_"
    sel = np.zeros((32, 32 * 128), np.float32)
    for n in range(32):
        sel[n, n * 128:(n + 1) * 128] = 1.0
    return {
        "wino": np.asarray(inp[p + "W_in_o"], np.float32),
        "bino": colmaj(inp[p + "b_in_o"]),
        "wini": np.asarray(inp[p + "W_in_i"], np.float32),
        "convw": np.ascontiguousarray(
            np.asarray(inp[p + "conv_w"], np.float32).reshape(16, 128, 4)
            .transpose(1, 0, 2).reshape(128, 64)),
        "convb": colmaj(inp[p + "conv_b"]),
        "wxp": np.asarray(inp[p + "W_xp"], np.float32),
        "wdt": np.asarray(inp[p + "W_dt"], np.float32),
        "bdt": colmaj(inp[p + "b_dt"]),
        "alog": np.ascontiguousarray(
            np.asarray(inp[p + "A_log"], np.float32).reshape(16, 128, 16)
            .transpose(1, 0, 2).reshape(128, 256)),
        "dskip": colmaj(inp[p + "D_skip"]),
        "woi": np.asarray(inp[p + "W_out_i"], np.float32),
        "woo": np.asarray(inp[p + "W_out_o"], np.float32),
        "boo": colmaj(inp[p + "b_out_o"]),
        "rowsel": np_bf16(sel),
        "onecol": np.ones((128, 1), np.float32),
        "onerow": np.ones((1, 128), np.float32),
    }


def run_mamba(x_cores, inp):
    kb, nc = get_prog("mamba")
    base = mamba_inputs(inp)
    in_maps = [dict(base, x_in=x) for x in x_cores]
    res = run_bass_kernel_spmd(nc, in_maps, core_ids=list(range(NCORES)))
    return [r["x_out"] for r in res.results]



def emit_attn(kb, cm, ar, ps):
    wq_in = kb.din("wqT", [DM, DM], F32)
    wk_in = kb.din("wkT", [DM, DM], F32)
    wv_in = kb.din("wvT", [DM, DM], F32)
    wo_in = kb.din("woT", [DM, DM], F32)
    bq_in = kb.din("bq", [128, 8], F32)
    bk_in = kb.din("bk", [128, 8], F32)
    bv_in = kb.din("bvrow", [1, DM], F32)
    bo_in = kb.din("bo", [128, 8], F32)
    mask_in = kb.din("cmask", [128, 128], F32)
    ident_in = kb.din("ident", [128, 128], BF16)
    onerowb_in = kb.din("onerowb", [1, 128], BF16)

    cm.set_ln(ar)
    qT = ar.take(2 * T, BF16)
    kT = ar.take(2 * T, BF16)
    vtm = ar.take(16 * 256, BF16)
    oT = ar.take(NG * T, BF16)
    srow = ar.take(T, F32)
    P = ar.take(T, BF16)
    Pt = ar.take(T, BF16)
    mask = ar.take(128, F32)
    ident = ar.take(128, BF16)
    bq = ar.take(8, F32)
    bk = ar.take(8, F32)
    bo = ar.take(8, F32)
    bvr = ar.take(DM, BF16, rows=1)
    mxc = ar.take(4, F32)
    slabq = ar.take(2 * 8 * 128, BF16)
    slabv = ar.take(8 * 256, BF16)
    svq = slabq[:, :].rearrange("p (b kg j) -> p b kg j", b=2, kg=8)
    svv = slabv[:, :].rearrange("p (kg j) -> p kg j", kg=8)

    ps_a = ps["a"]
    ps_b = ps["b"]
    ps_tr = ps["f2"][:, 0:512].bitcast(BF16)
    ps_o = ps["f1"]

    consts = {}
    for name, dst, src in [("bq", bq, bq_in), ("bk", bk, bk_in), ("bo", bo, bo_in),
                           ("mask", mask, mask_in), ("ident", ident, ident_in)]:
        consts[name] = kb.dma("scalar", dst[:], src, chan="c_" + name)
    consts["bvr"] = kb.dma("gpsimd", bvr[:], bv_in, chan="c_bvr")
    onerowb = ar.take(128, BF16, rows=1)
    consts["orb"] = kb.dma("scalar", onerowb[:], onerowb_in, chan="c_orb")
    bqs = kb.ts("vector", bq[:], bq[:], 0.125, waits=[consts["bq"]])

    cm.emit_ln()

    wq_v = wq_in.rearrange("(kg p) f -> p kg f", p=128)
    wk_v = wk_in.rearrange("(kg p) f -> p kg f", p=128)
    wv_v = wv_in.rearrange("(kg p) f -> p kg f", p=128)
    wo_v = wo_in.rearrange("(kg p) f -> p kg f", p=128)

    state = {}
    slab_guard = {0: None, 1: None}
    bank_guard = {0: None, 1: None}
    tr_guard = None
    o_guard = None

    def qk_build(w_v, dst, bias_sb, scale, hg, chan, extra=()):
        toks = []
        for mq in range(2):
            m = hg * 2 + mq
            sb = mq % 2
            bias_col = bias_sb[:, m:m + 1]
            ld = kb.dma("gpsimd", svq[:, sb], w_v[:, :, m * 128:(m + 1) * 128],
                        chan=f"{chan}{sb}", waits=[slab_guard[sb]])
            for t in range(NTT):
                ps = (ps_a, ps_b)[t % 2]
                mml = None
                for kg in range(NG):
                    mml = kb.mm(ps[:], svq[:, sb, kg],
                                cm.xn[:, kg * T + t * 512: kg * T + (t + 1) * 512],
                                kg == 0, kg == NG - 1,
                                waits=[ld, cm.ln_done, bank_guard[t % 2]] + list(extra)
                                if kg == 0 else ())
                if scale is None:
                    ev = kb.ts("vector", dst[:, mq * T + t * 512: mq * T + (t + 1) * 512],
                               ps[:], bias_col, None, ALU.add, waits=[mml])
                else:
                    ev = kb.ts("vector", dst[:, mq * T + t * 512: mq * T + (t + 1) * 512],
                               ps[:], scale, bias_col, ALU.mult, ALU.add, waits=[mml])
                bank_guard[t % 2] = ev
                toks.append(ev)
            slab_guard[sb] = mml
        return toks

    for hg in range(4):
        qtok = qk_build(wq_v, qT, bq, 0.125, hg, "slq", extra=[bqs])
        ktok = qk_build(wk_v, kT, bk, None, hg, "slk", extra=[consts["bk"]])
        vl = kb.dma("gpsimd", svv[:, :, :], wv_v[:, :, hg * 256:(hg + 1) * 256],
                    chan="slv", waits=[state.get("v_slab_free")])
        vtoks = []
        for tt_ in range(16):
            mml = None
            for kg in range(NG):
                mml = kb.mm(ps_a[:, 0:256], cm.xn[:, kg * T + tt_ * 128: kg * T + (tt_ + 1) * 128],
                            svv[:, kg], kg == 0, False,
                            waits=[vl, cm.ln_done, bank_guard[0]] if kg == 0 else ())
            mml = kb.mm(ps_a[:, 0:256], onerowb[:], bvr[0:1, hg * 256:(hg + 1) * 256],
                        False, True, waits=[consts["bvr"], consts["orb"]])
            ev = kb.copy("vector", vtm[:, tt_ * 256:(tt_ + 1) * 256], ps_a[:, 0:256],
                         waits=[mml])
            bank_guard[0] = ev
            vtoks.append(ev)
        state["v_slab_free"] = mml

        for hl in range(4):
            h = hg * 4 + hl
            qbase = (hl % 2) * 64
            qhalf = (hl // 2) * T
            for qc in range(16):
                L = (qc + 1) * 128
                nkt = (L + 511) // 512
                ev_toks = []
                for kt in range(nkt):
                    wdt_ = min(512, L - kt * 512)
                    ps = (ps_a, ps_b)[kt % 2]
                    mml = kb.mm(ps[:, 0:wdt_],
                                qT[qbase:qbase + 64, qhalf + qc * 128: qhalf + (qc + 1) * 128],
                                kT[qbase:qbase + 64, qhalf + kt * 512: qhalf + kt * 512 + wdt_],
                                True, True,
                                waits=[qtok[-1], ktok[-1], bank_guard[kt % 2]])
                    ev = kb.copy("vector", srow[:, kt * 512:kt * 512 + wdt_],
                                 ps[:, 0:wdt_], waits=[mml, state.get("srow_free")])
                    bank_guard[kt % 2] = ev
                    ev_toks.append(ev)
                mk_ = kb.tt("vector", srow[:, L - 128:L], srow[:, L - 128:L], mask[:],
                            ALU.add, waits=[consts["mask"], ev_toks[-1]])
                mx = kb.red(mxc[:, 0:1], srow[:, 0:L], ALU.max,
                            waits=[mk_] + ev_toks)
                ng = kb.ts("vector", mxc[:, 1:2], mxc[:, 0:1], -1.0, waits=[mx])
                pe_ = kb.act(P[:, 0:L], srow[:, 0:L], AF.Exp, bias=mxc[:, 1:2],
                             accum=mxc[:, 2:3], waits=[ng, state.get("P_free")])
                rc_ = kb.sticky(kb.do("vector", lambda e, o=mxc[:, 3:4], i=mxc[:, 2:3]:
                            e.reciprocal(o, i), waits=[pe_]))
                pn = kb.ts("vector", P[:, 0:L], P[:, 0:L], mxc[:, 3:4], None,
                           ALU.mult, waits=[rc_])
                pt_toks = []
                for k0 in range(0, qc + 1, 8):
                    k1 = min(k0 + 8, qc + 1)
                    tr = None
                    for ktile in range(k0, k1):
                        tr = kb.do("tensor", lambda e,
                                   o=ps_tr[:, (ktile - k0) * 128:(ktile - k0 + 1) * 128],
                                   i=P[:, ktile * 128:(ktile + 1) * 128], idn=ident[:]:
                                   e.transpose(o, i, idn),
                                   waits=[pn, consts["ident"], tr_guard]
                                   if ktile == k0 else ())
                    tev = kb.copy("scalar", Pt[:, k0 * 128:k1 * 128],
                                  ps_tr[:, 0:(k1 - k0) * 128],
                                  waits=[tr, state.get("Pt_free")])
                    tr_guard = tev
                    pt_toks.append(tev)
                mml = None
                for ktile in range(qc + 1):
                    mml = kb.mm(ps_o[0:64, 0:128],
                                vtm[:, ktile * 256 + hl * 64: ktile * 256 + hl * 64 + 64],
                                Pt[:, ktile * 128:(ktile + 1) * 128],
                                ktile == 0, ktile == qc,
                                waits=[pt_toks[ktile // 8], vtoks[ktile],
                                       o_guard if ktile == 0 else None])
                g_o = h // 2
                off_o = (h % 2) * 64
                oev = kb.copy("vector",
                              oT[off_o:off_o + 64, g_o * T + qc * 128: g_o * T + (qc + 1) * 128],
                              ps_o[0:64, 0:128], waits=[mml])
                o_guard = oev
                state["srow_free"] = pe_
                state["P_free"] = tr
                state["Pt_free"] = mml
    o_done = kb.val("V")

    for m in range(8):
        sb = m % 2
        ld = kb.dma("gpsimd", svq[:, sb], wo_v[:, :, m * 128:(m + 1) * 128],
                    chan=f"slo{sb}", waits=[slab_guard[sb]])
        for t in range(NTT):
            ps = (ps_a, ps_b)[t % 2]
            mml = None
            for kg in range(NG):
                mml = kb.mm(ps[:], svq[:, sb, kg],
                            oT[:, kg * T + t * 512: kg * T + (t + 1) * 512],
                            kg == 0, kg == NG - 1,
                            waits=[ld, o_done, bank_guard[t % 2]] if kg == 0 else ())
            xg = cm.x[:, m * T + t * 512: m * T + (t + 1) * 512]
            ev = kb.stt("vector", xg, ps[:], bo[:, m:m + 1], xg, ALU.add, ALU.add,
                        waits=[mml, consts["bo"]])
            bank_guard[t % 2] = ev
        slab_guard[sb] = mml
    return


def build_attn(debug=False):
    kb, cm, ar, ps = make_ctx("attn")
    cm.load_x()
    emit_attn(kb, cm, ar, ps)
    cm.store_x([kb.val("V")])
    return kb, cm


def attn_inputs(inp):
    mask = np.zeros((128, 128), np.float32)
    mask[np.triu_indices(128, 1)] = -1e30
    return {
        "onerowb": np_bf16(np.ones((1, 128))),
        "wqT": tr(inp["L1_attn_Wq"]), "wkT": tr(inp["L1_attn_Wk"]),
        "wvT": tr(inp["L1_attn_Wv"]), "woT": tr(inp["L1_attn_Wo"]),
        "bq": colmaj(inp["L1_attn_bq"]), "bk": colmaj(inp["L1_attn_bk"]),
        "bvrow": np.asarray(inp["L1_attn_bv"], np.float32).reshape(1, DM),
        "bo": colmaj(inp["L1_attn_bo"]),
        "cmask": mask, "ident": np_bf16(np.eye(128)),
        "onecol": np.ones((128, 1), np.float32),
        "onerow": np.ones((1, 128), np.float32),
    }


def run_attn(x_cores, inp):
    kb, nc = get_prog("attn")
    base = attn_inputs(inp)
    in_maps = [dict(base, x_in=x) for x in x_cores]
    res = run_bass_kernel_spmd(nc, in_maps, core_ids=list(range(NCORES)))
    return [r["x_out"] for r in res.results]



def emit_moe(kb, cm, ar, ps, sim1=False):
    kb.notrust = True
    ah, abase = ar.take_raw(44 * 1024)
    def carve(off, n, dt, rows=128):
        nb = n * mybir.dt.size(dt)
        return ah[0:rows, abase + off:abase + off + nb].bitcast(dt)
    sq_ap = carve(0, NG * 512, F32)
    Gfm = carve(0, NG * EROWS, BF16)
    lntmp = ar.take(512, F32)
    rows_ap = ar.take(1536, F32, rows=1)
    cm.set_ln_custom(sq_ap, lntmp, rows_ap, with_stats=True)
    cm.xn = ar.take(NG * T, BF16)

    if not hasattr(cm, "moe_io"):
        cm.moe_io = (kb.din("routerw", [DM, 8], F32),
                     kb.din("moew1", [8, DM, 4096], F32),
                     kb.din("moeb1", [128, 8 * 32], F32),
                     kb.din("moew2", [8, 4096, DM], F32),
                     kb.din("moeb2", [128, 8 * 8], F32),
                     kb.din("presel", [8, 1], F32),
                     kb.din("cumt", [128, 128], F32),
                     kb.din("e8", [128, 8], F32),
                     kb.din("identb", [128, 128], BF16),
                     kb.din("identf", [128, 128], F32),
                     kb.dtmp("G_d", [EROWS, DM], BF16),
                     kb.dtmp("Go_d", [EROWS, DM], BF16),
                     kb.dtmp("cc_in", [1, 8], F32),
                     kb.dtmp("cc_out", [8, 8], F32))
    (rw_in, w1_in, b1_in, w2_in, b2_in, presel_in, cumt_in, e8_in, ident_in,
     identf_in, G_d, Go_d, cc_in, cc_out) = cm.moe_io
    h_sb = ar.take(32 * KCAP, BF16)
    xntm = ar.take(DM, BF16)
    moetm = ar.take(DM, BF16)
    gtm = ar.take(DM, BF16)
    slab1 = ar.take(2 * 8 * 128, BF16)
    slab2 = ar.take(2 * 32 * 128, BF16)
    sv1 = slab1[:, :].rearrange("p (b kg j) -> p b kg j", b=2, kg=8)
    sv2 = slab2[:, :].rearrange("p (b kg j) -> p b kg j", b=2, kg=32)
    rw_sb = ar.take(8 * 8, F32)
    sw_sb = ar.take(8, F32)
    stat16 = ar.take(2 * 128, F32, rows=16)
    mtm = ar.take(16, F32)
    rtm = ar.take(16, F32)
    logit = ar.take(8, F32)
    emask = ar.take(16 * 8, F32)
    cum = ar.take(16 * 8, F32)
    gate = ar.take(16, F32)
    gk = ar.take(16, F32)
    growi = ar.take(16, I32)
    runrow = ar.take(16, F32, rows=1)
    ag_sb = ar.take(8, F32, rows=8)
    offs128 = ar.take(8, F32)
    scr = ar.take(8 * 4, F32)
    cumt = ar.take(128, F32)
    e8 = ar.take(8, F32)
    identb = ar.take(128, BF16)
    identf = ar.take(128, F32)
    presel = ar.take(1, F32, rows=8)
    b1_sb = ar.take(8 * 32, F32)
    b2_sb = ar.take(8 * 8, F32)

    ps_a = ps["a"]
    ps_b = ps["b"]
    ps_c = ps["f1"]
    ps_tr = ps["f2"][:, 0:256].bitcast(BF16)

    consts = {}
    rwload = (rw_sb[:].rearrange("p (kg e) -> p kg e", e=8),
              rw_in.rearrange("(kg p) e -> p kg e", p=128))
    for name, dst, src in [("rw", *rwload),
                           ("cumt", cumt, cumt_in), ("e8", e8, e8_in),
                           ("identb", identb, ident_in), ("identf", identf, identf_in),
                           ("presel", presel, presel_in), ("b1", b1_sb, b1_in),
                           ("b2", b2_sb, b2_in)]:
        consts[name] = kb.dma("scalar", dst if name == "rw" else dst[:], src,
                              chan="c_" + name)
    zz = kb.memset("vector", ag_sb[:], 0.0)
    rwv = rw_sb[:].rearrange("p (kg e) -> p kg e", e=8)

    gz = kb.memset("vector", gtm[:], 0.0)
    gzt = []
    for st in range(EROWS // 128):
        gzt.append(kb.dma("sync", G_d[st * 128:(st + 1) * 128, :], gtm[:],
                          chan="gz", waits=[gz]))

    cm.emit_ln()

    swm = None
    for kg in range(NG):
        swm = kb.mm(ps_c[0:1, 0:8], cm.onecol, rwv[:, kg], kg == 0, kg == NG - 1,
                    waits=[consts["rw"], cm.ones_loaded[0]] if kg == 0 else ())
    swr = kb.copy("vector", runrow[0:1, 8:16], ps_c[0:1, 0:8], waits=[swm])
    swb = kb.mm(ps_c[:, 0:8], cm.onerow, runrow[0:1, 8:16], True, True,
                waits=[swr, cm.ones_loaded[1]])
    swe = kb.copy("vector", sw_sb[:], ps_c[:, 0:8], waits=[swb])

    stl = kb.dma("sync", stat16[:].rearrange("t (h p) -> t h p", h=2),
                 cm.stat_d.rearrange("o (h tt p) -> (o tt) h p", p=128, h=2),
                 chan="statl", waits=cm.stat_w)
    t_m = kb.do("tensor", lambda e, o=ps_a[:, 0:16], i=stat16[0:16, 0:128],
                idn=identf[0:16, 0:16]: e.transpose(o, i, idn),
                waits=[stl, consts["identf"]])
    ev_m = kb.copy("vector", mtm[:], ps_a[:, 0:16], waits=[t_m])
    t_r = kb.do("tensor", lambda e, o=ps_a[:, 16:32], i=stat16[0:16, 128:256],
                idn=identf[0:16, 0:16]: e.transpose(o, i, idn),
                waits=[stl, ev_m])
    ev_r = kb.copy("vector", rtm[:], ps_a[:, 16:32], waits=[t_r])

    zrun = kb.memset("vector", runrow[0:1, 0:8], 0.0)
    emv = emask[:].rearrange("p (tt e) -> p tt e", e=8)
    cuv = cum[:].rearrange("p (tt e) -> p tt e", e=8)
    rtoks = []
    run_tok = zrun
    prev_ps = None
    for tt_ in range(16):
        mml = None
        for kg in range(NG):
            mml = kb.mm(ps_b[:, 0:8], cm.x[:, kg * T + tt_ * 128:kg * T + (tt_ + 1) * 128],
                        rwv[:, kg], kg == 0, kg == NG - 1,
                        waits=[consts["rw"], cm.ln_done, prev_ps] if kg == 0 else ())
        s1 = kb.ts("vector", scr[:, 0:8], sw_sb[:], mtm[:, tt_:tt_ + 1], None,
                   ALU.mult, waits=[swe, ev_m, state_none(rtoks)])
        s2 = kb.tt("vector", scr[:, 0:8], ps_b[:, 0:8], scr[:, 0:8], ALU.subtract,
                   waits=[mml, s1])
        prev_ps = s2
        lg = kb.ts("vector", logit[:], scr[:, 0:8], rtm[:, tt_:tt_ + 1], None,
                   ALU.mult, waits=[s2, ev_r, rtoks[-1][0] if rtoks else None])
        mx = kb.red(scr[:, 8:9], logit[:], ALU.max, waits=[lg])
        ng = kb.ts("vector", scr[:, 9:10], scr[:, 8:9], -1.0, waits=[mx])
        pe_ = kb.act(scr[:, 16:24], logit[:], AF.Exp, bias=scr[:, 9:10],
                     accum=scr[:, 10:11], waits=[ng])
        rc = kb.sticky(kb.do("vector", lambda e, o=gate[:, tt_:tt_ + 1], i=scr[:, 10:11]:
                   e.reciprocal(o, i), waits=[pe_]))
        em = kb.ts("vector", emv[:, tt_], logit[:], scr[:, 8:9], None, ALU.is_ge,
                   waits=[mx, lg])
        c1 = kb.mm(ps_c[:, 0:8], cumt[:], emv[:, tt_], True, False,
                   waits=[em, consts["cumt"], prev_cum_ev(rtoks), run_tok, swe])
        c2 = kb.mm(ps_c[:, 0:8], cm.onerow, runrow[0:1, 0:8], False, True,
                   waits=[run_tok])
        cv = kb.copy("vector", cuv[:, tt_], ps_c[:, 0:8], waits=[c2])
        c3 = kb.mm(ps_c[0:1, 16:24], cm.onecol, emv[:, tt_], True, True,
                   waits=[em, cv])
        tr_ = kb.copy("vector", runrow[0:1, 8:16], ps_c[0:1, 16:24], waits=[c3, swe])
        run_tok = kb.tt("vector", runrow[0:1, 0:8], runrow[0:1, 0:8],
                        runrow[0:1, 8:16], ALU.add, waits=[tr_])
        rtoks.append((em, cv))
    ccw = kb.dma("gpsimd", cc_in, runrow[0:1, 0:8], chan="ccw", waits=[run_tok])
    groups = [[0]] if sim1 else [list(range(NCORES))]
    kb.sem("C_cc")
    ccn = kb.dma_uses["cc"] = kb.dma_uses.get("cc", 0) + 1
    cc_out_ap = cc_out[0:1, :] if sim1 else cc_out

    def ccfn(e, g=groups, sem=kb.semh["C_cc"], oap=cc_out_ap):
        e.collective_compute("AllGather", ALU.bypass, replica_groups=g,
                             ins=[cc_in], outs=[oap]).then_inc(sem, 1)
    kb.do("gpsimd", ccfn, waits=[ccw])
    cctok = ("C_cc", ccn)
    agl = kb.dma("gpsimd", ag_sb[0:1 if sim1 else 8, :],
                 cc_out[0:1 if sim1 else 8, :], chan="agl", waits=[cctok, zz])
    of1 = kb.mm(ps_c[0:1, 0:8], presel[:], ag_sb[:], True, True,
                waits=[agl, consts["presel"], run_tok])
    of2 = kb.copy("vector", runrow[0:1, 8:16], ps_c[0:1, 0:8], waits=[of1, run_tok])
    of3 = kb.mm(ps_c[:, 0:8], cm.onerow, runrow[0:1, 8:16], True, True, waits=[of2])
    of4 = kb.copy("vector", offs128[:], ps_c[:, 0:8], waits=[of3])

    gtoks = []
    for tt_ in range(16):
        g1 = kb.tt("vector", scr[:, 0:8], cuv[:, tt_], offs128[:], ALU.add,
                   waits=[of4, rtoks[tt_][1]] + ([gtoks[-1]] if gtoks else []))
        g2 = kb.tt("vector", scr[:, 0:8], scr[:, 0:8], emv[:, tt_], ALU.mult,
                   waits=[g1])
        g3 = kb.red(scr[:, 8:9], scr[:, 0:8], ALU.add, waits=[g2])
        g4 = kb.ts("vector", scr[:, 9:10], scr[:, 8:9], float(T * NCORES // 8), None,
                   ALU.is_le, waits=[g3])
        g5 = kb.tt("vector", scr[:, 0:8], cuv[:, tt_], emv[:, tt_], ALU.mult,
                   waits=[g2])
        g6 = kb.red(scr[:, 10:11], scr[:, 0:8], ALU.add, waits=[g5])
        g7 = kb.ts("vector", scr[:, 11:12], scr[:, 10:11], -1.0, None, ALU.add,
                   waits=[g6])
        g8 = kb.ts("vector", scr[:, 12:13], scr[:, 11:12], float(KCAP - 1), None,
                   ALU.is_le, waits=[g7])
        g9 = kb.tt("vector", scr[:, 0:8], emv[:, tt_], e8[:], ALU.mult,
                   waits=[g5, consts["e8"]])
        g10 = kb.red(scr[:, 13:14], scr[:, 0:8], ALU.add, waits=[g9])
        g11 = kb.stt("vector", scr[:, 14:15], scr[:, 13:14], float(KCAP),
                     scr[:, 11:12], ALU.mult, ALU.add, waits=[g10, g7])
        g12 = kb.ts("vector", scr[:, 15:16], scr[:, 12:13], -1e6, 1e6,
                    ALU.mult, ALU.add, waits=[g8])
        g13 = kb.tt("vector", scr[:, 14:15], scr[:, 14:15], scr[:, 15:16],
                    ALU.add, waits=[g11, g12])
        g14 = kb.copy("vector", growi[:, tt_:tt_ + 1], scr[:, 14:15], waits=[g13])
        g15 = kb.tt("vector", gk[:, tt_:tt_ + 1], gate[:, tt_:tt_ + 1],
                    scr[:, 9:10], ALU.mult, waits=[g4])
        g16 = kb.ts("vector", gk[:, tt_:tt_ + 1], gk[:, tt_:tt_ + 1], 0.5,
                    waits=[g15])
        gtoks.append(g14)
        gtoks.append(g16)

    tr_guard = None
    sc_toks = []
    prev_sc = None
    for tt_ in range(16):
        evs = []
        for kg in range(NG):
            trx = kb.do("tensor", lambda e, o=ps_tr[:, 0:128],
                        i=cm.xn[:, kg * T + tt_ * 128:kg * T + (tt_ + 1) * 128],
                        idn=identb[:]: e.transpose(o, i, idn),
                        waits=[cm.ln_done, consts["identb"], tr_guard])
            ev = kb.copy("scalar", xntm[:, kg * 128:(kg + 1) * 128], ps_tr[:, 0:128],
                         waits=[trx, prev_sc])
            tr_guard = ev
            evs.append(ev)
        sc = kb.idma(G_d, growi[:, tt_:tt_ + 1], xntm[:], None, chan="scat",
                     bounds=EROWS - 1,
                     waits=evs + [gtoks[2 * tt_]] + gzt)
        prev_sc = sc
        sc_toks.append(sc)

    gl_toks = []
    gfm_evs = []
    for st in range(EROWS // 128):
        gl = kb.dma("sync", gtm[:], G_d[st * 128:(st + 1) * 128, :], chan="gl",
                    waits=sc_toks + [gz] + ([gfm_evs[-1]] if gfm_evs else []))
        for kg in range(NG):
            trx = kb.do("tensor", lambda e, o=ps_tr[:, 0:128],
                        i=gtm[:, kg * 128:(kg + 1) * 128], idn=identb[:]:
                        e.transpose(o, i, idn), waits=[gl, tr_guard])
            ev = kb.copy("scalar", Gfm[:, kg * EROWS + st * 128:kg * EROWS + (st + 1) * 128],
                         ps_tr[:, 0:128], waits=[trx])
            tr_guard = ev
            gfm_evs.append(ev)
    gfm_done = gfm_evs[-1]

    w1v = [w1_in[e].rearrange("(kg p) f -> p kg f", p=128) for e in range(8)]
    w2v = [w2_in[e].rearrange("(kg p) f -> p kg f", p=128) for e in range(8)]
    slab_guard = {("1", 0): None, ("1", 1): None, ("2", 0): None, ("2", 1): None}
    bank_guard = {0: None, 1: None}
    hv = h_sb[:].rearrange("p (kg j) -> p kg j", j=KCAP)
    out_evs = []
    for e_ in range(8):
        esl = slice(e_ * KCAP, (e_ + 1) * KCAP)
        h_toks = []
        for m in range(32):
            sb = m % 2
            ld = kb.dma("gpsimd", sv1[:, sb], w1v[e_][:, :, m * 128:(m + 1) * 128],
                        chan=f"m1{sb}", waits=[slab_guard[("1", sb)]])
            ps = (ps_a, ps_b)[m % 2]
            mml = None
            for kg in range(NG):
                mml = kb.mm(ps[:, 0:KCAP], sv1[:, sb, kg],
                            Gfm[:, kg * EROWS + e_ * KCAP:kg * EROWS + (e_ + 1) * KCAP],
                            kg == 0, kg == NG - 1,
                            waits=[ld, gfm_done, bank_guard[m % 2]] +
                                  (out_evs if m == 0 and kg == 0 else [])
                            if kg == 0 else ())
            slab_guard[("1", sb)] = mml
            ev = kb.act(hv[:, m], ps[:, 0:KCAP], AF.Relu,
                        bias=b1_sb[:, e_ * 32 + m:e_ * 32 + m + 1],
                        waits=[mml, consts["b1"], state_h_free(out_evs, h_toks)])
            bank_guard[m % 2] = ev
            h_toks.append(ev)
        out_evs = []
        for m in range(8):
            sb = m % 2
            ld = kb.dma("gpsimd", sv2[:, sb], w2v[e_][:, :, m * 128:(m + 1) * 128],
                        chan=f"m2{sb}", waits=[slab_guard[("2", sb)]])
            ps = (ps_a, ps_b)[m % 2]
            mml = None
            for kg in range(32):
                mml = kb.mm(ps[:, 0:KCAP], sv2[:, sb, kg], hv[:, kg],
                            kg == 0, kg == 31,
                            waits=([ld, bank_guard[m % 2]] + h_toks)
                            if kg == 0 else ())
            slab_guard[("2", sb)] = mml
            ev = kb.ts("vector", Gfm[:, m * EROWS + e_ * KCAP:m * EROWS + (e_ + 1) * KCAP],
                       ps[:, 0:KCAP], b2_sb[:, e_ * 8 + m:e_ * 8 + m + 1], None,
                       ALU.add, waits=[mml, consts["b2"], h_toks[-1]])
            bank_guard[m % 2] = ev
            out_evs.append(ev)

    go_st = []
    for st in range(EROWS // 128):
        evs = []
        for kg in range(NG):
            trx = kb.do("tensor", lambda e, o=ps_tr[:, 0:128],
                        i=Gfm[:, kg * EROWS + st * 128:kg * EROWS + (st + 1) * 128],
                        idn=identb[:]: e.transpose(o, i, idn),
                        waits=[tr_guard] + out_evs)
            ev = kb.copy("scalar", gtm[:, kg * 128:(kg + 1) * 128], ps_tr[:, 0:128],
                         waits=[trx, go_st[-1] if go_st else None])
            tr_guard = ev
            evs.append(ev)
        go_st.append(kb.dma("sync", Go_d[st * 128:(st + 1) * 128, :], gtm[:],
                            chan="gost", waits=evs))

    prev_add = None
    for tt_ in range(16):
        mz = kb.memset("vector", moetm[:], 0.0, waits=[prev_add])
        ga = kb.idma(moetm[:], None, Go_d, growi[:, tt_:tt_ + 1], chan="gab",
                     bounds=EROWS - 1, waits=[mz] + go_st + [gtoks[2 * tt_]])
        gm = kb.ts("vector", moetm[:], moetm[:], gk[:, tt_:tt_ + 1], None,
                   ALU.mult, waits=[ga, gtoks[2 * tt_ + 1]])
        for kg in range(NG):
            trx = kb.do("tensor", lambda e, o=ps_tr[:, 0:128],
                        i=moetm[:, kg * 128:(kg + 1) * 128], idn=identb[:]:
                        e.transpose(o, i, idn), waits=[gm, tr_guard])
            xg = cm.x[:, kg * T + tt_ * 128:kg * T + (tt_ + 1) * 128]
            prev_add = kb.tt("vector", xg, xg, ps_tr[:, 0:128], ALU.add,
                             waits=[trx])
            tr_guard = prev_add
    kb.notrust = False
    return


def build_moe(sim1=False, debug=False):
    kb, cm, ar, ps = make_ctx("moe")
    cm.load_x()
    emit_moe(kb, cm, ar, ps, sim1=sim1)
    cm.store_x([kb.val("V")])
    return kb, cm


def state_none(rtoks):
    return rtoks[-1][0] if rtoks else None


def prev_cum_ev(rtoks):
    return rtoks[-1][1] if rtoks else None


def state_h_free(out_evs, h_toks):
    return h_toks[-1] if h_toks else (out_evs[-1] if out_evs else None)


def moe_inputs(inp, core_id):
    presel = np.zeros((8, 1), np.float32)
    presel[:core_id] = 1.0
    b1 = np.concatenate([colmaj(inp["L1_ffn_b1"][e]) for e in range(8)], axis=1)
    b2 = np.concatenate([colmaj(inp["L1_ffn_b2"][e]) for e in range(8)], axis=1)
    return {
        "routerw": np.asarray(inp["L1_ffn_router_w"], np.float32),
        "moew1": np.asarray(inp["L1_ffn_w1"], np.float32),
        "moeb1": b1,
        "moew2": np.asarray(inp["L1_ffn_w2"], np.float32),
        "moeb2": b2,
        "presel": presel,
        "cumt": np.triu(np.ones((128, 128), np.float32)),
        "e8": np.broadcast_to(np.arange(8, dtype=np.float32), (128, 8)).copy(),
        "identb": np_bf16(np.eye(128)),
        "identf": np.eye(128, dtype=np.float32),
        "onecol": np.ones((128, 1), np.float32),
        "onerow": np.ones((1, 128), np.float32),
    }


def run_moe(x_cores, inp):
    kb, nc = get_prog("moe")
    in_maps = [dict(moe_inputs(inp, c), x_in=x_cores[c]) for c in range(NCORES)]
    res = run_bass_kernel_spmd(nc, in_maps, core_ids=list(range(NCORES)))
    return [r["x_out"] for r in res.results]



def build_fused(sim1=False):
    kb, cm, ar, ps = make_ctx("fused")
    cm.load_x()
    emit_ffn0(kb, cm, ar, ps, quant=True)
    kb.barrier(); ar.reset()
    emit_mamba(kb, cm, ar, ps)
    kb.barrier(); ar.reset()
    emit_ffn0(kb, cm, ar, ps, quant=False)
    kb.barrier(); ar.reset()
    emit_moe(kb, cm, ar, ps, sim1=sim1)
    kb.barrier(); ar.reset()
    emit_attn(kb, cm, ar, ps)
    kb.barrier(); ar.reset()
    emit_moe(kb, cm, ar, ps, sim1=sim1)
    cm.store_x([kb.val("V")])
    return kb, cm


def fused_inputs(inp, core_id):
    d = {}
    d.update(ffn0_inputs(inp))
    d.update(mamba_inputs(inp))
    d.update(attn_inputs(inp))
    d.update(moe_inputs(inp, core_id))
    return d


def run_fused(x_cores, inp):
    kb, nc = get_prog("fused")
    in_maps = [dict(fused_inputs(inp, c), x_in=x_cores[c]) for c in range(NCORES)]
    res = run_bass_kernel_spmd(nc, in_maps, core_ids=list(range(NCORES)))
    return [r["x_out"] for r in res.results]



_cache = {}


def get_prog(name, **kw):
    key = (name, tuple(sorted(kw.items())))
    if key not in _cache:
        kb, cm = {"ffn0": build_ffn0, "mamba": build_mamba, "attn": build_attn, "moe": build_moe, "fused": build_fused}[name.split(":")[0]](**kw)
        nc = kb.build()
        _cache[key] = (kb, nc)
    return _cache[key]


def ffn0_inputs(inp):
    return {
        "w1t": tr(inp["L0_ffn_W1"]),
        "w2t": tr(inp["L0_ffn_W2"]),
        "b1c": colmaj(inp["L0_ffn_b1"]),
        "b2c": colmaj(inp["L0_ffn_b2"]),
        "onecol": np.ones((128, 1), np.float32),
        "onerow": np.ones((1, 128), np.float32),
    }


def run_ffn0(x_cores, inp):
    kb, nc = get_prog("ffn0")
    base = ffn0_inputs(inp)
    in_maps = [dict(base, x_in=x) for x in x_cores]
    res = run_bass_kernel_spmd(nc, in_maps, core_ids=list(range(NCORES)))
    return [r["x_out"] for r in res.results]


if __name__ == "__main__":
    print("dev module; use dev_run.py")



def kernel(**inputs):
    x = np.asarray(inputs["L0_attn_x"], np.float32)
    xs = [np.ascontiguousarray(x[c].T) for c in range(NCORES)]
    xs = run_fused(xs, inputs)
    return np.ascontiguousarray(np.stack([np.asarray(o).T for o in xs])).astype(np.float32)
